# version 15
# baseline (speedup 1.0000x reference)
import numpy as np
import ml_dtypes
from contextlib import ExitStack

import concourse.bass as bass
import concourse.mybir as mybir
from concourse.bass_utils import run_bass_kernel_spmd

F32 = mybir.dt.float32
BF16 = mybir.dt.bfloat16
OP = mybir.AluOpType
ACT = mybir.ActivationFunctionType
AX = mybir.AxisListType.X

MAGIC = 12582912.0
DECAY = 0.99
NCORES = 8
L = 1024
B = 8
RPC = L * B
D = 1024
M = 256
FF = 4096
C = 64
SUB = C * B
NSUB = L // C
EPS_LN = 1e-5
EPS_Q = 1e-8
R255 = float(np.float32(1.0 / 255.0))
R127 = float(np.float32(1.0 / 127.0))
RD = float(np.float32(1.0 / D))
D64 = float(np.float32(DECAY ** 64))

ENGINES = ("sync", "tensor", "vector", "scalar", "gpsimd")


class _Sem:
    __slots__ = ("h", "n")

    def __init__(self, h):
        self.h = h
        self.n = 0


def _build_nc(debug_dump=False, stop_stage=None):
    nc = bass.Bass("TRN2", target_bir_lowering=False, detect_race_conditions=False)
    _dk = "ExternalOutput" if debug_dump else "Internal"

    xr = nc.dram_tensor("xr", [RPC, D], F32, kind="ExternalInput").ap()
    wqt = nc.dram_tensor("wqt", [D, M], F32, kind="ExternalInput").ap()
    wkt = nc.dram_tensor("wkt", [D, M], F32, kind="ExternalInput").ap()
    wvt = nc.dram_tensor("wvt", [D, M], F32, kind="ExternalInput").ap()
    wot = nc.dram_tensor("wot", [M, D], F32, kind="ExternalInput").ap()
    m0t = nc.dram_tensor("m0t", [M, M], F32, kind="ExternalInput").ap()
    gate_in = nc.dram_tensor("gate_in", [1, 1], F32, kind="ExternalInput").ap()
    w1_in = nc.dram_tensor("w1_in", [FF, D], F32, kind="ExternalInput").ap()
    b1_in = nc.dram_tensor("b1_in", [128, FF // 128], F32, kind="ExternalInput").ap()
    w2_in = nc.dram_tensor("w2_in", [D, FF], F32, kind="ExternalInput").ap()
    b2_in = nc.dram_tensor("b2_in", [128, D // 128], F32, kind="ExternalInput").ap()
    maskc = nc.dram_tensor("maskc", [128, 128], F32, kind="ExternalInput").ap()
    identf_in = nc.dram_tensor("identf_in", [128, 128], F32, kind="ExternalInput").ap()
    identb_in = nc.dram_tensor("identb_in", [128, 128], BF16, kind="ExternalInput").ap()
    dqrow_in = nc.dram_tensor("dqrow_in", [1, SUB], F32, kind="ExternalInput").ap()
    dkrow_in = nc.dram_tensor("dkrow_in", [1, SUB], F32, kind="ExternalInput").ap()
    coef_in = nc.dram_tensor("coef_in", [1, 16], F32, kind="ExternalInput").ap()

    out_d = nc.dram_tensor("out", [RPC, D], F32, kind="ExternalOutput").ap()

    qg_d = nc.dram_tensor("qg_d", [NSUB, 128, 2 * SUB], F32, kind=_dk).ap()
    rp_d = nc.dram_tensor("rp_d", [NSUB, 128, 2 * SUB], F32, kind=_dk).ap()
    x1_d = nc.dram_tensor("x1_d", [64, 128, D], F32, kind=_dk).ap()
    i1_d = nc.dram_tensor("i1_d", [16, 32, 128, 512], F32, kind=_dk).ap()
    g_in = nc.dram_tensor("g_in", [128, 512], F32, kind="Internal")
    g_out = nc.dram_tensor("g_out", [NCORES * 128, 512], F32, kind="Internal")
    m2_i = nc.dram_tensor("m2_i", [1, 1], F32, kind="Internal")
    m2_o = nc.dram_tensor("m2_o", [1, 1], F32, kind="Internal")
    m3_i = nc.dram_tensor("m3_i", [1, 1], F32, kind="Internal")
    m3_o = nc.dram_tensor("m3_o", [1, 1], F32, kind="Internal")
    rowb_d = nc.dram_tensor("rowb_d", [2, 8, 128], F32, kind="Internal")
    if debug_dump:
        dbg_hint = nc.dram_tensor("dbg_hint", [128, 32, 256], BF16,
                                  kind="ExternalOutput").ap()
        dbg_cb = nc.dram_tensor("dbg_cb", [128, 144], F32,
                                kind="ExternalOutput").ap()
        dbg_w2 = nc.dram_tensor("dbg_w2", [128, 32, 1024], BF16,
                                kind="ExternalOutput").ap()
        dbg_sb = nc.dram_tensor("dbg_sb", [128, 2048], F32,
                                kind="ExternalOutput").ap()
        dbg_ln = nc.dram_tensor("dbg_ln", [128, 64], F32,
                                kind="ExternalOutput").ap()
        dbg_ha = nc.dram_tensor("dbg_ha", [512, 128, 512], BF16,
                                kind="ExternalOutput").ap()
        dbg_xn = nc.dram_tensor("dbg_xn", [128, 1024], F32,
                                kind="ExternalOutput").ap()
        dbg_x0 = nc.dram_tensor("dbg_x0", [128, 1024], F32,
                                kind="ExternalOutput").ap()

    stack = ExitStack()
    with stack:
        sb = lambda name, shape, dt=F32: stack.enter_context(
            nc.sbuf_tensor(name, shape, dt))

        ones1 = sb("ones1", [1, 128])
        identf = sb("identf", [128, 128])
        identb = sb("identb", [128, 128], BF16)
        mask = sb("mask", [128, 128])
        wq_s = sb("wq_s", [128, 8, M])
        wk_s = sb("wk_s", [128, 8, M])
        wv_s = sb("wv_s", [128, 8, M])
        wo_s = sb("wo_s", [128, 2, D])
        m0_s = sb("m0_s", [128, 2, M])
        dqr = sb("dqr", [1, SUB])
        dkr = sb("dkr", [1, SUB])
        dq_b = sb("dq_b", [128, SUB])
        dk_b = sb("dk_b", [128, SUB])
        coefr = sb("coefr", [1, 16])
        coefb = sb("coefb", [128, 16])
        scm = sb("scm", [128, 16])
        srow = sb("srow", [1, 16])
        stats = sb("stats", [128, 64, 2])
        mf = sb("mf", [128, 32])
        sw1 = sb("sw1", [128, 32])
        s1sw = sb("s1sw", [128, 32])
        b1i = sb("b1i", [128, 32])
        ccol = sb("ccol", [128, 32])
        bcol = sb("bcol", [128, 32])
        b1_s = sb("b1_s", [128, 32])
        sw2 = sb("sw2", [128, 8])
        b2i = sb("b2i", [128, 8])
        b2_s = sb("b2_s", [128, 8])
        s2sw = sb("s2sw", [128, 8])
        bq2v = sb("bq2v", [128, 8])
        s2row = sb("s2row", [8, 128])
        bq2row = sb("bq2row", [8, 128])
        s2rowf = sb("s2rowf", [1, 1024])
        bq2rowf = sb("bq2rowf", [1, 1024])
        zt = sb("zt", [128, 2, 2, M])
        wc_s = sb("wc_s", [128, 2, M])
        lnw = sb("lnw", [128, 16])
        xmax = sb("xmax", [128, 1])
        sc1 = sb("sc1", [1, 4])
        wint = sb("wint", [128, 32768], BF16)

        AR = 17408
        arena = sb("arena", [128, AR])

        def carve(off, n):
            assert off + n <= AR, (off, n)
            return arena[:, off:off + n], off + n

        o = 0
        a_x, o = carve(o, 4096)
        a_xn, o = carve(o, 1024)
        a_xnt, o = carve(o, 4096)
        a_q, o = carve(o, 1024)
        a_qg, o = carve(o, 1024)
        a_k, o = carve(o, 1024)
        a_kt, o = carve(o, 1024)
        a_v, o = carve(o, 1024)
        a_a, o = carve(o, 2048)
        a_r, o = carve(o, 1024)
        assert o <= AR

        x_t = a_x.rearrange("p (t c) -> p t c", t=4)
        xnt = a_xnt.rearrange("p (dc c) -> p dc c", dc=8)
        qt = a_q.rearrange("p (mc c) -> p mc c", mc=2)
        qgt = a_qg.rearrange("p (mc c) -> p mc c", mc=2)
        kt_ = a_k.rearrange("p (mc c) -> p mc c", mc=2)
        ktok = a_kt.rearrange("p (t c) -> p t c", t=4)
        v_s = a_v.rearrange("p (t c) -> p t c", t=4)
        a_sb = a_a.rearrange("p (k c) -> p k c", k=4)
        rt = a_r.rearrange("p (ic c) -> p ic c", ic=2)

        x1b = a_xnt.rearrange("p (t c) -> p t c", t=4)
        rpb = a_k.rearrange("p (ic c) -> p ic c", ic=2)
        gall = a_x.rearrange("p (j c) -> p j c", j=8)
        gacc = a_xn

        p3_tq = arena[:, 4096 + 1024 + 4096:4096 + 1024 + 4096 + 1024]
        p3_xint = a_k.bitcast(BF16)
        p3_xintT = a_xnt.bitcast(BF16)[:, 0:4096].rearrange(
            "p (dc c) -> p dc c", dc=8)
        p4_w2q = a_xnt.bitcast(BF16)[:, 0:4096]
        p3_i1 = a_a.rearrange("p (g c) -> p g c", g=4)
        p3_wbuf = arena[:, 13312:13312 + 4096]

        p4_hint = arena[:, 0:2048].bitcast(BF16).rearrange(
            "p (r c) -> p r c", r=8)
        p4_i1 = arena[:, 2048:2048 + 2048].rearrange(
            "p (r c) -> p r c", r=4)
        p4_t0 = arena[:, 4096:4096 + 1024].rearrange(
            "p (r c) -> p r c", r=2)
        p4_t1 = arena[:, 5120:5120 + 512]
        p4_x1 = arena[:, 5632:5632 + 4096].rearrange(
            "p (t c) -> p t c", t=4)
        p4_out = arena[:, 9728:9728 + 4096].rearrange(
            "p (t c) -> p t c", t=4)
        p4_u = arena[:, 13824:13824 + 512]
        p4_sb2 = arena[:, 14336:14336 + 1024]
        p4_bq2 = arena[:, 15360:15360 + 1024]

        w1v = wint.rearrange("p (dc f) -> p dc f", dc=8)
        w2v = wint.rearrange("p (fc d) -> p fc d", fc=32)

        ps = [stack.enter_context(nc.psum_tensor(f"ps{i}", [128, 512], F32))
              for i in range(8)]
        ps7b = ps[7].ap().bitcast(BF16)

        sL = _Sem(stack.enter_context(nc.semaphore("sL")))
        sS = _Sem(stack.enter_context(nc.semaphore("sS")))
        sG = _Sem(stack.enter_context(nc.semaphore("sG")))
        sT = _Sem(stack.enter_context(nc.semaphore("sT")))
        sV = _Sem(stack.enter_context(nc.semaphore("sV")))
        sC = _Sem(stack.enter_context(nc.semaphore("sC")))
        sCC = _Sem(stack.enter_context(nc.semaphore("sCC")))

        plan = {k: [] for k in ENGINES}
        lastw = {k: {} for k in ENGINES}
        milestones = {}

        def milestone(name):
            milestones[name] = {k: len(v) for k, v in plan.items()}

        def wait(eng, sem, val=None):
            v = sem.n if val is None else val
            if v <= 0:
                return
            if lastw[eng].get(id(sem), -1) >= v:
                return
            lastw[eng][id(sem)] = v
            plan[eng].append(lambda e, s=sem.h, vv=v: e.wait_ge(s, vv))

        def op(eng, fn, inc=None, k=1, serialize=False):
            if inc is None:
                plan[eng].append(fn)
            else:
                inc.n += k
                plan[eng].append(
                    lambda e, f=fn, s=inc.h, kk=k: f(e).then_inc(s, kk))
                if serialize:
                    v = inc.n
                    plan[eng].append(
                        lambda e, s=inc.h, vv=v: e.wait_ge(s, vv))
                    lastw[eng][id(inc)] = max(lastw[eng].get(id(inc), -1), v)

        for dst, src in [(wq_s[:, :, :], wqt.rearrange("(dc p) m -> p dc m", p=128)),
                         (wk_s[:, :, :], wkt.rearrange("(dc p) m -> p dc m", p=128)),
                         (wv_s[:, :, :], wvt.rearrange("(dc p) m -> p dc m", p=128)),
                         (wo_s[:, :, :], wot.rearrange("(mc p) h -> p mc h", p=128)),
                         (m0_s[:, :, :], m0t.rearrange("(jc p) i -> p jc i", p=128)),
                         (mask[:, :], maskc),
                         (identf[:, :], identf_in),
                         (identb[:, :], identb_in),
                         (dqr[:, :], dqrow_in),
                         (dkr[:, :], dkrow_in),
                         (coefr[:, :], coef_in),
                         (sc1[:, 0:1], gate_in),
                         (b1_s[:, :], b1_in),
                         (b2_s[:, :], b2_in)]:
            op("sync", lambda e, d=dst, s=src: e.dma_start(out=d, in_=s),
               inc=sL, k=16)
        consts_loaded = sL.n

        op("vector", lambda e: e.memset(ones1[:, :], 1.0), inc=sV)
        op("vector", lambda e: e.memset(zt[:, 0, :, :], 0.0), inc=sV)
        op("vector", lambda e: e.memset(mf[:, :], -3.0e38), inc=sV)
        op("vector", lambda e: e.memset(xmax[:, :], -3.0e38), inc=sV)
        op("vector", lambda e: e.memset(srow[:, :], 0.0), inc=sV)
        op("vector", lambda e: e.memset(lnw[:, :], 0.0), inc=sV)
        wait("vector", sL, consts_loaded)
        op("vector", lambda e: e.memset(sc1[0:1, 2:4], 0.0), inc=sV)
        wait("scalar", sL, consts_loaded)
        op("scalar", lambda e: e.activation(sc1[0:1, 1:2], sc1[0:1, 0:1],
                                            ACT.Sigmoid), inc=sC)
        wait("tensor", sL, consts_loaded)
        wait("tensor", sV, sV.n)
        wait("tensor", sC, sC.n)
        op("tensor", lambda e: e.matmul(ps[0][:, :], ones1[:, :], dqr[:, :],
                                        start=True, stop=True))
        op("tensor", lambda e: e.matmul(ps[1][:, :], ones1[:, :], dkr[:, :],
                                        start=True, stop=True))
        op("tensor", lambda e: e.matmul(ps[2][:, 0:16], ones1[:, :], coefr[:, :],
                                        start=True, stop=True))
        op("tensor", lambda e: e.matmul(ps[2][:, 16:17], ones1[:, :],
                                        sc1[0:1, 1:2], start=True, stop=True),
           inc=sT)
        wait("vector", sT, sT.n)
        op("vector", lambda e: e.tensor_copy(dq_b[:, :], ps[0][:, :]), inc=sV)
        op("vector", lambda e: e.tensor_copy(dk_b[:, :], ps[1][:, :]), inc=sV)
        op("vector", lambda e: e.tensor_copy(coefb[:, :], ps[2][:, 0:16]), inc=sV)
        op("vector", lambda e: e.tensor_copy(scm[:, 0:1], ps[2][:, 16:17]),
           inc=sV)
        const_done_v = sV.n
        const_done_t = sT.n

        milestone("consts")
        ln_evac_v = [0] * 5
        z_update_v = const_done_v
        last_tr = 0
        qg_spill_g = 0
        rp_spill_g = 0
        xld = [0] * 64
        xn_done_g = [0] * 64
        rs_done_g = [0] * 64
        xld_prev = 0

        def emit_x_tile_load(g):
            nonlocal_ = None
            if g >= 4:
                wait("sync", sC, xn_done_g[g - 4])
                wait("sync", sV, rs_done_g[g - 4])
            if g >= 1:
                wait("sync", sL, xld[g - 1])
            op("sync", lambda e, g_=g: e.dma_start(
                out=x_t[:, g_ % 4, :],
                in_=xr[g_ * 128:(g_ + 1) * 128, :]), inc=sL, k=16)
            xld[g] = sL.n

        for g in range(4):
            emit_x_tile_load(g)

        for s in range(NSUB):
            par = s % 2
            dsub = float(np.float32(DECAY ** (C * s)))

            for tt in range(4):
                wait("scalar", sL, xld[s * 4 + tt])
                wait("scalar", sT, last_tr)
                op("scalar", lambda e, tt_=tt: e.activation(
                    a_xn[:, :], x_t[:, tt_, :], ACT.Square,
                    accum_out=lnw[:, 4 + tt_:5 + tt_]), inc=sC)
                sq_done = sC.n
                wait("vector", sL, xld[s * 4 + tt])
                op("vector", lambda e, tt_=tt: e.reduce_sum(
                    lnw[:, tt_:tt_ + 1], x_t[:, tt_, :], axis=AX), inc=sV,
                    serialize=True)
                rs_done_g[s * 4 + tt] = sV.n
                wait("vector", sC, sq_done)
                op("vector", lambda e, tt_=tt: e.tensor_scalar(
                    out=lnw[:, 8:9], in0=lnw[:, tt_:tt_ + 1], scalar1=RD,
                    scalar2=None, op0=OP.mult), inc=sV, serialize=True)
                op("vector", lambda e: e.tensor_tensor(
                    out=lnw[:, 9:10], in0=lnw[:, 8:9], in1=lnw[:, 8:9],
                    op=OP.mult), inc=sV, serialize=True)
                op("vector", lambda e, tt_=tt: e.scalar_tensor_tensor(
                    out=lnw[:, 10:11], in0=lnw[:, 4 + tt_:5 + tt_], scalar=RD,
                    in1=lnw[:, 9:10], op0=OP.mult, op1=OP.subtract), inc=sV,
                    serialize=True)
                op("vector", lambda e: e.tensor_scalar(
                    out=lnw[:, 11:12], in0=lnw[:, 10:11], scalar1=EPS_LN,
                    scalar2=None, op0=OP.add), inc=sV, serialize=True)
                op("vector", lambda e: e.reciprocal(
                    lnw[:, 12:13], lnw[:, 11:12]), inc=sV, serialize=True)
                rec_done = sV.n
                wait("scalar", sV, rec_done)
                op("scalar", lambda e: e.activation(
                    lnw[:, 13:14], lnw[:, 12:13], ACT.Sqrt), inc=sC)
                rstd_done = sC.n
                wait("vector", sC, rstd_done)
                op("vector", lambda e: e.scalar_tensor_tensor(
                    out=lnw[:, 14:15], in0=lnw[:, 8:9], scalar=-1.0,
                    in1=lnw[:, 13:14], op0=OP.mult, op1=OP.mult), inc=sV,
                    serialize=True)
                nmr_done = sV.n
                wait("scalar", sV, nmr_done)
                op("scalar", lambda e, tt_=tt: e.activation(
                    a_xn[:, :], x_t[:, tt_, :], ACT.Identity,
                    bias=lnw[:, 14:15], scale=lnw[:, 13:14]), inc=sC)
                xn_done = sC.n
                xn_done_g[s * 4 + tt] = xn_done
                if debug_dump and s == 0:
                    wait("sync", sV, nmr_done)
                    op("sync", lambda e, tt_=tt: e.dma_start(
                        out=dbg_ln[:, tt_ * 16:(tt_ + 1) * 16],
                        in_=lnw[:, 0:16]), inc=sS, k=16)
                    if tt == 0:
                        wait("sync", sC, xn_done)
                        op("sync", lambda e: e.dma_start(
                            out=dbg_xn[:, :], in_=a_xn[:, :]), inc=sS, k=16)
                        op("sync", lambda e: e.dma_start(
                            out=dbg_x0[:, :], in_=x_t[:, 0, :]), inc=sS, k=16)
                        wait("scalar", sS, sS.n)
                        wait("vector", sS, sS.n)
                wait("tensor", sC, xn_done)
                wait("tensor", sV, ln_evac_v[-1] if s + tt else const_done_v)
                for dc in range(8):
                    bank, sl_ = (0, dc) if dc < 4 else (1, dc - 4)
                    last = dc == 7
                    op("tensor", lambda e, b=bank, q=sl_, dc_=dc: e.matmul(
                        ps[b][:, q * 128:(q + 1) * 128],
                        a_xn[:, dc_ * 128:(dc_ + 1) * 128], identf[:, :],
                        is_transpose=True), inc=sT if last else None)
                tr_done = sT.n
                last_tr = tr_done
                wait("vector", sT, tr_done)
                op("vector", lambda e, tt_=tt: e.tensor_copy(
                    xnt[:, 0:4, tt_ * 128:(tt_ + 1) * 128],
                    ps[0][:, :].rearrange("p (a c) -> p a c", a=4)), inc=sV)
                op("vector", lambda e, tt_=tt: e.tensor_copy(
                    xnt[:, 4:8, tt_ * 128:(tt_ + 1) * 128],
                    ps[1][:, :].rearrange("p (a c) -> p a c", a=4)), inc=sV)
                ln_evac_v.append(sV.n)

            xnt_done = sV.n
            for tt2 in range(4):
                g2 = (s + 1) * 4 + tt2
                if g2 < 64:
                    emit_x_tile_load(g2)

            wait("tensor", sV, xnt_done)
            for (wsrc, bank0) in ((wq_s, 2), (wk_s, 4)):
                for mc in range(2):
                    for dc in range(8):
                        op("tensor", lambda e, w=wsrc, b=bank0 + mc, mc_=mc,
                           dc_=dc: e.matmul(
                               ps[b][:, :],
                               w[:, dc_, mc_ * 128:(mc_ + 1) * 128],
                               xnt[:, dc_, :],
                               start=(dc_ == 0), stop=(dc_ == 7)),
                           inc=sT if dc == 7 else None)
            qk_done = sT.n
            wait("vector", sT, qk_done)
            wait("vector", sG, qg_spill_g)
            for mc in range(2):
                op("vector", lambda e, mc_=mc: e.tensor_tensor(
                    out=qt[:, mc_, :], in0=ps[2 + mc_][:, :], in1=dq_b[:, :],
                    op=OP.mult), inc=sV)
                op("vector", lambda e, mc_=mc, d_=dsub: e.tensor_scalar(
                    out=qgt[:, mc_, :], in0=qt[:, mc_, :], scalar1=d_,
                    scalar2=None, op0=OP.mult), inc=sV)
                op("vector", lambda e, mc_=mc: e.tensor_tensor(
                    out=kt_[:, mc_, :], in0=ps[4 + mc_][:, :], in1=dk_b[:, :],
                    op=OP.mult), inc=sV)
            qk_evac = sV.n
            wait("gpsimd", sV, qk_evac)
            op("gpsimd", lambda e, s_=s: e.dma_start(
                out=qg_d[s_, :, :], in_=a_qg), inc=sG, k=16)
            qg_spill_g = sG.n

            wait("tensor", sV, qk_evac)
            for tt in range(4):
                bank = 2 + tt // 2
                cofs = (tt % 2) * 256
                for dc in range(8):
                    op("tensor", lambda e, b=bank, co=cofs, tt_=tt, dc_=dc:
                       e.matmul(ps[b][:, co:co + 256],
                                xnt[:, dc_, tt_ * 128:(tt_ + 1) * 128],
                                wv_s[:, dc_, :],
                                start=(dc_ == 0), stop=(dc_ == 7)),
                       inc=sT if dc == 7 else None)
            v_done = sT.n
            wait("vector", sT, v_done)
            op("vector", lambda e: e.tensor_copy(
                v_s[:, 0:2, :], ps[2][:, :].rearrange("p (a c) -> p a c", a=2)),
                inc=sV)
            op("vector", lambda e: e.tensor_copy(
                v_s[:, 2:4, :], ps[3][:, :].rearrange("p (a c) -> p a c", a=2)),
                inc=sV)
            v_evac = sV.n

            wait("tensor", sV, v_evac)
            for tt in range(4):
                for mc in range(2):
                    idx = tt * 2 + mc
                    bank, sl_ = (0, idx) if idx < 4 else (1, idx - 4)
                    op("tensor", lambda e, b=bank, q=sl_, tt_=tt, mc_=mc:
                       e.matmul(ps[b][:, q * 128:(q + 1) * 128],
                                kt_[:, mc_, tt_ * 128:(tt_ + 1) * 128],
                                identf[:, :], is_transpose=True),
                       inc=sT if idx == 7 else None)
            ktok_tr = sT.n
            wait("vector", sT, ktok_tr)
            op("vector", lambda e: e.tensor_copy(
                ktok[:, 0:2, :], ps[0][:, :].rearrange("p (a c) -> p a c", a=2)),
                inc=sV)
            op("vector", lambda e: e.tensor_copy(
                ktok[:, 2:4, :], ps[1][:, :].rearrange("p (a c) -> p a c", a=2)),
                inc=sV)
            ktok_evac = sV.n

            a_banks = {0: 4, 1: 5, 2: 2, 3: 3}
            wait("tensor", sV, ktok_evac)
            for kt2 in range(4):
                b = a_banks[kt2]
                for mc in range(2):
                    op("tensor", lambda e, b_=b, kt3=kt2, mc_=mc: e.matmul(
                        ps[b_][:, kt3 * 128:512],
                        kt_[:, mc_, kt3 * 128:(kt3 + 1) * 128],
                        qt[:, mc_, kt3 * 128:512],
                        start=(mc_ == 0), stop=(mc_ == 1)),
                       inc=sT if (kt2 == 3 and mc == 1) else None)
            at_done = sT.n
            wait("vector", sT, at_done)
            for kt2 in range(4):
                b = a_banks[kt2]
                op("vector", lambda e, b_=b, kt3=kt2: e.tensor_tensor(
                    out=a_sb[:, kt3, kt3 * 128:(kt3 + 1) * 128],
                    in0=ps[b_][:, kt3 * 128:(kt3 + 1) * 128],
                    in1=mask[:, :], op=OP.mult), inc=sV)
                if kt2 < 3:
                    op("vector", lambda e, b_=b, kt3=kt2: e.tensor_copy(
                        a_sb[:, kt3, (kt3 + 1) * 128:512],
                        ps[b_][:, (kt3 + 1) * 128:512]), inc=sV)
            a_evac = sV.n

            wait("tensor", sV, a_evac)
            wait("tensor", sV, z_update_v)
            for ic in range(2):
                b = 6 + ic
                for jc in range(2):
                    op("tensor", lambda e, b_=b, ic_=ic, jc_=jc, p_=par:
                       e.matmul(ps[b_][:, :],
                                zt[:, p_, jc_, ic_ * 128:(ic_ + 1) * 128],
                                qt[:, jc_, :],
                                start=(jc_ == 0), stop=False))
                for kt2 in range(4):
                    op("tensor", lambda e, b_=b, ic_=ic, kt3=kt2: e.matmul(
                        ps[b_][:, kt3 * 128:512],
                        v_s[:, kt3, ic_ * 128:(ic_ + 1) * 128],
                        a_sb[:, kt3, kt3 * 128:512],
                        start=False, stop=(kt3 == 3)),
                       inc=sT if (ic == 1 and kt2 == 3) else None)
            r_done = sT.n
            wait("vector", sT, r_done)
            wait("vector", sG, rp_spill_g)
            for ic in range(2):
                op("vector", lambda e, ic_=ic: e.tensor_copy(
                    rt[:, ic_, :], ps[6 + ic_][:, :]), inc=sV)
            r_evac = sV.n
            wait("gpsimd", sV, r_evac)
            op("gpsimd", lambda e, s_=s: e.dma_start(
                out=rp_d[s_, :, :], in_=a_r), inc=sG, k=16)
            rp_spill_g = sG.n

            wait("tensor", sV, a_evac)
            for jc in range(2):
                for ut in range(4):
                    op("tensor", lambda e, jc_=jc, ut_=ut: e.matmul(
                        ps[4][:, jc_ * 256:(jc_ + 1) * 256],
                        ktok[:, ut_, jc_ * 128:(jc_ + 1) * 128],
                        v_s[:, ut_, :],
                        start=(ut_ == 0), stop=(ut_ == 3)),
                       inc=sT if (jc == 1 and ut == 3) else None)
            p_done = sT.n
            wait("vector", sT, p_done)
            op("vector", lambda e, p_=par: e.tensor_tensor(
                out=zt[:, 1 - p_, :, :].rearrange("p a c -> p (a c)"),
                in0=zt[:, p_, :, :].rearrange("p a c -> p (a c)"),
                in1=ps[4][:, :], op=OP.add), inc=sV)
            op("vector", lambda e, p_=par: e.tensor_scalar(
                out=zt[:, 1 - p_, :, :].rearrange("p a c -> p (a c)"),
                in0=zt[:, 1 - p_, :, :].rearrange("p a c -> p (a c)"),
                scalar1=D64, scalar2=None, op0=OP.mult), inc=sV)
            z_update_v = sV.n

        milestone("p1a")
        g_parity = NSUB % 2

        wait("gpsimd", sV, z_update_v)
        op("gpsimd", lambda e: e.dma_start(
            out=g_in.ap(), in_=zt[:, g_parity, :, :].rearrange(
                "p a c -> p (a c)")), inc=sG, k=16)
        wait("gpsimd", sG, sG.n)
        op("gpsimd", lambda e: e.collective_compute(
            "AllGather", OP.bypass, replica_groups=[list(range(NCORES))],
            ins=[g_in.ap().opt()], outs=[g_out.ap().opt()]), inc=sCC)
        wait("gpsimd", sCC, sCC.n)
        gall2 = a_xnt.rearrange("p (j c) -> p j c", j=8)
        op("gpsimd", lambda e: e.dma_start(
            out=gall2[:, :, :],
            in_=g_out.ap().rearrange("(j p) c -> p j c", p=128)), inc=sG, k=16)
        g_loaded = sG.n
        wait("vector", sG, g_loaded)
        op("vector", lambda e: e.tensor_scalar(
            out=gacc[:, 0:512], in0=gall2[:, 0, :], scalar1=coefb[:, 0:1],
            scalar2=None, op0=OP.mult), inc=sV)
        for j in range(1, NCORES):
            op("vector", lambda e, j_=j: e.scalar_tensor_tensor(
                out=gacc[:, 0:512], in0=gall2[:, j_, :],
                scalar=coefb[:, j_:j_ + 1], in1=gacc[:, 0:512],
                op0=OP.mult, op1=OP.add), inc=sV)
        op("vector", lambda e: e.scalar_tensor_tensor(
            out=wc_s[:, :, :].rearrange("p a c -> p (a c)"),
            in0=m0_s[:, :, :].rearrange("p a c -> p (a c)"),
            scalar=coefb[:, 8:9], in1=gacc[:, 0:512],
            op0=OP.mult, op1=OP.add), inc=sV)
        wc_done = sV.n

        milestone("cc1")
        qg_bufs = [qgt, a_q.rearrange("p (mc c) -> p mc c", mc=2)]
        rp_bufs = [rpb, a_kt.rearrange("p (mc c) -> p mc c", mc=2)]
        qg_raw = [a_qg, a_q]
        rp_raw = [a_k, a_kt]
        qg_ld_s = [0] * NSUB
        x_ld_s = [0] * NSUB
        rcorr_done_s = [0] * NSUB
        rf_done_s = [0] * NSUB
        x1stt_done_s = [0] * NSUB
        x1store_s = [0] * NSUB
        spills_done = max(qg_spill_g, rp_spill_g)

        def emit_p1b_loads(s):
            pb = s % 2
            if s == 0:
                wait("sync", sG, spills_done)
                wait("sync", sT, sT.n)
                wait("sync", sV, sV.n)
            else:
                wait("sync", sL, sL.n)
                if s >= 2:
                    wait("sync", sT, rcorr_done_s[s - 2])
                    wait("sync", sV, rf_done_s[s - 2])
            op("sync", lambda e, s_=s, p_=pb: e.dma_start(
                out=qg_raw[p_], in_=qg_d[s_, :, :]), inc=sL, k=16)
            op("sync", lambda e, s_=s, p_=pb: e.dma_start(
                out=rp_raw[p_], in_=rp_d[s_, :, :]), inc=sL, k=16)
            qg_ld_s[s] = sL.n

        def emit_p1b_xload(s):
            wait("sync", sL, sL.n)
            if s == 0:
                wait("sync", sC, xn_done_g[63])
                wait("sync", sV, rs_done_g[63])
            else:
                wait("sync", sV, x1stt_done_s[s - 1])
            op("sync", lambda e, s_=s: e.dma_start(
                out=x_t[:, :, :],
                in_=xr[s_ * SUB:(s_ + 1) * SUB, :].rearrange(
                    "(t p) c -> p t c", p=128)), inc=sL, k=16)
            x_ld_s[s] = sL.n

        emit_p1b_loads(0)
        emit_p1b_xload(0)
        for s in range(NSUB):
            pb = s % 2
            qgv = qg_bufs[pb]
            rpv = rp_bufs[pb]

            wait("tensor", sL, qg_ld_s[s])
            wait("tensor", sV, wc_done)
            if s >= 1:
                wait("tensor", sV, rf_done_s[s - 1])
            for ic in range(2):
                for jc in range(2):
                    op("tensor", lambda e, ic_=ic, jc_=jc, q_=qgv: e.matmul(
                        ps[6 + ic_][:, :],
                        wc_s[:, jc_, ic_ * 128:(ic_ + 1) * 128],
                        q_[:, jc_, :],
                        start=(jc_ == 0), stop=(jc_ == 1)),
                       inc=sT if (ic == 1 and jc == 1) else None)
            rc_done = sT.n
            rcorr_done_s[s] = rc_done
            wait("vector", sT, rc_done)
            for ic in range(2):
                op("vector", lambda e, ic_=ic, r_=rpv: e.tensor_tensor(
                    out=rt[:, ic_, :], in0=ps[6 + ic_][:, :],
                    in1=r_[:, ic_, :], op=OP.add), inc=sV)
            rf_done = sV.n
            rf_done_s[s] = rf_done

            if s + 1 < NSUB:
                emit_p1b_loads(s + 1)

            wait("tensor", sV, rf_done)
            x1_done_hist = []
            first_stt = True
            for tt in range(4):
                b0 = 2 + (tt % 2)
                b1 = 4 + (tt % 2)
                if tt >= 2:
                    wait("tensor", sV, x1_done_hist[tt - 2])
                for half, bb in ((0, b0), (1, b1)):
                    for ic in range(2):
                        op("tensor", lambda e, b_=bb, ic_=ic, tt_=tt, h_=half:
                           e.matmul(ps[b_][:, :],
                                    rt[:, ic_, tt_ * 128:(tt_ + 1) * 128],
                                    wo_s[:, ic_, h_ * 512:(h_ + 1) * 512],
                                    start=(ic_ == 0), stop=(ic_ == 1)),
                           inc=sT if ic == 1 else None)
                wo_done = sT.n
                wait("vector", sT, wo_done)
                wait("vector", sL, x_ld_s[s])
                if first_stt and s >= 1:
                    wait("vector", sG, x1store_s[s - 1])
                    first_stt = False
                for half, bb in ((0, b0), (1, b1)):
                    op("vector", lambda e, b_=bb, tt_=tt, h_=half:
                       e.scalar_tensor_tensor(
                           out=x1b[:, tt_, h_ * 512:(h_ + 1) * 512],
                           in0=ps[b_][:, :], scalar=scm[:, 0:1],
                           in1=x_t[:, tt_, h_ * 512:(h_ + 1) * 512],
                           op0=OP.mult, op1=OP.add), inc=sV)
                x1_done = sV.n
                x1_done_hist.append(x1_done)

                wait("scalar", sV, x1_done)
                op("scalar", lambda e, tt_=tt: e.activation(
                    a_xn[:, :], x1b[:, tt_, :], ACT.Square,
                    accum_out=lnw[:, 4:5]), inc=sC)
                sq2 = sC.n
                op("vector", lambda e, tt_=tt: e.reduce_sum(
                    lnw[:, 0:1], x1b[:, tt_, :], axis=AX), inc=sV,
                    serialize=True)
                wait("vector", sC, sq2)
                op("vector", lambda e: e.tensor_scalar(
                    out=lnw[:, 8:9], in0=lnw[:, 0:1], scalar1=RD,
                    scalar2=None, op0=OP.mult), inc=sV, serialize=True)
                op("vector", lambda e: e.tensor_tensor(
                    out=lnw[:, 9:10], in0=lnw[:, 8:9], in1=lnw[:, 8:9],
                    op=OP.mult), inc=sV, serialize=True)
                op("vector", lambda e: e.scalar_tensor_tensor(
                    out=lnw[:, 10:11], in0=lnw[:, 4:5], scalar=RD,
                    in1=lnw[:, 9:10], op0=OP.mult, op1=OP.subtract), inc=sV,
                    serialize=True)
                op("vector", lambda e: e.tensor_scalar(
                    out=lnw[:, 11:12], in0=lnw[:, 10:11], scalar1=EPS_LN,
                    scalar2=None, op0=OP.add), inc=sV, serialize=True)
                op("vector", lambda e: e.reciprocal(
                    lnw[:, 12:13], lnw[:, 11:12]), inc=sV, serialize=True)
                rec2 = sV.n
                wait("scalar", sV, rec2)
                tile_idx = s * 4 + tt
                op("scalar", lambda e, ti=tile_idx: e.activation(
                    stats[:, ti, 0:1], lnw[:, 12:13], ACT.Sqrt), inc=sC)
                rstd2 = sC.n
                wait("vector", sC, rstd2)
                op("vector", lambda e, ti=tile_idx: e.scalar_tensor_tensor(
                    out=stats[:, ti, 1:2], in0=lnw[:, 8:9], scalar=-1.0,
                    in1=stats[:, ti, 0:1], op0=OP.mult, op1=OP.mult), inc=sV,
                    serialize=True)
                op("vector", lambda e, tt_=tt: e.reduce_max(
                    lnw[:, 14:15], x1b[:, tt_, :], axis=AX), inc=sV,
                    serialize=True)
                op("vector", lambda e, ti=tile_idx: e.tensor_scalar(
                    out=lnw[:, 15:16], in0=lnw[:, 14:15],
                    scalar1=lnw[:, 8:9], scalar2=stats[:, ti, 0:1],
                    op0=OP.subtract, op1=OP.mult), inc=sV, serialize=True)
                op("vector", lambda e: e.tensor_tensor(
                    out=xmax[:, :], in0=xmax[:, :], in1=lnw[:, 15:16],
                    op=OP.max), inc=sV, serialize=True)
            slab_v = sV.n
            x1stt_done_s[s] = x1_done_hist[-1]
            wait("gpsimd", sV, slab_v)
            op("gpsimd", lambda e, s_=s: e.dma_start(
                out=x1_d[s_ * 4:(s_ + 1) * 4, :, :].rearrange(
                    "t p c -> p t c"), in_=x1b[:, :, :]), inc=sG, k=16)
            x1store_s[s] = sG.n
            if s + 1 < NSUB:
                emit_p1b_xload(s + 1)

        milestone("p1b")
        p1b_t_end = sT.n
        p1b_v_end = sV.n
        wait("tensor", sV, sV.n)
        op("tensor", lambda e: e.matmul(
            ps[0][0:1, 0:128], xmax[:, :], identf[:, :], is_transpose=True),
            inc=sT)
        wait("vector", sT, sT.n)
        op("vector", lambda e: e.reduce_max(
            sc1[0:1, 2:3], ps[0][0:1, 0:128], axis=AX), inc=sV,
            serialize=True)
        wait("gpsimd", sV, sV.n)
        op("gpsimd", lambda e: e.dma_start(out=m2_i.ap(), in_=sc1[0:1, 2:3]),
           inc=sG, k=16)
        wait("gpsimd", sG, sG.n)
        op("gpsimd", lambda e: e.collective_compute(
            "AllReduce", OP.max, replica_groups=[list(range(NCORES))],
            ins=[m2_i.ap().opt()], outs=[m2_o.ap().opt()]), inc=sCC)
        wait("sync", sV, p1b_v_end)
        wait("sync", sT, p1b_t_end)
        wait("vector", sT, p1b_t_end)
        wait("tensor", sV, p1b_v_end)
        w1_ld = [0] * 32
        w1_q_done = [0] * 32

        def p3w_load(ft):
            if ft >= 1:
                wait("sync", sL, w1_ld[ft - 1])
            if ft >= 2:
                wait("sync", sV, w1_q_done[ft - 2])
            op("sync", lambda e, ft_=ft, h_=ft % 2: e.dma_start(
                out=p3_wbuf[:, h_ * 1024:(h_ + 1) * 1024],
                in_=w1_in[ft_ * 128:(ft_ + 1) * 128, :]), inc=sL, k=16)
            w1_ld[ft] = sL.n

        p3w_load(0)
        p3w_load(1)
        for ft in range(32):
            h = ft % 2
            wb = p3_wbuf[:, h * 1024:(h + 1) * 1024]
            wait("vector", sL, w1_ld[ft])
            op("vector", lambda e, w_=wb: e.tensor_reduce(
                lnw[:, 0:1], w_, AX, OP.max,
                apply_absolute_value=True), inc=sV, serialize=True)
            op("vector", lambda e, ft_=ft: e.tensor_scalar(
                out=sw1[:, ft_:ft_ + 1], in0=lnw[:, 0:1], scalar1=EPS_Q,
                scalar2=R127, op0=OP.max, op1=OP.mult), inc=sV, serialize=True)
            op("vector", lambda e, ft_=ft: e.reciprocal(
                lnw[:, 1:2], sw1[:, ft_:ft_ + 1]), inc=sV, serialize=True)
            op("vector", lambda e, w_=wb: e.tensor_scalar(
                out=p3_tq[:, :], in0=w_, scalar1=lnw[:, 1:2],
                scalar2=MAGIC, op0=OP.mult, op1=OP.add), inc=sV)
            w1_q_done[ft] = sV.n
            op("vector", lambda e: e.tensor_scalar(
                out=p3_xint[:, 0:1024], in0=p3_tq[:, :], scalar1=MAGIC,
                scalar2=None, op0=OP.subtract), inc=sV)
            wq_done = sV.n
            if ft + 2 < 32:
                p3w_load(ft + 2)
            wait("tensor", sV, wq_done)
            for dc in range(8):
                op("tensor", lambda e, dc_=dc: e.matmul(
                    ps7b[:, dc_ * 128:(dc_ + 1) * 128],
                    p3_xint[:, dc_ * 128:(dc_ + 1) * 128], identb[:, :],
                    is_transpose=True), inc=sT if dc == 7 else None)
            wtr = sT.n
            wait("vector", sT, wtr)
            op("vector", lambda e, ft_=ft: e.tensor_copy(
                w1v[:, :, ft_ * 128:(ft_ + 1) * 128],
                ps7b[:, :].rearrange("p (a c) -> p a c", a=8)), inc=sV)
        wait("gpsimd", sCC, sCC.n)
        op("gpsimd", lambda e: e.dma_start(out=sc1[0:1, 2:3], in_=m2_o.ap()),
           inc=sG, k=16)
        wait("vector", sG, sG.n)
        op("vector", lambda e: e.tensor_scalar(
            out=srow[0:1, 2:3], in0=sc1[0:1, 2:3], scalar1=EPS_Q,
            scalar2=R255, op0=OP.max, op1=OP.mult), inc=sV, serialize=True)
        op("vector", lambda e: e.reciprocal(srow[0:1, 1:2], srow[0:1, 2:3]),
           inc=sV, serialize=True)
        s1_ready = sV.n
        wait("tensor", sV, s1_ready)
        op("tensor", lambda e: e.matmul(
            ps[0][:, 0:16], ones1[:, :], srow[0:1, 0:16], start=True,
            stop=True), inc=sT)
        wait("vector", sT, sT.n)
        op("vector", lambda e: e.tensor_copy(scm[:, 1:3], ps[0][:, 1:3]),
           inc=sV, serialize=True)
        s1b_done = sV.n

        milestone("cc2")
        w1_ready_v = sV.n
        wait("vector", sV, max(w1_ready_v, s1b_done))
        op("vector", lambda e: e.tensor_scalar(
            out=s1sw[:, :], in0=sw1[:, :], scalar1=scm[:, 2:3], scalar2=None,
            op0=OP.mult), inc=sV, serialize=True)
        op("vector", lambda e: e.reciprocal(ccol[:, :], s1sw[:, :]), inc=sV,
           serialize=True)
        op("vector", lambda e: e.tensor_tensor(
            out=bcol[:, :], in0=b1_s[:, :], in1=ccol[:, :], op=OP.mult),
            inc=sV, serialize=True)
        op("vector", lambda e: e.tensor_scalar(
            out=bcol[:, :], in0=bcol[:, :], scalar1=MAGIC, scalar2=None,
            op0=OP.add), inc=sV, serialize=True)
        op("vector", lambda e: e.tensor_scalar(
            out=b1i[:, :], in0=bcol[:, :], scalar1=MAGIC, scalar2=None,
            op0=OP.subtract), inc=sV, serialize=True)
        b1i_done = sV.n

        milestone("p3w")
        i1_evac_c = [0] * 5
        wait("sync", sS, sS.n)
        wait("sync", sG, sG.n)
        p3_tq_done = 0
        p3_xintT2 = a_xnt.bitcast(BF16).rearrange(
            "p (dc pp c) -> p dc pp c", dc=8, pp=2)
        xn2_done_s = [0] * NSUB
        xT_done_s = [0] * NSUB
        mm_done_s = [0] * NSUB
        x1_ld_p3 = [0] * NSUB

        wait("sync", sT, sT.n)
        wait("sync", sC, sC.n)
        wait("sync", sV, sV.n)
        op("sync", lambda e: e.dma_start(
            out=x_t[:, :, :],
            in_=x1_d[0:4, :, :].rearrange("t p c -> p t c")), inc=sL, k=16)
        x1_ld_p3[0] = sL.n

        p3_tqd = [0]

        def p3_produce_tile(sl, tt):
            ti = sl * 4 + tt
            wait("scalar", sL, x1_ld_p3[sl])
            wait("scalar", sV, max(b1i_done, p3_tqd[0]))
            op("scalar", lambda e, tt_=tt, ti_=ti: e.activation(
                a_xn[:, :], x_t[:, tt_, :], ACT.Identity,
                bias=stats[:, ti_, 1:2], scale=stats[:, ti_, 0:1]), inc=sC)
            xn2d = sC.n
            xn2_done_s[sl] = xn2d
            wait("vector", sC, xn2d)
            op("vector", lambda e: e.tensor_scalar(
                out=p3_tq[:, :], in0=a_xn[:, :], scalar1=scm[:, 1:2],
                scalar2=MAGIC, op0=OP.mult, op1=OP.add), inc=sV)
            p3_tqd[0] = sV.n
            op("vector", lambda e: e.tensor_scalar(
                out=p3_xint[:, 0:1024], in0=p3_tq[:, :], scalar1=MAGIC,
                scalar2=MAGIC, op0=OP.max, op1=OP.subtract), inc=sV)
            xq_done = sV.n
            wait("tensor", sV, xq_done)
            for dc in range(8):
                op("tensor", lambda e, dc_=dc: e.matmul(
                    ps7b[:, dc_ * 128:(dc_ + 1) * 128],
                    p3_xint[:, dc_ * 128:(dc_ + 1) * 128], identb[:, :],
                    is_transpose=True), inc=sT if dc == 7 else None)
            xtr = sT.n
            wait("vector", sT, xtr)
            if sl >= 2:
                wait("vector", sT, mm_done_s[sl - 2])
            op("vector", lambda e, tt_=tt, p_=sl % 2: e.tensor_copy(
                p3_xintT2[:, :, p_, tt_ * 128:(tt_ + 1) * 128],
                ps7b[:, :].rearrange("p (a c) -> p a c", a=8)), inc=sV)
            if tt == 3:
                xT_done_s[sl] = sV.n

        def p3_mm_chunk(sl, f0):
            pp = sl % 2
            wait("tensor", sV, xT_done_s[sl])
            for fc in range(f0, f0 + 8):
                b = fc % 4
                if sl * 32 + fc >= 4:
                    wait("tensor", sC, i1_evac_c[-4])
                for dc in range(8):
                    op("tensor", lambda e, b_=b, fc_=fc, dc_=dc, p_=pp:
                       e.matmul(ps[b_][:, :],
                                w1v[:, dc_, fc_ * 128:(fc_ + 1) * 128],
                                p3_xintT2[:, dc_, p_, :],
                                start=(dc_ == 0), stop=(dc_ == 7)),
                       inc=sT if dc == 7 else None)
                mm_done = sT.n
                wait("scalar", sT, mm_done)
                op("scalar", lambda e, b_=b, fc_=fc: e.copy(
                    p3_i1[:, fc_ % 4, :], ps[b_][:, :]), inc=sC)
                i1_evac_c.append(sC.n)
                wait("vector", sC, sC.n)
                op("vector", lambda e, fc_=fc: e.reduce_max(
                    lnw[:, 3:4], p3_i1[:, fc_ % 4, :], axis=AX), inc=sV,
                    serialize=True)
                op("vector", lambda e, fc_=fc: e.tensor_tensor(
                    out=mf[:, fc_:fc_ + 1], in0=mf[:, fc_:fc_ + 1],
                    in1=lnw[:, 3:4], op=OP.max), inc=sV, serialize=True)
                if fc % 4 == 3:
                    wait("sync", sC, sC.n)
                    op("sync", lambda e, s_=sl, ff=fc - 3: e.dma_start(
                        out=i1_d[s_, ff:ff + 4, :, :].rearrange(
                            "f p c -> p f c"), in_=p3_i1[:, :, :]), inc=sS,
                        k=16)
                    st_i1 = sS.n
                    wait("scalar", sS, st_i1)
                    wait("scalar", sV, sV.n)
            mm_done_s[sl] = sT.n

        for i in range(NSUB + 1):
            for k in range(4):
                if i < NSUB:
                    p3_produce_tile(i, k)
                if i >= 1:
                    p3_mm_chunk(i - 1, 8 * k)
            if i < NSUB - 1:
                wait("sync", sL, sL.n)
                wait("sync", sC, xn2_done_s[i])
                op("sync", lambda e, s_=i + 1: e.dma_start(
                    out=x_t[:, :, :],
                    in_=x1_d[s_ * 4:(s_ + 1) * 4, :, :].rearrange(
                        "t p c -> p t c")), inc=sL, k=16)
                x1_ld_p3[i + 1] = sL.n

        milestone("p3")
        p3_t_done = sT.n
        p3_c_end = sC.n
        p3_v_end = sV.n
        w2buf = [p3_wbuf, arena[:, 0:4096]]
        w2_ld = [0] * 8
        w2_q_done = [0] * 8

        def p4w_load(ft):
            if ft == 0:
                wait("sync", sV, p3_v_end)
                wait("sync", sT, p3_t_done)
            else:
                wait("sync", sL, w2_ld[ft - 1])
            if ft == 1:
                wait("sync", sC, p3_c_end)
            if ft >= 2:
                wait("sync", sV, w2_q_done[ft - 2])
            op("sync", lambda e, ft_=ft, w_=w2buf[ft % 2]: e.dma_start(
                out=w_, in_=w2_in[ft_ * 128:(ft_ + 1) * 128, :]), inc=sL, k=16)
            w2_ld[ft] = sL.n

        p4w_load(0)
        p4w_load(1)
        wait("vector", sT, p3_t_done)
        for ft in range(8):
            wb = w2buf[ft % 2]
            wait("vector", sL, w2_ld[ft])
            op("vector", lambda e, w_=wb: e.tensor_reduce(
                lnw[:, 0:1], w_, AX, OP.max,
                apply_absolute_value=True), inc=sV, serialize=True)
            op("vector", lambda e, ft_=ft: e.tensor_scalar(
                out=sw2[:, ft_:ft_ + 1], in0=lnw[:, 0:1], scalar1=EPS_Q,
                scalar2=R127, op0=OP.max, op1=OP.mult), inc=sV, serialize=True)
            op("vector", lambda e, ft_=ft: e.reciprocal(
                lnw[:, 1:2], sw2[:, ft_:ft_ + 1]), inc=sV, serialize=True)
            for hh in range(4):
                op("vector", lambda e, hh_=hh, w_=wb: e.tensor_scalar(
                    out=p3_tq[:, :], in0=w_[:, hh_ * 1024:(hh_ + 1) * 1024],
                    scalar1=lnw[:, 1:2], scalar2=MAGIC, op0=OP.mult,
                    op1=OP.add), inc=sV)
                op("vector", lambda e, hh_=hh: e.tensor_scalar(
                    out=p4_w2q[:, hh_ * 1024:(hh_ + 1) * 1024], in0=p3_tq[:, :],
                    scalar1=MAGIC, scalar2=None, op0=OP.subtract), inc=sV)
            wq2 = sV.n
            w2_q_done[ft] = wq2
            if ft + 2 < 8:
                p4w_load(ft + 2)
            wait("tensor", sV, wq2)
            wait("tensor", sT, p3_t_done)
            for g in range(4):
                wait("tensor", sV, sV.n)
                for q in range(8):
                    fc = g * 8 + q
                    op("tensor", lambda e, q_=q, fc_=fc: e.matmul(
                        ps7b[:, q_ * 128:(q_ + 1) * 128],
                        p4_w2q[:, fc_ * 128:(fc_ + 1) * 128], identb[:, :],
                        is_transpose=True), inc=sT if q == 7 else None)
                wt2 = sT.n
                wait("vector", sT, wt2)
                op("vector", lambda e, g_=g, ft_=ft: e.tensor_copy(
                    w2v[:, g_ * 8:(g_ + 1) * 8, ft_ * 128:(ft_ + 1) * 128],
                    ps7b[:, :].rearrange("p (a c) -> p a c", a=8)), inc=sV)
        w2_ready_v = sV.n

        wait("vector", sV, w2_ready_v)
        op("vector", lambda e: e.tensor_tensor(
            out=ccol[:, :], in0=mf[:, :], in1=b1i[:, :], op=OP.add), inc=sV,
            serialize=True)
        op("vector", lambda e: e.tensor_tensor(
            out=ccol[:, :], in0=ccol[:, :], in1=s1sw[:, :], op=OP.mult),
            inc=sV, serialize=True)
        op("vector", lambda e: e.reduce_max(lnw[:, 5:6], ccol[:, :], axis=AX),
           inc=sV, serialize=True)
        hm_col = sV.n
        wait("tensor", sV, hm_col)
        op("tensor", lambda e: e.matmul(
            ps[0][0:1, 0:128], lnw[:, 5:6], identf[:, :], is_transpose=True),
            inc=sT)
        wait("vector", sT, sT.n)
        op("vector", lambda e: e.reduce_max(
            sc1[0:1, 3:4], ps[0][0:1, 0:128], axis=AX), inc=sV,
            serialize=True)
        op("vector", lambda e: e.tensor_scalar(
            out=sc1[0:1, 3:4], in0=sc1[0:1, 3:4], scalar1=0.0, scalar2=None,
            op0=OP.max), inc=sV, serialize=True)
        wait("gpsimd", sV, sV.n)
        op("gpsimd", lambda e: e.dma_start(out=m3_i.ap(), in_=sc1[0:1, 3:4]),
           inc=sG, k=16)
        wait("gpsimd", sG, sG.n)
        op("gpsimd", lambda e: e.collective_compute(
            "AllReduce", OP.max, replica_groups=[list(range(NCORES))],
            ins=[m3_i.ap().opt()], outs=[m3_o.ap().opt()]), inc=sCC)
        wait("gpsimd", sCC, sCC.n)
        op("gpsimd", lambda e: e.dma_start(out=sc1[0:1, 3:4], in_=m3_o.ap()),
           inc=sG, k=16)
        wait("vector", sG, sG.n)
        op("vector", lambda e: e.tensor_scalar(
            out=srow[0:1, 4:5], in0=sc1[0:1, 3:4], scalar1=EPS_Q,
            scalar2=R255, op0=OP.max, op1=OP.mult), inc=sV, serialize=True)
        op("vector", lambda e: e.reciprocal(srow[0:1, 3:4], srow[0:1, 4:5]),
           inc=sV, serialize=True)
        s2_ready = sV.n
        wait("tensor", sV, s2_ready)
        op("tensor", lambda e: e.matmul(
            ps[0][:, 0:16], ones1[:, :], srow[0:1, 0:16], start=True,
            stop=True), inc=sT)
        wait("vector", sT, sT.n)
        op("vector", lambda e: e.tensor_copy(scm[:, 3:5], ps[0][:, 3:5]),
           inc=sV, serialize=True)
        op("vector", lambda e: e.tensor_scalar(
            out=ccol[:, :], in0=s1sw[:, :], scalar1=scm[:, 3:4], scalar2=None,
            op0=OP.mult), inc=sV, serialize=True)
        op("vector", lambda e: e.tensor_tensor(
            out=bcol[:, :], in0=b1i[:, :], in1=ccol[:, :], op=OP.mult),
            inc=sV, serialize=True)
        op("vector", lambda e: e.tensor_scalar(
            out=s2sw[:, :], in0=sw2[:, :], scalar1=scm[:, 4:5], scalar2=None,
            op0=OP.mult), inc=sV, serialize=True)
        op("vector", lambda e: e.reciprocal(b2i[:, :], s2sw[:, :]), inc=sV,
           serialize=True)
        op("vector", lambda e: e.tensor_tensor(
            out=b2i[:, :], in0=b2_s[:, :], in1=b2i[:, :], op=OP.mult), inc=sV,
            serialize=True)
        op("vector", lambda e: e.tensor_scalar(
            out=b2i[:, :], in0=b2i[:, :], scalar1=MAGIC, scalar2=None,
            op0=OP.add), inc=sV, serialize=True)
        op("vector", lambda e: e.tensor_scalar(
            out=b2i[:, :], in0=b2i[:, :], scalar1=MAGIC, scalar2=None,
            op0=OP.subtract), inc=sV, serialize=True)
        op("vector", lambda e: e.tensor_tensor(
            out=bq2v[:, :], in0=b2i[:, :], in1=s2sw[:, :], op=OP.mult),
            inc=sV, serialize=True)
        rows_ready = sV.n
        wait("tensor", sV, rows_ready)
        op("tensor", lambda e: e.matmul(
            ps[0][0:8, 0:128], s2sw[:, :], identf[:, :], is_transpose=True))
        op("tensor", lambda e: e.matmul(
            ps[1][0:8, 0:128], bq2v[:, :], identf[:, :], is_transpose=True),
            inc=sT)
        wait("vector", sT, sT.n)
        op("vector", lambda e: e.tensor_copy(s2row[:, :], ps[0][0:8, 0:128]),
           inc=sV)
        op("vector", lambda e: e.tensor_copy(bq2row[:, :], ps[1][0:8, 0:128]),
           inc=sV)
        wait("gpsimd", sV, sV.n)
        op("gpsimd", lambda e: e.dma_start(
            out=rowb_d.ap()[0], in_=s2row[:, :]), inc=sG, k=16)
        op("gpsimd", lambda e: e.dma_start(
            out=rowb_d.ap()[1], in_=bq2row[:, :]), inc=sG, k=16)
        wait("gpsimd", sG, sG.n)
        op("gpsimd", lambda e: e.dma_start(
            out=s2rowf[0:1, :],
            in_=rowb_d.ap()[0].rearrange("j c -> (j c)").unsqueeze(0)),
            inc=sG, k=16)
        op("gpsimd", lambda e: e.dma_start(
            out=bq2rowf[0:1, :],
            in_=rowb_d.ap()[1].rearrange("j c -> (j c)").unsqueeze(0)),
            inc=sG, k=16)
        wait("tensor", sG, sG.n)
        for i_, (bk, rowf) in enumerate(
                ((2, s2rowf), (3, s2rowf), (4, bq2rowf), (5, bq2rowf))):
            co = (i_ % 2) * 512
            op("tensor", lambda e, b_=bk, co_=co, r_=rowf: e.matmul(
                ps[b_][:, :], ones1[:, :], r_[0:1, co_:co_ + 512],
                start=True, stop=True), inc=sT if i_ == 3 else None)
        wait("vector", sT, sT.n)
        op("vector", lambda e: e.tensor_copy(p4_sb2[:, 0:512], ps[2][:, :]))
        op("vector", lambda e: e.tensor_copy(p4_sb2[:, 512:1024], ps[3][:, :]))
        op("vector", lambda e: e.tensor_copy(p4_bq2[:, 0:512], ps[4][:, :]))
        op("vector", lambda e: e.tensor_copy(p4_bq2[:, 512:1024], ps[5][:, :]),
           inc=sV)
        p4_consts = sV.n

        milestone("p4w")
        wait("sync", sS, sS.n)
        if debug_dump:
            wait("sync", sV, sV.n)
            op("sync", lambda e: e.dma_start(out=dbg_w2[:, :, :], in_=w2v[:, :, :]), inc=sS, k=16)
            op("sync", lambda e: e.dma_start(out=dbg_sb[:, 0:1024], in_=p4_sb2[:, :]), inc=sS, k=16)
            op("sync", lambda e: e.dma_start(out=dbg_sb[:, 1024:2048], in_=p4_bq2[:, :]), inc=sS, k=16)
            op("sync", lambda e: e.dma_start(out=dbg_cb[:, 0:32], in_=ccol[:, :]), inc=sS, k=16)
            op("sync", lambda e: e.dma_start(out=dbg_cb[:, 32:64], in_=bcol[:, :]), inc=sS, k=16)
            op("sync", lambda e: e.dma_start(out=dbg_cb[:, 64:96], in_=s1sw[:, :]), inc=sS, k=16)
            op("sync", lambda e: e.dma_start(out=dbg_cb[:, 96:128], in_=b1i[:, :]), inc=sS, k=16)
            op("sync", lambda e: e.dma_start(out=dbg_cb[:, 128:133], in_=scm[:, 0:5]), inc=sS, k=16)
        NF = 16 * 32
        i1_ld = [0] * NF
        t0_inc = [0] * NF
        t1_inc = [0] * NF
        hint_inc = [0] * NF
        fc_mm = [0] * NF
        ep_mult_v = [0] * 16
        ep_add_v = [0] * 16
        out_st = [0] * 16
        x1_ld4 = [0] * 16

        wait("sync", sT, sT.n)
        wait("sync", sC, sC.n)
        wait("sync", sV, sV.n)

        ep_mult_bank = [[0] * 8 for _ in range(16)]

        def emit_epilogue_mults(sl):
            if sl > 0:
                wait("vector", sG, out_st[sl - 1])
            wait("vector", sT, fc_mm[sl * 32 + 31])
            for tt in range(4):
                for hh in range(2):
                    b = tt * 2 + hh
                    op("vector", lambda e, b_=b, hh_=hh, tt_=tt:
                       e.tensor_tensor(
                           out=p4_out[:, tt_, hh_ * 512:(hh_ + 1) * 512],
                           in0=ps[b_][:, :],
                           in1=p4_sb2[:, hh_ * 512:(hh_ + 1) * 512],
                           op=OP.mult), inc=sV)
                    ep_mult_bank[sl][b] = sV.n
            ep_mult_v[sl] = sV.n

        def emit_epilogue_adds(sl):
            wait("vector", sG, x1_ld4[sl])
            for tt in range(4):
                op("vector", lambda e, tt_=tt: e.tensor_tensor(
                    out=p4_x1[:, tt_, :], in0=p4_x1[:, tt_, :],
                    in1=p4_bq2[:, :], op=OP.add), inc=sV)
            for tt in range(4):
                op("vector", lambda e, tt_=tt: e.tensor_tensor(
                    out=p4_out[:, tt_, :], in0=p4_out[:, tt_, :],
                    in1=p4_x1[:, tt_, :], op=OP.add), inc=sV)
            ep_add_v[sl] = sV.n
            wait("gpsimd", sV, ep_add_v[sl])
            op("gpsimd", lambda e, s_=sl: e.dma_start(
                out=out_d[s_ * 512:(s_ + 1) * 512, :].rearrange(
                    "(t p) c -> p t c", p=128), in_=p4_out[:, :, :]), inc=sG,
                k=16)
            out_st[sl] = sG.n

        for F in range(NF):
            sl, fc = divmod(F, 32)
            if F >= 4:
                wait("sync", sC, t0_inc[F - 4])

            if F >= 1:
                wait("sync", sL, i1_ld[F - 1])
            op("sync", lambda e, s_=sl, f_=fc, r=F % 4: e.dma_start(
                out=p4_i1[:, r, :], in_=i1_d[s_, f_, :, :]), inc=sL, k=16)
            i1_ld[F] = sL.n
            if fc == 9:
                if sl > 0:
                    wait("gpsimd", sV, ep_add_v[sl - 1])
                op("gpsimd", lambda e, s_=sl: e.dma_start(
                    out=p4_x1[:, :, :],
                    in_=x1_d[s_ * 4:(s_ + 1) * 4, :, :].rearrange(
                        "t p c -> p t c")), inc=sG, k=16)
                x1_ld4[sl] = sG.n

            wait("scalar", sL, i1_ld[F])
            wait("scalar", sV, p4_consts)
            if F >= 2:
                wait("scalar", sV, t1_inc[F - 2])
            op("scalar", lambda e, f_=fc, r=F % 4, r0=F % 2: e.activation(
                p4_t0[:, r0, :], p4_i1[:, r, :], ACT.Identity,
                bias=bcol[:, f_:f_ + 1], scale=ccol[:, f_:f_ + 1]), inc=sC)
            t0_inc[F] = sC.n

            wait("vector", sC, t0_inc[F])
            op("vector", lambda e, r0=F % 2: e.tensor_scalar(
                out=p4_t1[:, :], in0=p4_t0[:, r0, :], scalar1=MAGIC,
                scalar2=None, op0=OP.add), inc=sV)
            t1_inc[F] = sV.n
            if F >= 8:
                wait("vector", sT, fc_mm[F - 8])
            op("vector", lambda e, r=F % 8: e.tensor_scalar(
                out=p4_hint[:, r, :], in0=p4_t1[:, :], scalar1=MAGIC,
                scalar2=MAGIC, op0=OP.max, op1=OP.subtract), inc=sV)
            hint_inc[F] = sV.n
            if debug_dump:
                wait("sync", sV, hint_inc[F])
                op("sync", lambda e, F_=F, r=F % 8: e.dma_start(
                    out=dbg_ha[F_], in_=p4_hint[:, r, :]), inc=sS, k=16)
            if fc == 2 and sl > 0:
                emit_epilogue_mults(sl - 1)
            if fc == 8 and sl > 0:
                emit_epilogue_adds(sl - 1)

            def emit_tensor_fc(Ft, slack_v):
                slt, fct = divmod(Ft, 32)
                wait("tensor", sV, slack_v)
                for tt in range(4):
                    for hh in range(2):
                        b = tt * 2 + hh
                        if fct == 0 and slt > 0:
                            wait("tensor", sV, ep_mult_bank[slt - 1][b])
                        op("tensor", lambda e, b_=b, tt_=tt, hh_=hh,
                           f_=fct, r=Ft % 8: e.matmul(
                               ps[b_][:, :],
                               p4_hint[:, r, tt_ * 128:(tt_ + 1) * 128],
                               w2v[:, f_, hh_ * 512:(hh_ + 1) * 512],
                               start=(f_ == 0), stop=(f_ == 31)),
                           inc=sT if (tt == 3 and hh == 1) else None)
                fc_mm[Ft] = sT.n

            if F >= 2:
                emit_tensor_fc(F - 2, hint_inc[F])
        op("scalar", lambda e: e.memzero(lnw[:, 1:2]), inc=sC)
        op("vector", lambda e: e.memzero(p4_t1[:, 0:1]), inc=sV)
        op("vector", lambda e: e.memzero(p4_t1[:, 1:2]), inc=sV)
        emit_tensor_fc(NF - 2, sV.n)
        emit_tensor_fc(NF - 1, sV.n)
        emit_epilogue_mults(15)
        emit_epilogue_adds(15)

        milestone("full")
        wait("sync", sS, sS.n)
        wait("gpsimd", sG, sG.n)
        if stop_stage is not None:
            cut = milestones[stop_stage]
            for k in ENGINES:
                del plan[k][cut[k]:]

        with nc.Block() as block:
            @block.sync
            def _(eng):
                for f in plan["sync"]:
                    f(eng)

            @block.tensor
            def _(eng):
                for f in plan["tensor"]:
                    f(eng)

            @block.vector
            def _(eng):
                for f in plan["vector"]:
                    f(eng)

            @block.scalar
            def _(eng):
                for f in plan["scalar"]:
                    f(eng)

            @block.gpsimd
            def _(eng):
                for f in plan["gpsimd"]:
                    f(eng)

    return nc


def _host_inputs(L0_attn_x, L0_attn_Wq, L0_attn_Wk, L0_attn_Wv, L0_attn_Wo,
                 L0_attn_gate, L0_attn_M0, L0_ffn_w1, L0_ffn_b1, L0_ffn_w2,
                 L0_ffn_b2):
    f32 = np.float32
    x = np.asarray(L0_attn_x, dtype=f32)
    wq = np.asarray(L0_attn_Wq, dtype=f32)
    wk = np.asarray(L0_attn_Wk, dtype=f32)
    wv = np.asarray(L0_attn_Wv, dtype=f32)
    wo = np.asarray(L0_attn_Wo, dtype=f32)
    gate = np.asarray(L0_attn_gate, dtype=f32).reshape(1, 1)
    m0 = np.asarray(L0_attn_M0, dtype=f32)
    w1 = np.ascontiguousarray(np.asarray(L0_ffn_w1, dtype=f32))
    b1 = np.asarray(L0_ffn_b1, dtype=f32)
    w2 = np.ascontiguousarray(np.asarray(L0_ffn_w2, dtype=f32))
    b2 = np.asarray(L0_ffn_b2, dtype=f32)

    wqt = np.ascontiguousarray(wq.T)
    wkt = np.ascontiguousarray(wk.T)
    wvt = np.ascontiguousarray(wv.T)
    wot = np.ascontiguousarray(wo.T)
    m0t = np.ascontiguousarray(m0.T)
    b1r = np.ascontiguousarray(b1.reshape(FF // 128, 128).T)
    b2r = np.ascontiguousarray(b2.reshape(D // 128, 128).T)

    maskv = np.zeros((128, 128), dtype=f32)
    for kr in range(128):
        for qr in range(128):
            if kr // 8 < qr // 8:
                maskv[kr, qr] = 1.0
    identf = np.eye(128, dtype=f32)
    identb = np.eye(128, dtype=f32).astype(ml_dtypes.bfloat16)
    taus = np.arange(SUB) // B
    dqrow = (DECAY ** taus.astype(np.float64)).astype(f32).reshape(1, SUB)
    dkrow = ((DECAY ** (-1.0 - taus.astype(np.float64))) / B).astype(
        f32).reshape(1, SUB)

    in_maps = []
    for c in range(NCORES):
        xs = np.ascontiguousarray(
            x[:, c * L:(c + 1) * L, :].transpose(1, 0, 2).reshape(RPC, D))
        coef = np.zeros((1, 16), dtype=f32)
        for j in range(c):
            coef[0, j] = f32(DECAY ** (L * (c - 1 - j)))
        coef[0, 8] = f32(DECAY ** (L * c))
        in_maps.append(dict(
            xr=xs, wqt=wqt, wkt=wkt, wvt=wvt, wot=wot, m0t=m0t,
            gate_in=gate, w1_in=w1, b1_in=b1r, w2_in=w2, b2_in=b2r,
            maskc=maskv, identf_in=identf, identb_in=identb,
            dqrow_in=dqrow, dkrow_in=dkrow, coef_in=coef))
    return in_maps


def kernel(**inputs):
    in_maps = _host_inputs(**inputs)
    nc = _build_nc()
    res = run_bass_kernel_spmd(nc, in_maps, core_ids=list(range(NCORES)))
    outs = []
    for c in range(NCORES):
        oc = np.asarray(res.results[c]["out"])
        outs.append(oc.reshape(L, B, D).transpose(1, 0, 2))
    return np.ascontiguousarray(np.concatenate(outs, axis=1))


# version 16
# speedup vs baseline: 1.0061x; 1.0061x over previous
import numpy as np
import ml_dtypes
from contextlib import ExitStack

import concourse.bass as bass
import concourse.mybir as mybir
from concourse.bass_utils import run_bass_kernel_spmd

F32 = mybir.dt.float32
BF16 = mybir.dt.bfloat16
OP = mybir.AluOpType
ACT = mybir.ActivationFunctionType
AX = mybir.AxisListType.X

MAGIC = 12582912.0
DECAY = 0.99
NCORES = 8
L = 1024
B = 8
RPC = L * B
D = 1024
M = 256
FF = 4096
C = 64
SUB = C * B
NSUB = L // C
EPS_LN = 1e-5
EPS_Q = 1e-8
R255 = float(np.float32(1.0 / 255.0))
R127 = float(np.float32(1.0 / 127.0))
RD = float(np.float32(1.0 / D))
D64 = float(np.float32(DECAY ** 64))

ENGINES = ("sync", "tensor", "vector", "scalar", "gpsimd")


class _Sem:
    __slots__ = ("h", "n")

    def __init__(self, h):
        self.h = h
        self.n = 0


def _build_nc(debug_dump=False, stop_stage=None):
    nc = bass.Bass("TRN2", target_bir_lowering=False, detect_race_conditions=False)
    _dk = "ExternalOutput" if debug_dump else "Internal"

    xr = nc.dram_tensor("xr", [RPC, D], F32, kind="ExternalInput").ap()
    wqt = nc.dram_tensor("wqt", [D, M], F32, kind="ExternalInput").ap()
    wkt = nc.dram_tensor("wkt", [D, M], F32, kind="ExternalInput").ap()
    wvt = nc.dram_tensor("wvt", [D, M], F32, kind="ExternalInput").ap()
    wot = nc.dram_tensor("wot", [M, D], F32, kind="ExternalInput").ap()
    m0t = nc.dram_tensor("m0t", [M, M], F32, kind="ExternalInput").ap()
    gate_in = nc.dram_tensor("gate_in", [1, 1], F32, kind="ExternalInput").ap()
    w1_in = nc.dram_tensor("w1_in", [FF, D], F32, kind="ExternalInput").ap()
    b1_in = nc.dram_tensor("b1_in", [128, FF // 128], F32, kind="ExternalInput").ap()
    w2_in = nc.dram_tensor("w2_in", [D, FF], F32, kind="ExternalInput").ap()
    b2_in = nc.dram_tensor("b2_in", [128, D // 128], F32, kind="ExternalInput").ap()
    maskc = nc.dram_tensor("maskc", [128, 128], F32, kind="ExternalInput").ap()
    identf_in = nc.dram_tensor("identf_in", [128, 128], F32, kind="ExternalInput").ap()
    identb_in = nc.dram_tensor("identb_in", [128, 128], BF16, kind="ExternalInput").ap()
    dqrow_in = nc.dram_tensor("dqrow_in", [1, SUB], F32, kind="ExternalInput").ap()
    dkrow_in = nc.dram_tensor("dkrow_in", [1, SUB], F32, kind="ExternalInput").ap()
    coef_in = nc.dram_tensor("coef_in", [1, 16], F32, kind="ExternalInput").ap()

    out_d = nc.dram_tensor("out", [RPC, D], F32, kind="ExternalOutput").ap()

    qg_d = nc.dram_tensor("qg_d", [NSUB, 128, 2 * SUB], F32, kind=_dk).ap()
    rp_d = nc.dram_tensor("rp_d", [NSUB, 128, 2 * SUB], F32, kind=_dk).ap()
    x1_d = nc.dram_tensor("x1_d", [64, 128, D], F32, kind=_dk).ap()
    i1_d = nc.dram_tensor("i1_d", [16, 32, 128, 512], F32, kind=_dk).ap()
    g_in = nc.dram_tensor("g_in", [128, 512], F32, kind="Internal")
    g_out = nc.dram_tensor("g_out", [NCORES * 128, 512], F32, kind="Internal")
    m2_i = nc.dram_tensor("m2_i", [1, 1], F32, kind="Internal")
    m2_o = nc.dram_tensor("m2_o", [1, 1], F32, kind="Internal")
    m3_i = nc.dram_tensor("m3_i", [1, 1], F32, kind="Internal")
    m3_o = nc.dram_tensor("m3_o", [1, 1], F32, kind="Internal")
    rowb_d = nc.dram_tensor("rowb_d", [2, 8, 128], F32, kind="Internal")
    if debug_dump:
        dbg_hint = nc.dram_tensor("dbg_hint", [128, 32, 256], BF16,
                                  kind="ExternalOutput").ap()
        dbg_cb = nc.dram_tensor("dbg_cb", [128, 144], F32,
                                kind="ExternalOutput").ap()
        dbg_w2 = nc.dram_tensor("dbg_w2", [128, 32, 1024], BF16,
                                kind="ExternalOutput").ap()
        dbg_sb = nc.dram_tensor("dbg_sb", [128, 2048], F32,
                                kind="ExternalOutput").ap()
        dbg_ln = nc.dram_tensor("dbg_ln", [128, 64], F32,
                                kind="ExternalOutput").ap()
        dbg_ha = nc.dram_tensor("dbg_ha", [512, 128, 512], BF16,
                                kind="ExternalOutput").ap()
        dbg_xn = nc.dram_tensor("dbg_xn", [128, 1024], F32,
                                kind="ExternalOutput").ap()
        dbg_x0 = nc.dram_tensor("dbg_x0", [128, 1024], F32,
                                kind="ExternalOutput").ap()

    stack = ExitStack()
    with stack:
        sb = lambda name, shape, dt=F32: stack.enter_context(
            nc.sbuf_tensor(name, shape, dt))

        ones1 = sb("ones1", [1, 128])
        identf = sb("identf", [128, 128])
        identb = sb("identb", [128, 128], BF16)
        mask = sb("mask", [128, 128])
        wq_s = sb("wq_s", [128, 8, M])
        wk_s = sb("wk_s", [128, 8, M])
        wv_s = sb("wv_s", [128, 8, M])
        wo_s = sb("wo_s", [128, 2, D])
        m0_s = sb("m0_s", [128, 2, M])
        dqr = sb("dqr", [1, SUB])
        dkr = sb("dkr", [1, SUB])
        dq_b = sb("dq_b", [128, SUB])
        dk_b = sb("dk_b", [128, SUB])
        coefr = sb("coefr", [1, 16])
        coefb = sb("coefb", [128, 16])
        scm = sb("scm", [128, 16])
        srow = sb("srow", [1, 16])
        stats = sb("stats", [128, 64, 2])
        mf = sb("mf", [128, 32])
        sw1 = sb("sw1", [128, 32])
        s1sw = sb("s1sw", [128, 32])
        b1i = sb("b1i", [128, 32])
        ccol = sb("ccol", [128, 32])
        bcol = sb("bcol", [128, 32])
        b1_s = sb("b1_s", [128, 32])
        sw2 = sb("sw2", [128, 8])
        b2i = sb("b2i", [128, 8])
        b2_s = sb("b2_s", [128, 8])
        s2sw = sb("s2sw", [128, 8])
        bq2v = sb("bq2v", [128, 8])
        s2row = sb("s2row", [8, 128])
        bq2row = sb("bq2row", [8, 128])
        s2rowf = sb("s2rowf", [1, 1024])
        bq2rowf = sb("bq2rowf", [1, 1024])
        zt = sb("zt", [128, 2, 2, M])
        wc_s = sb("wc_s", [128, 2, M])
        lnw = sb("lnw", [128, 16])
        xmax = sb("xmax", [128, 1])
        sc1 = sb("sc1", [1, 4])
        wint = sb("wint", [128, 32768], BF16)

        AR = 17408
        arena = sb("arena", [128, AR])

        def carve(off, n):
            assert off + n <= AR, (off, n)
            return arena[:, off:off + n], off + n

        o = 0
        a_x, o = carve(o, 4096)
        a_xn, o = carve(o, 1024)
        a_xnt, o = carve(o, 4096)
        a_q, o = carve(o, 1024)
        a_qg, o = carve(o, 1024)
        a_k, o = carve(o, 1024)
        a_kt, o = carve(o, 1024)
        a_v, o = carve(o, 1024)
        a_a, o = carve(o, 2048)
        a_r, o = carve(o, 1024)
        assert o <= AR

        x_t = a_x.rearrange("p (t c) -> p t c", t=4)
        xnt = a_xnt.rearrange("p (dc c) -> p dc c", dc=8)
        qt = a_q.rearrange("p (mc c) -> p mc c", mc=2)
        qgt = a_qg.rearrange("p (mc c) -> p mc c", mc=2)
        kt_ = a_k.rearrange("p (mc c) -> p mc c", mc=2)
        ktok = a_kt.rearrange("p (t c) -> p t c", t=4)
        v_s = a_v.rearrange("p (t c) -> p t c", t=4)
        a_sb = a_a.rearrange("p (k c) -> p k c", k=4)
        rt = a_r.rearrange("p (ic c) -> p ic c", ic=2)

        x1b = a_xnt.rearrange("p (t c) -> p t c", t=4)
        rpb = a_k.rearrange("p (ic c) -> p ic c", ic=2)
        gall = a_x.rearrange("p (j c) -> p j c", j=8)
        gacc = a_xn

        p3_tq = arena[:, 4096 + 1024 + 4096:4096 + 1024 + 4096 + 1024]
        p3_xint = a_k.bitcast(BF16)
        p3_xintT = a_xnt.bitcast(BF16)[:, 0:4096].rearrange(
            "p (dc c) -> p dc c", dc=8)
        p4_w2q = a_xnt.bitcast(BF16)[:, 0:4096]
        p3_i1 = a_a.rearrange("p (g c) -> p g c", g=4)
        p3_wbuf = arena[:, 13312:13312 + 4096]

        p4_hint = arena[:, 0:2048].bitcast(BF16).rearrange(
            "p (r c) -> p r c", r=8)
        p4_i1 = arena[:, 2048:2048 + 2048].rearrange(
            "p (r c) -> p r c", r=4)
        p4_t0 = arena[:, 4096:4096 + 1024].rearrange(
            "p (r c) -> p r c", r=2)
        p4_t1 = arena[:, 5120:5120 + 512]
        p4_x1 = arena[:, 5632:5632 + 4096].rearrange(
            "p (t c) -> p t c", t=4)
        p4_out = arena[:, 9728:9728 + 4096].rearrange(
            "p (t c) -> p t c", t=4)
        p4_u = arena[:, 13824:13824 + 512]
        p4_sb2 = arena[:, 14336:14336 + 1024]
        p4_bq2 = arena[:, 15360:15360 + 1024]

        w1v = wint.rearrange("p (dc f) -> p dc f", dc=8)
        w2v = wint.rearrange("p (fc d) -> p fc d", fc=32)

        ps = [stack.enter_context(nc.psum_tensor(f"ps{i}", [128, 512], F32))
              for i in range(8)]
        ps7b = ps[7].ap().bitcast(BF16)

        sL = _Sem(stack.enter_context(nc.semaphore("sL")))
        sS = _Sem(stack.enter_context(nc.semaphore("sS")))
        sG = _Sem(stack.enter_context(nc.semaphore("sG")))
        sT = _Sem(stack.enter_context(nc.semaphore("sT")))
        sV = _Sem(stack.enter_context(nc.semaphore("sV")))
        sC = _Sem(stack.enter_context(nc.semaphore("sC")))
        sCC = _Sem(stack.enter_context(nc.semaphore("sCC")))

        plan = {k: [] for k in ENGINES}
        lastw = {k: {} for k in ENGINES}
        milestones = {}

        def milestone(name):
            milestones[name] = {k: len(v) for k, v in plan.items()}

        def wait(eng, sem, val=None):
            v = sem.n if val is None else val
            if v <= 0:
                return
            if lastw[eng].get(id(sem), -1) >= v:
                return
            lastw[eng][id(sem)] = v
            plan[eng].append(lambda e, s=sem.h, vv=v: e.wait_ge(s, vv))

        def op(eng, fn, inc=None, k=1, serialize=False):
            if inc is None:
                plan[eng].append(fn)
            else:
                inc.n += k
                plan[eng].append(
                    lambda e, f=fn, s=inc.h, kk=k: f(e).then_inc(s, kk))
                if serialize:
                    v = inc.n
                    plan[eng].append(
                        lambda e, s=inc.h, vv=v: e.wait_ge(s, vv))
                    lastw[eng][id(inc)] = max(lastw[eng].get(id(inc), -1), v)

        for dst, src in [(wq_s[:, :, :], wqt.rearrange("(dc p) m -> p dc m", p=128)),
                         (wk_s[:, :, :], wkt.rearrange("(dc p) m -> p dc m", p=128)),
                         (wv_s[:, :, :], wvt.rearrange("(dc p) m -> p dc m", p=128)),
                         (wo_s[:, :, :], wot.rearrange("(mc p) h -> p mc h", p=128)),
                         (m0_s[:, :, :], m0t.rearrange("(jc p) i -> p jc i", p=128)),
                         (mask[:, :], maskc),
                         (identf[:, :], identf_in),
                         (identb[:, :], identb_in),
                         (dqr[:, :], dqrow_in),
                         (dkr[:, :], dkrow_in),
                         (coefr[:, :], coef_in),
                         (sc1[:, 0:1], gate_in),
                         (b1_s[:, :], b1_in),
                         (b2_s[:, :], b2_in)]:
            op("sync", lambda e, d=dst, s=src: e.dma_start(out=d, in_=s),
               inc=sL, k=16)
        consts_loaded = sL.n

        op("vector", lambda e: e.memset(ones1[:, :], 1.0), inc=sV)
        op("vector", lambda e: e.memset(zt[:, 0, :, :], 0.0), inc=sV)
        op("vector", lambda e: e.memset(mf[:, :], -3.0e38), inc=sV)
        op("vector", lambda e: e.memset(xmax[:, :], -3.0e38), inc=sV)
        op("vector", lambda e: e.memset(srow[:, :], 0.0), inc=sV)
        op("vector", lambda e: e.memset(lnw[:, :], 0.0), inc=sV)
        wait("vector", sL, consts_loaded)
        op("vector", lambda e: e.memset(sc1[0:1, 2:4], 0.0), inc=sV)
        wait("scalar", sL, consts_loaded)
        op("scalar", lambda e: e.activation(sc1[0:1, 1:2], sc1[0:1, 0:1],
                                            ACT.Sigmoid), inc=sC)
        wait("tensor", sL, consts_loaded)
        wait("tensor", sV, sV.n)
        wait("tensor", sC, sC.n)
        op("tensor", lambda e: e.matmul(ps[0][:, :], ones1[:, :], dqr[:, :],
                                        start=True, stop=True))
        op("tensor", lambda e: e.matmul(ps[1][:, :], ones1[:, :], dkr[:, :],
                                        start=True, stop=True))
        op("tensor", lambda e: e.matmul(ps[2][:, 0:16], ones1[:, :], coefr[:, :],
                                        start=True, stop=True))
        op("tensor", lambda e: e.matmul(ps[2][:, 16:17], ones1[:, :],
                                        sc1[0:1, 1:2], start=True, stop=True),
           inc=sT)
        wait("vector", sT, sT.n)
        op("vector", lambda e: e.tensor_copy(dq_b[:, :], ps[0][:, :]), inc=sV)
        op("vector", lambda e: e.tensor_copy(dk_b[:, :], ps[1][:, :]), inc=sV)
        op("vector", lambda e: e.tensor_copy(coefb[:, :], ps[2][:, 0:16]), inc=sV)
        op("vector", lambda e: e.tensor_copy(scm[:, 0:1], ps[2][:, 16:17]),
           inc=sV)
        const_done_v = sV.n
        const_done_t = sT.n

        milestone("consts")
        ln_evac_v = [0] * 5
        z_update_v = const_done_v
        last_tr = 0
        qg_spill_g = 0
        rp_spill_g = 0
        xld = [0] * 64
        xn_done_g = [0] * 64
        rs_done_g = [0] * 64
        xld_prev = 0

        def emit_x_tile_load(g):
            nonlocal_ = None
            if g >= 4:
                wait("sync", sC, xn_done_g[g - 4])
                wait("sync", sV, rs_done_g[g - 4])
            if g >= 1:
                wait("sync", sL, xld[g - 1])
            op("sync", lambda e, g_=g: e.dma_start(
                out=x_t[:, g_ % 4, :],
                in_=xr[g_ * 128:(g_ + 1) * 128, :]), inc=sL, k=16)
            xld[g] = sL.n

        for g in range(4):
            emit_x_tile_load(g)

        for s in range(NSUB):
            par = s % 2
            dsub = float(np.float32(DECAY ** (C * s)))

            for tt in range(4):
                wait("scalar", sL, xld[s * 4 + tt])
                wait("scalar", sT, last_tr)
                op("scalar", lambda e, tt_=tt: e.activation(
                    a_xn[:, :], x_t[:, tt_, :], ACT.Square,
                    accum_out=lnw[:, 4 + tt_:5 + tt_]), inc=sC)
                sq_done = sC.n
                wait("vector", sL, xld[s * 4 + tt])
                op("vector", lambda e, tt_=tt: e.reduce_sum(
                    lnw[:, tt_:tt_ + 1], x_t[:, tt_, :], axis=AX), inc=sV,
                    serialize=True)
                rs_done_g[s * 4 + tt] = sV.n
                wait("vector", sC, sq_done)
                op("vector", lambda e, tt_=tt: e.tensor_scalar(
                    out=lnw[:, 8:9], in0=lnw[:, tt_:tt_ + 1], scalar1=RD,
                    scalar2=None, op0=OP.mult), inc=sV, serialize=True)
                op("vector", lambda e: e.tensor_tensor(
                    out=lnw[:, 9:10], in0=lnw[:, 8:9], in1=lnw[:, 8:9],
                    op=OP.mult), inc=sV, serialize=True)
                op("vector", lambda e, tt_=tt: e.scalar_tensor_tensor(
                    out=lnw[:, 10:11], in0=lnw[:, 4 + tt_:5 + tt_], scalar=RD,
                    in1=lnw[:, 9:10], op0=OP.mult, op1=OP.subtract), inc=sV,
                    serialize=True)
                op("vector", lambda e: e.tensor_scalar(
                    out=lnw[:, 11:12], in0=lnw[:, 10:11], scalar1=EPS_LN,
                    scalar2=None, op0=OP.add), inc=sV, serialize=True)
                op("vector", lambda e: e.reciprocal(
                    lnw[:, 12:13], lnw[:, 11:12]), inc=sV, serialize=True)
                rec_done = sV.n
                wait("scalar", sV, rec_done)
                op("scalar", lambda e: e.activation(
                    lnw[:, 13:14], lnw[:, 12:13], ACT.Sqrt), inc=sC)
                rstd_done = sC.n
                wait("vector", sC, rstd_done)
                op("vector", lambda e: e.scalar_tensor_tensor(
                    out=lnw[:, 14:15], in0=lnw[:, 8:9], scalar=-1.0,
                    in1=lnw[:, 13:14], op0=OP.mult, op1=OP.mult), inc=sV,
                    serialize=True)
                nmr_done = sV.n
                wait("scalar", sV, nmr_done)
                op("scalar", lambda e, tt_=tt: e.activation(
                    a_xn[:, :], x_t[:, tt_, :], ACT.Identity,
                    bias=lnw[:, 14:15], scale=lnw[:, 13:14]), inc=sC)
                xn_done = sC.n
                xn_done_g[s * 4 + tt] = xn_done
                if debug_dump and s == 0:
                    wait("sync", sV, nmr_done)
                    op("sync", lambda e, tt_=tt: e.dma_start(
                        out=dbg_ln[:, tt_ * 16:(tt_ + 1) * 16],
                        in_=lnw[:, 0:16]), inc=sS, k=16)
                    if tt == 0:
                        wait("sync", sC, xn_done)
                        op("sync", lambda e: e.dma_start(
                            out=dbg_xn[:, :], in_=a_xn[:, :]), inc=sS, k=16)
                        op("sync", lambda e: e.dma_start(
                            out=dbg_x0[:, :], in_=x_t[:, 0, :]), inc=sS, k=16)
                        wait("scalar", sS, sS.n)
                        wait("vector", sS, sS.n)
                wait("tensor", sC, xn_done)
                wait("tensor", sV, ln_evac_v[-1] if s + tt else const_done_v)
                for dc in range(8):
                    bank, sl_ = (0, dc) if dc < 4 else (1, dc - 4)
                    last = dc == 7
                    op("tensor", lambda e, b=bank, q=sl_, dc_=dc: e.matmul(
                        ps[b][:, q * 128:(q + 1) * 128],
                        a_xn[:, dc_ * 128:(dc_ + 1) * 128], identf[:, :],
                        is_transpose=True), inc=sT if last else None)
                tr_done = sT.n
                last_tr = tr_done
                wait("vector", sT, tr_done)
                op("vector", lambda e, tt_=tt: e.tensor_copy(
                    xnt[:, 0:4, tt_ * 128:(tt_ + 1) * 128],
                    ps[0][:, :].rearrange("p (a c) -> p a c", a=4)), inc=sV)
                op("vector", lambda e, tt_=tt: e.tensor_copy(
                    xnt[:, 4:8, tt_ * 128:(tt_ + 1) * 128],
                    ps[1][:, :].rearrange("p (a c) -> p a c", a=4)), inc=sV)
                ln_evac_v.append(sV.n)

            xnt_done = sV.n
            for tt2 in range(4):
                g2 = (s + 1) * 4 + tt2
                if g2 < 64:
                    emit_x_tile_load(g2)

            wait("tensor", sV, xnt_done)
            for (wsrc, bank0) in ((wq_s, 2), (wk_s, 4)):
                for mc in range(2):
                    for dc in range(8):
                        op("tensor", lambda e, w=wsrc, b=bank0 + mc, mc_=mc,
                           dc_=dc: e.matmul(
                               ps[b][:, :],
                               w[:, dc_, mc_ * 128:(mc_ + 1) * 128],
                               xnt[:, dc_, :],
                               start=(dc_ == 0), stop=(dc_ == 7)),
                           inc=sT if dc == 7 else None)
            qk_done = sT.n
            wait("vector", sT, qk_done)
            wait("vector", sG, qg_spill_g)
            for mc in range(2):
                op("vector", lambda e, mc_=mc: e.tensor_tensor(
                    out=qt[:, mc_, :], in0=ps[2 + mc_][:, :], in1=dq_b[:, :],
                    op=OP.mult), inc=sV)
                op("vector", lambda e, mc_=mc, d_=dsub: e.tensor_scalar(
                    out=qgt[:, mc_, :], in0=qt[:, mc_, :], scalar1=d_,
                    scalar2=None, op0=OP.mult), inc=sV)
                op("vector", lambda e, mc_=mc: e.tensor_tensor(
                    out=kt_[:, mc_, :], in0=ps[4 + mc_][:, :], in1=dk_b[:, :],
                    op=OP.mult), inc=sV)
            qk_evac = sV.n
            wait("gpsimd", sV, qk_evac)
            op("gpsimd", lambda e, s_=s: e.dma_start(
                out=qg_d[s_, :, :], in_=a_qg), inc=sG, k=16)
            qg_spill_g = sG.n

            wait("tensor", sV, qk_evac)
            for tt in range(4):
                bank = 2 + tt // 2
                cofs = (tt % 2) * 256
                for dc in range(8):
                    op("tensor", lambda e, b=bank, co=cofs, tt_=tt, dc_=dc:
                       e.matmul(ps[b][:, co:co + 256],
                                xnt[:, dc_, tt_ * 128:(tt_ + 1) * 128],
                                wv_s[:, dc_, :],
                                start=(dc_ == 0), stop=(dc_ == 7)),
                       inc=sT if dc == 7 else None)
            v_done = sT.n
            wait("vector", sT, v_done)
            op("vector", lambda e: e.tensor_copy(
                v_s[:, 0:2, :], ps[2][:, :].rearrange("p (a c) -> p a c", a=2)),
                inc=sV)
            op("vector", lambda e: e.tensor_copy(
                v_s[:, 2:4, :], ps[3][:, :].rearrange("p (a c) -> p a c", a=2)),
                inc=sV)
            v_evac = sV.n

            wait("tensor", sV, v_evac)
            for tt in range(4):
                for mc in range(2):
                    idx = tt * 2 + mc
                    bank, sl_ = (0, idx) if idx < 4 else (1, idx - 4)
                    op("tensor", lambda e, b=bank, q=sl_, tt_=tt, mc_=mc:
                       e.matmul(ps[b][:, q * 128:(q + 1) * 128],
                                kt_[:, mc_, tt_ * 128:(tt_ + 1) * 128],
                                identf[:, :], is_transpose=True),
                       inc=sT if idx == 7 else None)
            ktok_tr = sT.n
            wait("vector", sT, ktok_tr)
            op("vector", lambda e: e.tensor_copy(
                ktok[:, 0:2, :], ps[0][:, :].rearrange("p (a c) -> p a c", a=2)),
                inc=sV)
            op("vector", lambda e: e.tensor_copy(
                ktok[:, 2:4, :], ps[1][:, :].rearrange("p (a c) -> p a c", a=2)),
                inc=sV)
            ktok_evac = sV.n

            a_banks = {0: 4, 1: 5, 2: 2, 3: 3}
            wait("tensor", sV, ktok_evac)
            for kt2 in range(4):
                b = a_banks[kt2]
                for mc in range(2):
                    op("tensor", lambda e, b_=b, kt3=kt2, mc_=mc: e.matmul(
                        ps[b_][:, kt3 * 128:512],
                        kt_[:, mc_, kt3 * 128:(kt3 + 1) * 128],
                        qt[:, mc_, kt3 * 128:512],
                        start=(mc_ == 0), stop=(mc_ == 1)),
                       inc=sT if (kt2 == 3 and mc == 1) else None)
            at_done = sT.n
            wait("vector", sT, at_done)
            for kt2 in range(4):
                b = a_banks[kt2]
                op("vector", lambda e, b_=b, kt3=kt2: e.tensor_tensor(
                    out=a_sb[:, kt3, kt3 * 128:(kt3 + 1) * 128],
                    in0=ps[b_][:, kt3 * 128:(kt3 + 1) * 128],
                    in1=mask[:, :], op=OP.mult), inc=sV)
                if kt2 < 3:
                    op("vector", lambda e, b_=b, kt3=kt2: e.tensor_copy(
                        a_sb[:, kt3, (kt3 + 1) * 128:512],
                        ps[b_][:, (kt3 + 1) * 128:512]), inc=sV)
            a_evac = sV.n

            wait("tensor", sV, a_evac)
            wait("tensor", sV, z_update_v)
            for ic in range(2):
                b = 6 + ic
                for jc in range(2):
                    op("tensor", lambda e, b_=b, ic_=ic, jc_=jc, p_=par:
                       e.matmul(ps[b_][:, :],
                                zt[:, p_, jc_, ic_ * 128:(ic_ + 1) * 128],
                                qt[:, jc_, :],
                                start=(jc_ == 0), stop=False))
                for kt2 in range(4):
                    op("tensor", lambda e, b_=b, ic_=ic, kt3=kt2: e.matmul(
                        ps[b_][:, kt3 * 128:512],
                        v_s[:, kt3, ic_ * 128:(ic_ + 1) * 128],
                        a_sb[:, kt3, kt3 * 128:512],
                        start=False, stop=(kt3 == 3)),
                       inc=sT if (ic == 1 and kt2 == 3) else None)
            r_done = sT.n
            wait("vector", sT, r_done)
            wait("vector", sG, rp_spill_g)
            for ic in range(2):
                op("vector", lambda e, ic_=ic: e.tensor_copy(
                    rt[:, ic_, :], ps[6 + ic_][:, :]), inc=sV)
            r_evac = sV.n
            wait("gpsimd", sV, r_evac)
            op("gpsimd", lambda e, s_=s: e.dma_start(
                out=rp_d[s_, :, :], in_=a_r), inc=sG, k=16)
            rp_spill_g = sG.n

            wait("tensor", sV, a_evac)
            for jc in range(2):
                for ut in range(4):
                    op("tensor", lambda e, jc_=jc, ut_=ut: e.matmul(
                        ps[4][:, jc_ * 256:(jc_ + 1) * 256],
                        ktok[:, ut_, jc_ * 128:(jc_ + 1) * 128],
                        v_s[:, ut_, :],
                        start=(ut_ == 0), stop=(ut_ == 3)),
                       inc=sT if (jc == 1 and ut == 3) else None)
            p_done = sT.n
            wait("vector", sT, p_done)
            op("vector", lambda e, p_=par: e.tensor_tensor(
                out=zt[:, 1 - p_, :, :].rearrange("p a c -> p (a c)"),
                in0=zt[:, p_, :, :].rearrange("p a c -> p (a c)"),
                in1=ps[4][:, :], op=OP.add), inc=sV)
            op("vector", lambda e, p_=par: e.tensor_scalar(
                out=zt[:, 1 - p_, :, :].rearrange("p a c -> p (a c)"),
                in0=zt[:, 1 - p_, :, :].rearrange("p a c -> p (a c)"),
                scalar1=D64, scalar2=None, op0=OP.mult), inc=sV)
            z_update_v = sV.n

        milestone("p1a")
        g_parity = NSUB % 2

        wait("gpsimd", sV, z_update_v)
        op("gpsimd", lambda e: e.dma_start(
            out=g_in.ap(), in_=zt[:, g_parity, :, :].rearrange(
                "p a c -> p (a c)")), inc=sG, k=16)
        wait("gpsimd", sG, sG.n)
        op("gpsimd", lambda e: e.collective_compute(
            "AllGather", OP.bypass, replica_groups=[list(range(NCORES))],
            ins=[g_in.ap().opt()], outs=[g_out.ap().opt()]), inc=sCC)
        wait("gpsimd", sCC, sCC.n)
        gall2 = a_xnt.rearrange("p (j c) -> p j c", j=8)
        op("gpsimd", lambda e: e.dma_start(
            out=gall2[:, :, :],
            in_=g_out.ap().rearrange("(j p) c -> p j c", p=128)), inc=sG, k=16)
        g_loaded = sG.n
        wait("vector", sG, g_loaded)
        op("vector", lambda e: e.tensor_scalar(
            out=gacc[:, 0:512], in0=gall2[:, 0, :], scalar1=coefb[:, 0:1],
            scalar2=None, op0=OP.mult), inc=sV)
        for j in range(1, NCORES):
            op("vector", lambda e, j_=j: e.scalar_tensor_tensor(
                out=gacc[:, 0:512], in0=gall2[:, j_, :],
                scalar=coefb[:, j_:j_ + 1], in1=gacc[:, 0:512],
                op0=OP.mult, op1=OP.add), inc=sV)
        op("vector", lambda e: e.scalar_tensor_tensor(
            out=wc_s[:, :, :].rearrange("p a c -> p (a c)"),
            in0=m0_s[:, :, :].rearrange("p a c -> p (a c)"),
            scalar=coefb[:, 8:9], in1=gacc[:, 0:512],
            op0=OP.mult, op1=OP.add), inc=sV)
        wc_done = sV.n

        milestone("cc1")
        qg_bufs = [qgt, a_q.rearrange("p (mc c) -> p mc c", mc=2)]
        rp_bufs = [rpb, a_kt.rearrange("p (mc c) -> p mc c", mc=2)]
        qg_raw = [a_qg, a_q]
        rp_raw = [a_k, a_kt]
        qg_ld_s = [0] * NSUB
        x_ld_s = [0] * NSUB
        rcorr_done_s = [0] * NSUB
        rf_done_s = [0] * NSUB
        x1stt_done_s = [0] * NSUB
        x1store_s = [0] * NSUB
        spills_done = max(qg_spill_g, rp_spill_g)

        def emit_p1b_loads(s):
            pb = s % 2
            if s == 0:
                wait("sync", sG, spills_done)
                wait("sync", sT, sT.n)
                wait("sync", sV, sV.n)
            else:
                wait("sync", sL, sL.n)
                if s >= 2:
                    wait("sync", sT, rcorr_done_s[s - 2])
                    wait("sync", sV, rf_done_s[s - 2])
            op("sync", lambda e, s_=s, p_=pb: e.dma_start(
                out=qg_raw[p_], in_=qg_d[s_, :, :]), inc=sL, k=16)
            op("sync", lambda e, s_=s, p_=pb: e.dma_start(
                out=rp_raw[p_], in_=rp_d[s_, :, :]), inc=sL, k=16)
            qg_ld_s[s] = sL.n

        def emit_p1b_xload(s):
            wait("sync", sL, sL.n)
            if s == 0:
                wait("sync", sC, xn_done_g[63])
                wait("sync", sV, rs_done_g[63])
            else:
                wait("sync", sV, x1stt_done_s[s - 1])
            op("sync", lambda e, s_=s: e.dma_start(
                out=x_t[:, :, :],
                in_=xr[s_ * SUB:(s_ + 1) * SUB, :].rearrange(
                    "(t p) c -> p t c", p=128)), inc=sL, k=16)
            x_ld_s[s] = sL.n

        emit_p1b_loads(0)
        emit_p1b_xload(0)
        for s in range(NSUB):
            pb = s % 2
            qgv = qg_bufs[pb]
            rpv = rp_bufs[pb]

            wait("tensor", sL, qg_ld_s[s])
            wait("tensor", sV, wc_done)
            if s >= 1:
                wait("tensor", sV, rf_done_s[s - 1])
            for ic in range(2):
                for jc in range(2):
                    op("tensor", lambda e, ic_=ic, jc_=jc, q_=qgv: e.matmul(
                        ps[6 + ic_][:, :],
                        wc_s[:, jc_, ic_ * 128:(ic_ + 1) * 128],
                        q_[:, jc_, :],
                        start=(jc_ == 0), stop=(jc_ == 1)),
                       inc=sT if (ic == 1 and jc == 1) else None)
            rc_done = sT.n
            rcorr_done_s[s] = rc_done
            wait("vector", sT, rc_done)
            for ic in range(2):
                op("vector", lambda e, ic_=ic, r_=rpv: e.tensor_tensor(
                    out=rt[:, ic_, :], in0=ps[6 + ic_][:, :],
                    in1=r_[:, ic_, :], op=OP.add), inc=sV)
            rf_done = sV.n
            rf_done_s[s] = rf_done

            if s + 1 < NSUB:
                emit_p1b_loads(s + 1)

            wait("tensor", sV, rf_done)
            x1_done_hist = []
            first_stt = True
            for tt in range(4):
                b0 = 2 + (tt % 2)
                b1 = 4 + (tt % 2)
                if tt >= 2:
                    wait("tensor", sV, x1_done_hist[tt - 2])
                for half, bb in ((0, b0), (1, b1)):
                    for ic in range(2):
                        op("tensor", lambda e, b_=bb, ic_=ic, tt_=tt, h_=half:
                           e.matmul(ps[b_][:, :],
                                    rt[:, ic_, tt_ * 128:(tt_ + 1) * 128],
                                    wo_s[:, ic_, h_ * 512:(h_ + 1) * 512],
                                    start=(ic_ == 0), stop=(ic_ == 1)),
                           inc=sT if ic == 1 else None)
                wo_done = sT.n
                wait("vector", sT, wo_done)
                wait("vector", sL, x_ld_s[s])
                if first_stt and s >= 1:
                    wait("vector", sG, x1store_s[s - 1])
                    first_stt = False
                for half, bb in ((0, b0), (1, b1)):
                    op("vector", lambda e, b_=bb, tt_=tt, h_=half:
                       e.scalar_tensor_tensor(
                           out=x1b[:, tt_, h_ * 512:(h_ + 1) * 512],
                           in0=ps[b_][:, :], scalar=scm[:, 0:1],
                           in1=x_t[:, tt_, h_ * 512:(h_ + 1) * 512],
                           op0=OP.mult, op1=OP.add), inc=sV)
                x1_done = sV.n
                x1_done_hist.append(x1_done)

                wait("scalar", sV, x1_done)
                op("scalar", lambda e, tt_=tt: e.activation(
                    a_xn[:, :], x1b[:, tt_, :], ACT.Square,
                    accum_out=lnw[:, 4:5]), inc=sC)
                sq2 = sC.n
                op("vector", lambda e, tt_=tt: e.reduce_sum(
                    lnw[:, 0:1], x1b[:, tt_, :], axis=AX), inc=sV,
                    serialize=True)
                wait("vector", sC, sq2)
                op("vector", lambda e: e.tensor_scalar(
                    out=lnw[:, 8:9], in0=lnw[:, 0:1], scalar1=RD,
                    scalar2=None, op0=OP.mult), inc=sV, serialize=True)
                op("vector", lambda e: e.tensor_tensor(
                    out=lnw[:, 9:10], in0=lnw[:, 8:9], in1=lnw[:, 8:9],
                    op=OP.mult), inc=sV, serialize=True)
                op("vector", lambda e: e.scalar_tensor_tensor(
                    out=lnw[:, 10:11], in0=lnw[:, 4:5], scalar=RD,
                    in1=lnw[:, 9:10], op0=OP.mult, op1=OP.subtract), inc=sV,
                    serialize=True)
                op("vector", lambda e: e.tensor_scalar(
                    out=lnw[:, 11:12], in0=lnw[:, 10:11], scalar1=EPS_LN,
                    scalar2=None, op0=OP.add), inc=sV, serialize=True)
                op("vector", lambda e: e.reciprocal(
                    lnw[:, 12:13], lnw[:, 11:12]), inc=sV, serialize=True)
                rec2 = sV.n
                wait("scalar", sV, rec2)
                tile_idx = s * 4 + tt
                op("scalar", lambda e, ti=tile_idx: e.activation(
                    stats[:, ti, 0:1], lnw[:, 12:13], ACT.Sqrt), inc=sC)
                rstd2 = sC.n
                wait("vector", sC, rstd2)
                op("vector", lambda e, ti=tile_idx: e.scalar_tensor_tensor(
                    out=stats[:, ti, 1:2], in0=lnw[:, 8:9], scalar=-1.0,
                    in1=stats[:, ti, 0:1], op0=OP.mult, op1=OP.mult), inc=sV,
                    serialize=True)
                op("vector", lambda e, tt_=tt: e.reduce_max(
                    lnw[:, 14:15], x1b[:, tt_, :], axis=AX), inc=sV,
                    serialize=True)
                op("vector", lambda e, ti=tile_idx: e.tensor_scalar(
                    out=lnw[:, 15:16], in0=lnw[:, 14:15],
                    scalar1=lnw[:, 8:9], scalar2=stats[:, ti, 0:1],
                    op0=OP.subtract, op1=OP.mult), inc=sV, serialize=True)
                op("vector", lambda e: e.tensor_tensor(
                    out=xmax[:, :], in0=xmax[:, :], in1=lnw[:, 15:16],
                    op=OP.max), inc=sV, serialize=True)
            slab_v = sV.n
            x1stt_done_s[s] = x1_done_hist[-1]
            wait("gpsimd", sV, slab_v)
            op("gpsimd", lambda e, s_=s: e.dma_start(
                out=x1_d[s_ * 4:(s_ + 1) * 4, :, :].rearrange(
                    "t p c -> p t c"), in_=x1b[:, :, :]), inc=sG, k=16)
            x1store_s[s] = sG.n
            if s + 1 < NSUB:
                emit_p1b_xload(s + 1)

        milestone("p1b")
        wait("tensor", sV, sV.n)
        op("tensor", lambda e: e.matmul(
            ps[0][0:1, 0:128], xmax[:, :], identf[:, :], is_transpose=True),
            inc=sT)
        wait("vector", sT, sT.n)
        op("vector", lambda e: e.reduce_max(
            sc1[0:1, 2:3], ps[0][0:1, 0:128], axis=AX), inc=sV,
            serialize=True)
        wait("gpsimd", sV, sV.n)
        op("gpsimd", lambda e: e.dma_start(out=m2_i.ap(), in_=sc1[0:1, 2:3]),
           inc=sG, k=16)
        wait("gpsimd", sG, sG.n)
        op("gpsimd", lambda e: e.collective_compute(
            "AllReduce", OP.max, replica_groups=[list(range(NCORES))],
            ins=[m2_i.ap().opt()], outs=[m2_o.ap().opt()]), inc=sCC)
        wait("gpsimd", sCC, sCC.n)
        op("gpsimd", lambda e: e.dma_start(out=sc1[0:1, 2:3], in_=m2_o.ap()),
           inc=sG, k=16)
        wait("vector", sG, sG.n)
        op("vector", lambda e: e.tensor_scalar(
            out=srow[0:1, 2:3], in0=sc1[0:1, 2:3], scalar1=EPS_Q,
            scalar2=R255, op0=OP.max, op1=OP.mult), inc=sV, serialize=True)
        op("vector", lambda e: e.reciprocal(srow[0:1, 1:2], srow[0:1, 2:3]),
           inc=sV, serialize=True)
        s1_ready = sV.n
        wait("tensor", sV, s1_ready)
        op("tensor", lambda e: e.matmul(
            ps[0][:, 0:16], ones1[:, :], srow[0:1, 0:16], start=True,
            stop=True), inc=sT)
        wait("vector", sT, sT.n)
        op("vector", lambda e: e.tensor_copy(scm[:, 1:3], ps[0][:, 1:3]),
           inc=sV, serialize=True)
        s1b_done = sV.n

        milestone("cc2")
        for ft in range(32):
            wait("sync", sV, sV.n)
            wait("sync", sT, sT.n)
            op("sync", lambda e, ft_=ft: e.dma_start(
                out=p3_wbuf[:, 0:1024], in_=w1_in[ft_ * 128:(ft_ + 1) * 128, :]),
                inc=sL, k=16)
            wl = sL.n
            wait("vector", sL, wl)
            op("vector", lambda e, ft_=ft: e.tensor_reduce(
                lnw[:, 0:1], p3_wbuf[:, 0:1024], AX, OP.max,
                apply_absolute_value=True), inc=sV, serialize=True)
            op("vector", lambda e, ft_=ft: e.tensor_scalar(
                out=sw1[:, ft_:ft_ + 1], in0=lnw[:, 0:1], scalar1=EPS_Q,
                scalar2=R127, op0=OP.max, op1=OP.mult), inc=sV, serialize=True)
            op("vector", lambda e, ft_=ft: e.reciprocal(
                lnw[:, 1:2], sw1[:, ft_:ft_ + 1]), inc=sV, serialize=True)
            op("vector", lambda e: e.tensor_scalar(
                out=p3_tq[:, :], in0=p3_wbuf[:, 0:1024], scalar1=lnw[:, 1:2],
                scalar2=MAGIC, op0=OP.mult, op1=OP.add), inc=sV)
            op("vector", lambda e: e.tensor_scalar(
                out=p3_xint[:, 0:1024], in0=p3_tq[:, :], scalar1=MAGIC,
                scalar2=None, op0=OP.subtract), inc=sV)
            wq_done = sV.n
            wait("tensor", sV, wq_done)
            for dc in range(8):
                op("tensor", lambda e, dc_=dc: e.matmul(
                    ps7b[:, dc_ * 128:(dc_ + 1) * 128],
                    p3_xint[:, dc_ * 128:(dc_ + 1) * 128], identb[:, :],
                    is_transpose=True), inc=sT if dc == 7 else None)
            wtr = sT.n
            wait("vector", sT, wtr)
            op("vector", lambda e, ft_=ft: e.tensor_copy(
                w1v[:, :, ft_ * 128:(ft_ + 1) * 128],
                ps7b[:, :].rearrange("p (a c) -> p a c", a=8)), inc=sV)
        w1_ready_v = sV.n
        wait("vector", sV, max(w1_ready_v, s1b_done))
        op("vector", lambda e: e.tensor_scalar(
            out=s1sw[:, :], in0=sw1[:, :], scalar1=scm[:, 2:3], scalar2=None,
            op0=OP.mult), inc=sV, serialize=True)
        op("vector", lambda e: e.reciprocal(ccol[:, :], s1sw[:, :]), inc=sV,
           serialize=True)
        op("vector", lambda e: e.tensor_tensor(
            out=bcol[:, :], in0=b1_s[:, :], in1=ccol[:, :], op=OP.mult),
            inc=sV, serialize=True)
        op("vector", lambda e: e.tensor_scalar(
            out=bcol[:, :], in0=bcol[:, :], scalar1=MAGIC, scalar2=None,
            op0=OP.add), inc=sV, serialize=True)
        op("vector", lambda e: e.tensor_scalar(
            out=b1i[:, :], in0=bcol[:, :], scalar1=MAGIC, scalar2=None,
            op0=OP.subtract), inc=sV, serialize=True)
        b1i_done = sV.n

        milestone("p3w")
        i1_evac_c = [0] * 5
        wait("sync", sS, sS.n)
        wait("sync", sG, sG.n)
        p3_tq_done = 0
        for sl in range(NSUB):
            wait("sync", sT, sT.n)
            wait("sync", sC, sC.n)
            wait("sync", sV, sV.n)
            op("sync", lambda e, s_=sl: e.dma_start(
                out=x_t[:, :, :],
                in_=x1_d[s_ * 4:(s_ + 1) * 4, :, :].rearrange("t p c -> p t c")),
                inc=sL, k=16)
            x1l = sL.n
            for tt in range(4):
                ti = sl * 4 + tt
                wait("scalar", sL, x1l)
                wait("scalar", sV, max(b1i_done, p3_tq_done))
                op("scalar", lambda e, tt_=tt, ti_=ti: e.activation(
                    a_xn[:, :], x_t[:, tt_, :], ACT.Identity,
                    bias=stats[:, ti_, 1:2], scale=stats[:, ti_, 0:1]), inc=sC)
                xn2d = sC.n
                wait("vector", sC, xn2d)
                op("vector", lambda e: e.tensor_scalar(
                    out=p3_tq[:, :], in0=a_xn[:, :], scalar1=scm[:, 1:2],
                    scalar2=MAGIC, op0=OP.mult, op1=OP.add), inc=sV)
                p3_tq_done = sV.n
                op("vector", lambda e: e.tensor_scalar(
                    out=p3_xint[:, 0:1024], in0=p3_tq[:, :], scalar1=MAGIC,
                    scalar2=MAGIC, op0=OP.max, op1=OP.subtract), inc=sV)
                xq_done = sV.n
                wait("tensor", sV, xq_done)
                for dc in range(8):
                    op("tensor", lambda e, dc_=dc: e.matmul(
                        ps7b[:, dc_ * 128:(dc_ + 1) * 128],
                        p3_xint[:, dc_ * 128:(dc_ + 1) * 128], identb[:, :],
                        is_transpose=True), inc=sT if dc == 7 else None)
                xtr = sT.n
                wait("vector", sT, xtr)
                op("vector", lambda e, tt_=tt: e.tensor_copy(
                    p3_xintT[:, :, tt_ * 128:(tt_ + 1) * 128],
                    ps7b[:, :].rearrange("p (a c) -> p a c", a=8)), inc=sV)
            xT_done = sV.n

            wait("tensor", sV, xT_done)
            for fc in range(32):
                b = fc % 4
                if fc >= 4:
                    wait("tensor", sC, i1_evac_c[-4])
                for dc in range(8):
                    op("tensor", lambda e, b_=b, fc_=fc, dc_=dc: e.matmul(
                        ps[b_][:, :],
                        w1v[:, dc_, fc_ * 128:(fc_ + 1) * 128],
                        p3_xintT[:, dc_, :],
                        start=(dc_ == 0), stop=(dc_ == 7)),
                       inc=sT if dc == 7 else None)
                mm_done = sT.n
                wait("scalar", sT, mm_done)
                op("scalar", lambda e, b_=b, fc_=fc: e.copy(
                    p3_i1[:, fc_ % 4, :], ps[b_][:, :]), inc=sC)
                i1_evac_c.append(sC.n)
                wait("vector", sC, sC.n)
                op("vector", lambda e, fc_=fc: e.reduce_max(
                    lnw[:, 3:4], p3_i1[:, fc_ % 4, :], axis=AX), inc=sV,
                    serialize=True)
                op("vector", lambda e, fc_=fc: e.tensor_tensor(
                    out=mf[:, fc_:fc_ + 1], in0=mf[:, fc_:fc_ + 1],
                    in1=lnw[:, 3:4], op=OP.max), inc=sV, serialize=True)
                if fc % 4 == 3:
                    wait("sync", sC, sC.n)
                    op("sync", lambda e, s_=sl, f0=fc - 3: e.dma_start(
                        out=i1_d[s_, f0:f0 + 4, :, :].rearrange(
                            "f p c -> p f c"), in_=p3_i1[:, :, :]), inc=sS,
                        k=16)
                    st_i1 = sS.n
                    wait("scalar", sS, st_i1)
                    wait("scalar", sV, sV.n)

        milestone("p3")
        p3_t_done = sT.n
        p3_c_end = sC.n
        p3_v_end = sV.n
        w2buf = [p3_wbuf, arena[:, 0:4096]]
        w2_ld = [0] * 8
        w2_q_done = [0] * 8

        def p4w_load(ft):
            if ft == 0:
                wait("sync", sV, p3_v_end)
                wait("sync", sT, p3_t_done)
            else:
                wait("sync", sL, w2_ld[ft - 1])
            if ft == 1:
                wait("sync", sC, p3_c_end)
            if ft >= 2:
                wait("sync", sV, w2_q_done[ft - 2])
            op("sync", lambda e, ft_=ft, w_=w2buf[ft % 2]: e.dma_start(
                out=w_, in_=w2_in[ft_ * 128:(ft_ + 1) * 128, :]), inc=sL, k=16)
            w2_ld[ft] = sL.n

        p4w_load(0)
        p4w_load(1)
        wait("vector", sT, p3_t_done)
        for ft in range(8):
            wb = w2buf[ft % 2]
            wait("vector", sL, w2_ld[ft])
            op("vector", lambda e, w_=wb: e.tensor_reduce(
                lnw[:, 0:1], w_, AX, OP.max,
                apply_absolute_value=True), inc=sV, serialize=True)
            op("vector", lambda e, ft_=ft: e.tensor_scalar(
                out=sw2[:, ft_:ft_ + 1], in0=lnw[:, 0:1], scalar1=EPS_Q,
                scalar2=R127, op0=OP.max, op1=OP.mult), inc=sV, serialize=True)
            op("vector", lambda e, ft_=ft: e.reciprocal(
                lnw[:, 1:2], sw2[:, ft_:ft_ + 1]), inc=sV, serialize=True)
            for hh in range(4):
                op("vector", lambda e, hh_=hh, w_=wb: e.tensor_scalar(
                    out=p3_tq[:, :], in0=w_[:, hh_ * 1024:(hh_ + 1) * 1024],
                    scalar1=lnw[:, 1:2], scalar2=MAGIC, op0=OP.mult,
                    op1=OP.add), inc=sV)
                op("vector", lambda e, hh_=hh: e.tensor_scalar(
                    out=p4_w2q[:, hh_ * 1024:(hh_ + 1) * 1024], in0=p3_tq[:, :],
                    scalar1=MAGIC, scalar2=None, op0=OP.subtract), inc=sV)
            wq2 = sV.n
            w2_q_done[ft] = wq2
            if ft + 2 < 8:
                p4w_load(ft + 2)
            wait("tensor", sV, wq2)
            wait("tensor", sT, p3_t_done)
            for g in range(4):
                wait("tensor", sV, sV.n)
                for q in range(8):
                    fc = g * 8 + q
                    op("tensor", lambda e, q_=q, fc_=fc: e.matmul(
                        ps7b[:, q_ * 128:(q_ + 1) * 128],
                        p4_w2q[:, fc_ * 128:(fc_ + 1) * 128], identb[:, :],
                        is_transpose=True), inc=sT if q == 7 else None)
                wt2 = sT.n
                wait("vector", sT, wt2)
                op("vector", lambda e, g_=g, ft_=ft: e.tensor_copy(
                    w2v[:, g_ * 8:(g_ + 1) * 8, ft_ * 128:(ft_ + 1) * 128],
                    ps7b[:, :].rearrange("p (a c) -> p a c", a=8)), inc=sV)
        w2_ready_v = sV.n

        wait("vector", sV, w2_ready_v)
        op("vector", lambda e: e.tensor_tensor(
            out=ccol[:, :], in0=mf[:, :], in1=b1i[:, :], op=OP.add), inc=sV,
            serialize=True)
        op("vector", lambda e: e.tensor_tensor(
            out=ccol[:, :], in0=ccol[:, :], in1=s1sw[:, :], op=OP.mult),
            inc=sV, serialize=True)
        op("vector", lambda e: e.reduce_max(lnw[:, 5:6], ccol[:, :], axis=AX),
           inc=sV, serialize=True)
        hm_col = sV.n
        wait("tensor", sV, hm_col)
        op("tensor", lambda e: e.matmul(
            ps[0][0:1, 0:128], lnw[:, 5:6], identf[:, :], is_transpose=True),
            inc=sT)
        wait("vector", sT, sT.n)
        op("vector", lambda e: e.reduce_max(
            sc1[0:1, 3:4], ps[0][0:1, 0:128], axis=AX), inc=sV,
            serialize=True)
        op("vector", lambda e: e.tensor_scalar(
            out=sc1[0:1, 3:4], in0=sc1[0:1, 3:4], scalar1=0.0, scalar2=None,
            op0=OP.max), inc=sV, serialize=True)
        wait("gpsimd", sV, sV.n)
        op("gpsimd", lambda e: e.dma_start(out=m3_i.ap(), in_=sc1[0:1, 3:4]),
           inc=sG, k=16)
        wait("gpsimd", sG, sG.n)
        op("gpsimd", lambda e: e.collective_compute(
            "AllReduce", OP.max, replica_groups=[list(range(NCORES))],
            ins=[m3_i.ap().opt()], outs=[m3_o.ap().opt()]), inc=sCC)
        wait("gpsimd", sCC, sCC.n)
        op("gpsimd", lambda e: e.dma_start(out=sc1[0:1, 3:4], in_=m3_o.ap()),
           inc=sG, k=16)
        wait("vector", sG, sG.n)
        op("vector", lambda e: e.tensor_scalar(
            out=srow[0:1, 4:5], in0=sc1[0:1, 3:4], scalar1=EPS_Q,
            scalar2=R255, op0=OP.max, op1=OP.mult), inc=sV, serialize=True)
        op("vector", lambda e: e.reciprocal(srow[0:1, 3:4], srow[0:1, 4:5]),
           inc=sV, serialize=True)
        s2_ready = sV.n
        wait("tensor", sV, s2_ready)
        op("tensor", lambda e: e.matmul(
            ps[0][:, 0:16], ones1[:, :], srow[0:1, 0:16], start=True,
            stop=True), inc=sT)
        wait("vector", sT, sT.n)
        op("vector", lambda e: e.tensor_copy(scm[:, 3:5], ps[0][:, 3:5]),
           inc=sV, serialize=True)
        op("vector", lambda e: e.tensor_scalar(
            out=ccol[:, :], in0=s1sw[:, :], scalar1=scm[:, 3:4], scalar2=None,
            op0=OP.mult), inc=sV, serialize=True)
        op("vector", lambda e: e.tensor_tensor(
            out=bcol[:, :], in0=b1i[:, :], in1=ccol[:, :], op=OP.mult),
            inc=sV, serialize=True)
        op("vector", lambda e: e.tensor_scalar(
            out=s2sw[:, :], in0=sw2[:, :], scalar1=scm[:, 4:5], scalar2=None,
            op0=OP.mult), inc=sV, serialize=True)
        op("vector", lambda e: e.reciprocal(b2i[:, :], s2sw[:, :]), inc=sV,
           serialize=True)
        op("vector", lambda e: e.tensor_tensor(
            out=b2i[:, :], in0=b2_s[:, :], in1=b2i[:, :], op=OP.mult), inc=sV,
            serialize=True)
        op("vector", lambda e: e.tensor_scalar(
            out=b2i[:, :], in0=b2i[:, :], scalar1=MAGIC, scalar2=None,
            op0=OP.add), inc=sV, serialize=True)
        op("vector", lambda e: e.tensor_scalar(
            out=b2i[:, :], in0=b2i[:, :], scalar1=MAGIC, scalar2=None,
            op0=OP.subtract), inc=sV, serialize=True)
        op("vector", lambda e: e.tensor_tensor(
            out=bq2v[:, :], in0=b2i[:, :], in1=s2sw[:, :], op=OP.mult),
            inc=sV, serialize=True)
        rows_ready = sV.n
        wait("tensor", sV, rows_ready)
        op("tensor", lambda e: e.matmul(
            ps[0][0:8, 0:128], s2sw[:, :], identf[:, :], is_transpose=True))
        op("tensor", lambda e: e.matmul(
            ps[1][0:8, 0:128], bq2v[:, :], identf[:, :], is_transpose=True),
            inc=sT)
        wait("vector", sT, sT.n)
        op("vector", lambda e: e.tensor_copy(s2row[:, :], ps[0][0:8, 0:128]),
           inc=sV)
        op("vector", lambda e: e.tensor_copy(bq2row[:, :], ps[1][0:8, 0:128]),
           inc=sV)
        wait("gpsimd", sV, sV.n)
        op("gpsimd", lambda e: e.dma_start(
            out=rowb_d.ap()[0], in_=s2row[:, :]), inc=sG, k=16)
        op("gpsimd", lambda e: e.dma_start(
            out=rowb_d.ap()[1], in_=bq2row[:, :]), inc=sG, k=16)
        wait("gpsimd", sG, sG.n)
        op("gpsimd", lambda e: e.dma_start(
            out=s2rowf[0:1, :],
            in_=rowb_d.ap()[0].rearrange("j c -> (j c)").unsqueeze(0)),
            inc=sG, k=16)
        op("gpsimd", lambda e: e.dma_start(
            out=bq2rowf[0:1, :],
            in_=rowb_d.ap()[1].rearrange("j c -> (j c)").unsqueeze(0)),
            inc=sG, k=16)
        wait("tensor", sG, sG.n)
        for i_, (bk, rowf) in enumerate(
                ((2, s2rowf), (3, s2rowf), (4, bq2rowf), (5, bq2rowf))):
            co = (i_ % 2) * 512
            op("tensor", lambda e, b_=bk, co_=co, r_=rowf: e.matmul(
                ps[b_][:, :], ones1[:, :], r_[0:1, co_:co_ + 512],
                start=True, stop=True), inc=sT if i_ == 3 else None)
        wait("vector", sT, sT.n)
        op("vector", lambda e: e.tensor_copy(p4_sb2[:, 0:512], ps[2][:, :]))
        op("vector", lambda e: e.tensor_copy(p4_sb2[:, 512:1024], ps[3][:, :]))
        op("vector", lambda e: e.tensor_copy(p4_bq2[:, 0:512], ps[4][:, :]))
        op("vector", lambda e: e.tensor_copy(p4_bq2[:, 512:1024], ps[5][:, :]),
           inc=sV)
        p4_consts = sV.n

        milestone("p4w")
        wait("sync", sS, sS.n)
        if debug_dump:
            wait("sync", sV, sV.n)
            op("sync", lambda e: e.dma_start(out=dbg_w2[:, :, :], in_=w2v[:, :, :]), inc=sS, k=16)
            op("sync", lambda e: e.dma_start(out=dbg_sb[:, 0:1024], in_=p4_sb2[:, :]), inc=sS, k=16)
            op("sync", lambda e: e.dma_start(out=dbg_sb[:, 1024:2048], in_=p4_bq2[:, :]), inc=sS, k=16)
            op("sync", lambda e: e.dma_start(out=dbg_cb[:, 0:32], in_=ccol[:, :]), inc=sS, k=16)
            op("sync", lambda e: e.dma_start(out=dbg_cb[:, 32:64], in_=bcol[:, :]), inc=sS, k=16)
            op("sync", lambda e: e.dma_start(out=dbg_cb[:, 64:96], in_=s1sw[:, :]), inc=sS, k=16)
            op("sync", lambda e: e.dma_start(out=dbg_cb[:, 96:128], in_=b1i[:, :]), inc=sS, k=16)
            op("sync", lambda e: e.dma_start(out=dbg_cb[:, 128:133], in_=scm[:, 0:5]), inc=sS, k=16)
        NF = 16 * 32
        i1_ld = [0] * NF
        t0_inc = [0] * NF
        t1_inc = [0] * NF
        hint_inc = [0] * NF
        fc_mm = [0] * NF
        ep_mult_v = [0] * 16
        ep_add_v = [0] * 16
        out_st = [0] * 16
        x1_ld4 = [0] * 16

        wait("sync", sT, sT.n)
        wait("sync", sC, sC.n)
        wait("sync", sV, sV.n)

        ep_mult_bank = [[0] * 8 for _ in range(16)]

        def emit_epilogue_mults(sl):
            if sl > 0:
                wait("vector", sG, out_st[sl - 1])
            wait("vector", sT, fc_mm[sl * 32 + 31])
            for tt in range(4):
                for hh in range(2):
                    b = tt * 2 + hh
                    op("vector", lambda e, b_=b, hh_=hh, tt_=tt:
                       e.tensor_tensor(
                           out=p4_out[:, tt_, hh_ * 512:(hh_ + 1) * 512],
                           in0=ps[b_][:, :],
                           in1=p4_sb2[:, hh_ * 512:(hh_ + 1) * 512],
                           op=OP.mult), inc=sV)
                    ep_mult_bank[sl][b] = sV.n
            ep_mult_v[sl] = sV.n

        def emit_epilogue_adds(sl):
            wait("vector", sG, x1_ld4[sl])
            for tt in range(4):
                op("vector", lambda e, tt_=tt: e.tensor_tensor(
                    out=p4_x1[:, tt_, :], in0=p4_x1[:, tt_, :],
                    in1=p4_bq2[:, :], op=OP.add), inc=sV)
            for tt in range(4):
                op("vector", lambda e, tt_=tt: e.tensor_tensor(
                    out=p4_out[:, tt_, :], in0=p4_out[:, tt_, :],
                    in1=p4_x1[:, tt_, :], op=OP.add), inc=sV)
            ep_add_v[sl] = sV.n
            wait("gpsimd", sV, ep_add_v[sl])
            op("gpsimd", lambda e, s_=sl: e.dma_start(
                out=out_d[s_ * 512:(s_ + 1) * 512, :].rearrange(
                    "(t p) c -> p t c", p=128), in_=p4_out[:, :, :]), inc=sG,
                k=16)
            out_st[sl] = sG.n

        for F in range(NF):
            sl, fc = divmod(F, 32)
            if F >= 4:
                wait("sync", sC, t0_inc[F - 4])

            if F >= 1:
                wait("sync", sL, i1_ld[F - 1])
            op("sync", lambda e, s_=sl, f_=fc, r=F % 4: e.dma_start(
                out=p4_i1[:, r, :], in_=i1_d[s_, f_, :, :]), inc=sL, k=16)
            i1_ld[F] = sL.n
            if fc == 9:
                if sl > 0:
                    wait("gpsimd", sV, ep_add_v[sl - 1])
                op("gpsimd", lambda e, s_=sl: e.dma_start(
                    out=p4_x1[:, :, :],
                    in_=x1_d[s_ * 4:(s_ + 1) * 4, :, :].rearrange(
                        "t p c -> p t c")), inc=sG, k=16)
                x1_ld4[sl] = sG.n

            wait("scalar", sL, i1_ld[F])
            wait("scalar", sV, p4_consts)
            if F >= 2:
                wait("scalar", sV, t1_inc[F - 2])
            op("scalar", lambda e, f_=fc, r=F % 4, r0=F % 2: e.activation(
                p4_t0[:, r0, :], p4_i1[:, r, :], ACT.Identity,
                bias=bcol[:, f_:f_ + 1], scale=ccol[:, f_:f_ + 1]), inc=sC)
            t0_inc[F] = sC.n

            wait("vector", sC, t0_inc[F])
            op("vector", lambda e, r0=F % 2: e.tensor_scalar(
                out=p4_t1[:, :], in0=p4_t0[:, r0, :], scalar1=MAGIC,
                scalar2=None, op0=OP.add), inc=sV)
            t1_inc[F] = sV.n
            if F >= 8:
                wait("vector", sT, fc_mm[F - 8])
            op("vector", lambda e, r=F % 8: e.tensor_scalar(
                out=p4_hint[:, r, :], in0=p4_t1[:, :], scalar1=MAGIC,
                scalar2=MAGIC, op0=OP.max, op1=OP.subtract), inc=sV)
            hint_inc[F] = sV.n
            if debug_dump:
                wait("sync", sV, hint_inc[F])
                op("sync", lambda e, F_=F, r=F % 8: e.dma_start(
                    out=dbg_ha[F_], in_=p4_hint[:, r, :]), inc=sS, k=16)
            if fc == 2 and sl > 0:
                emit_epilogue_mults(sl - 1)
            if fc == 8 and sl > 0:
                emit_epilogue_adds(sl - 1)

            def emit_tensor_fc(Ft, slack_v):
                slt, fct = divmod(Ft, 32)
                wait("tensor", sV, slack_v)
                for tt in range(4):
                    for hh in range(2):
                        b = tt * 2 + hh
                        if fct == 0 and slt > 0:
                            wait("tensor", sV, ep_mult_bank[slt - 1][b])
                        op("tensor", lambda e, b_=b, tt_=tt, hh_=hh,
                           f_=fct, r=Ft % 8: e.matmul(
                               ps[b_][:, :],
                               p4_hint[:, r, tt_ * 128:(tt_ + 1) * 128],
                               w2v[:, f_, hh_ * 512:(hh_ + 1) * 512],
                               start=(f_ == 0), stop=(f_ == 31)),
                           inc=sT if (tt == 3 and hh == 1) else None)
                fc_mm[Ft] = sT.n

            if F >= 2:
                emit_tensor_fc(F - 2, hint_inc[F])
        op("scalar", lambda e: e.memzero(lnw[:, 1:2]), inc=sC)
        op("vector", lambda e: e.memzero(p4_t1[:, 0:1]), inc=sV)
        op("vector", lambda e: e.memzero(p4_t1[:, 1:2]), inc=sV)
        emit_tensor_fc(NF - 2, sV.n)
        emit_tensor_fc(NF - 1, sV.n)
        emit_epilogue_mults(15)
        emit_epilogue_adds(15)

        milestone("full")
        wait("sync", sS, sS.n)
        wait("gpsimd", sG, sG.n)
        if stop_stage is not None:
            cut = milestones[stop_stage]
            for k in ENGINES:
                del plan[k][cut[k]:]

        with nc.Block() as block:
            @block.sync
            def _(eng):
                for f in plan["sync"]:
                    f(eng)

            @block.tensor
            def _(eng):
                for f in plan["tensor"]:
                    f(eng)

            @block.vector
            def _(eng):
                for f in plan["vector"]:
                    f(eng)

            @block.scalar
            def _(eng):
                for f in plan["scalar"]:
                    f(eng)

            @block.gpsimd
            def _(eng):
                for f in plan["gpsimd"]:
                    f(eng)

    return nc


def _host_inputs(L0_attn_x, L0_attn_Wq, L0_attn_Wk, L0_attn_Wv, L0_attn_Wo,
                 L0_attn_gate, L0_attn_M0, L0_ffn_w1, L0_ffn_b1, L0_ffn_w2,
                 L0_ffn_b2):
    f32 = np.float32
    x = np.asarray(L0_attn_x, dtype=f32)
    wq = np.asarray(L0_attn_Wq, dtype=f32)
    wk = np.asarray(L0_attn_Wk, dtype=f32)
    wv = np.asarray(L0_attn_Wv, dtype=f32)
    wo = np.asarray(L0_attn_Wo, dtype=f32)
    gate = np.asarray(L0_attn_gate, dtype=f32).reshape(1, 1)
    m0 = np.asarray(L0_attn_M0, dtype=f32)
    w1 = np.ascontiguousarray(np.asarray(L0_ffn_w1, dtype=f32))
    b1 = np.asarray(L0_ffn_b1, dtype=f32)
    w2 = np.ascontiguousarray(np.asarray(L0_ffn_w2, dtype=f32))
    b2 = np.asarray(L0_ffn_b2, dtype=f32)

    wqt = np.ascontiguousarray(wq.T)
    wkt = np.ascontiguousarray(wk.T)
    wvt = np.ascontiguousarray(wv.T)
    wot = np.ascontiguousarray(wo.T)
    m0t = np.ascontiguousarray(m0.T)
    b1r = np.ascontiguousarray(b1.reshape(FF // 128, 128).T)
    b2r = np.ascontiguousarray(b2.reshape(D // 128, 128).T)

    maskv = np.zeros((128, 128), dtype=f32)
    for kr in range(128):
        for qr in range(128):
            if kr // 8 < qr // 8:
                maskv[kr, qr] = 1.0
    identf = np.eye(128, dtype=f32)
    identb = np.eye(128, dtype=f32).astype(ml_dtypes.bfloat16)
    taus = np.arange(SUB) // B
    dqrow = (DECAY ** taus.astype(np.float64)).astype(f32).reshape(1, SUB)
    dkrow = ((DECAY ** (-1.0 - taus.astype(np.float64))) / B).astype(
        f32).reshape(1, SUB)

    in_maps = []
    for c in range(NCORES):
        xs = np.ascontiguousarray(
            x[:, c * L:(c + 1) * L, :].transpose(1, 0, 2).reshape(RPC, D))
        coef = np.zeros((1, 16), dtype=f32)
        for j in range(c):
            coef[0, j] = f32(DECAY ** (L * (c - 1 - j)))
        coef[0, 8] = f32(DECAY ** (L * c))
        in_maps.append(dict(
            xr=xs, wqt=wqt, wkt=wkt, wvt=wvt, wot=wot, m0t=m0t,
            gate_in=gate, w1_in=w1, b1_in=b1r, w2_in=w2, b2_in=b2r,
            maskc=maskv, identf_in=identf, identb_in=identb,
            dqrow_in=dqrow, dkrow_in=dkrow, coef_in=coef))
    return in_maps


def kernel(**inputs):
    in_maps = _host_inputs(**inputs)
    nc = _build_nc()
    res = run_bass_kernel_spmd(nc, in_maps, core_ids=list(range(NCORES)))
    outs = []
    for c in range(NCORES):
        oc = np.asarray(res.results[c]["out"])
        outs.append(oc.reshape(L, B, D).transpose(1, 0, 2))
    return np.ascontiguousarray(np.concatenate(outs, axis=1))


# version 17
# speedup vs baseline: 1.0233x; 1.0171x over previous
import numpy as np
import ml_dtypes
from contextlib import ExitStack

import concourse.bass as bass
import concourse.mybir as mybir
from concourse.bass_utils import run_bass_kernel_spmd

F32 = mybir.dt.float32
BF16 = mybir.dt.bfloat16
OP = mybir.AluOpType
ACT = mybir.ActivationFunctionType
AX = mybir.AxisListType.X

MAGIC = 12582912.0
DECAY = 0.99
NCORES = 8
L = 1024
B = 8
RPC = L * B
D = 1024
M = 256
FF = 4096
C = 64
SUB = C * B
NSUB = L // C
EPS_LN = 1e-5
EPS_Q = 1e-8
R255 = float(np.float32(1.0 / 255.0))
R127 = float(np.float32(1.0 / 127.0))
RD = float(np.float32(1.0 / D))
D64 = float(np.float32(DECAY ** 64))

ENGINES = ("sync", "tensor", "vector", "scalar", "gpsimd")


class _Sem:
    __slots__ = ("h", "n")

    def __init__(self, h):
        self.h = h
        self.n = 0


def _build_nc(debug_dump=False, stop_stage=None):
    nc = bass.Bass("TRN2", target_bir_lowering=False, detect_race_conditions=False)
    _dk = "ExternalOutput" if debug_dump else "Internal"

    xr = nc.dram_tensor("xr", [RPC, D], F32, kind="ExternalInput").ap()
    wqt = nc.dram_tensor("wqt", [D, M], F32, kind="ExternalInput").ap()
    wkt = nc.dram_tensor("wkt", [D, M], F32, kind="ExternalInput").ap()
    wvt = nc.dram_tensor("wvt", [D, M], F32, kind="ExternalInput").ap()
    wot = nc.dram_tensor("wot", [M, D], F32, kind="ExternalInput").ap()
    m0t = nc.dram_tensor("m0t", [M, M], F32, kind="ExternalInput").ap()
    gate_in = nc.dram_tensor("gate_in", [1, 1], F32, kind="ExternalInput").ap()
    w1_in = nc.dram_tensor("w1_in", [FF, D], F32, kind="ExternalInput").ap()
    b1_in = nc.dram_tensor("b1_in", [128, FF // 128], F32, kind="ExternalInput").ap()
    w2_in = nc.dram_tensor("w2_in", [D, FF], F32, kind="ExternalInput").ap()
    b2_in = nc.dram_tensor("b2_in", [128, D // 128], F32, kind="ExternalInput").ap()
    maskc = nc.dram_tensor("maskc", [128, 128], F32, kind="ExternalInput").ap()
    identf_in = nc.dram_tensor("identf_in", [128, 128], F32, kind="ExternalInput").ap()
    identb_in = nc.dram_tensor("identb_in", [128, 128], BF16, kind="ExternalInput").ap()
    dqrow_in = nc.dram_tensor("dqrow_in", [1, SUB], F32, kind="ExternalInput").ap()
    dkrow_in = nc.dram_tensor("dkrow_in", [1, SUB], F32, kind="ExternalInput").ap()
    coef_in = nc.dram_tensor("coef_in", [1, 16], F32, kind="ExternalInput").ap()

    out_d = nc.dram_tensor("out", [RPC, D], F32, kind="ExternalOutput").ap()

    qg_d = nc.dram_tensor("qg_d", [NSUB, 128, 2 * SUB], F32, kind=_dk).ap()
    rp_d = nc.dram_tensor("rp_d", [NSUB, 128, 2 * SUB], F32, kind=_dk).ap()
    x1_d = nc.dram_tensor("x1_d", [64, 128, D], F32, kind=_dk).ap()
    i1_d = nc.dram_tensor("i1_d", [16, 32, 128, 512], F32, kind=_dk).ap()
    g_in = nc.dram_tensor("g_in", [128, 512], F32, kind="Internal")
    g_out = nc.dram_tensor("g_out", [NCORES * 128, 512], F32, kind="Internal")
    m2_i = nc.dram_tensor("m2_i", [1, 1], F32, kind="Internal")
    m2_o = nc.dram_tensor("m2_o", [1, 1], F32, kind="Internal")
    m3_i = nc.dram_tensor("m3_i", [1, 1], F32, kind="Internal")
    m3_o = nc.dram_tensor("m3_o", [1, 1], F32, kind="Internal")
    rowb_d = nc.dram_tensor("rowb_d", [2, 8, 128], F32, kind="Internal")
    if debug_dump:
        dbg_hint = nc.dram_tensor("dbg_hint", [128, 32, 256], BF16,
                                  kind="ExternalOutput").ap()
        dbg_cb = nc.dram_tensor("dbg_cb", [128, 144], F32,
                                kind="ExternalOutput").ap()
        dbg_w2 = nc.dram_tensor("dbg_w2", [128, 32, 1024], BF16,
                                kind="ExternalOutput").ap()
        dbg_sb = nc.dram_tensor("dbg_sb", [128, 2048], F32,
                                kind="ExternalOutput").ap()
        dbg_ln = nc.dram_tensor("dbg_ln", [128, 64], F32,
                                kind="ExternalOutput").ap()
        dbg_ha = nc.dram_tensor("dbg_ha", [512, 128, 512], BF16,
                                kind="ExternalOutput").ap()
        dbg_xn = nc.dram_tensor("dbg_xn", [128, 1024], F32,
                                kind="ExternalOutput").ap()
        dbg_x0 = nc.dram_tensor("dbg_x0", [128, 1024], F32,
                                kind="ExternalOutput").ap()

    stack = ExitStack()
    with stack:
        sb = lambda name, shape, dt=F32: stack.enter_context(
            nc.sbuf_tensor(name, shape, dt))

        ones1 = sb("ones1", [1, 128])
        identf = sb("identf", [128, 128])
        identb = sb("identb", [128, 128], BF16)
        mask = sb("mask", [128, 128])
        wq_s = sb("wq_s", [128, 8, M])
        wk_s = sb("wk_s", [128, 8, M])
        wv_s = sb("wv_s", [128, 8, M])
        wo_s = sb("wo_s", [128, 2, D])
        m0_s = sb("m0_s", [128, 2, M])
        dqr = sb("dqr", [1, SUB])
        dkr = sb("dkr", [1, SUB])
        dq_b = sb("dq_b", [128, SUB])
        dk_b = sb("dk_b", [128, SUB])
        coefr = sb("coefr", [1, 16])
        coefb = sb("coefb", [128, 16])
        scm = sb("scm", [128, 16])
        srow = sb("srow", [1, 16])
        stats = sb("stats", [128, 64, 2])
        mf = sb("mf", [128, 32])
        sw1 = sb("sw1", [128, 32])
        s1sw = sb("s1sw", [128, 32])
        b1i = sb("b1i", [128, 32])
        ccol = sb("ccol", [128, 32])
        bcol = sb("bcol", [128, 32])
        b1_s = sb("b1_s", [128, 32])
        sw2 = sb("sw2", [128, 8])
        b2i = sb("b2i", [128, 8])
        b2_s = sb("b2_s", [128, 8])
        s2sw = sb("s2sw", [128, 8])
        bq2v = sb("bq2v", [128, 8])
        s2row = sb("s2row", [8, 128])
        bq2row = sb("bq2row", [8, 128])
        s2rowf = sb("s2rowf", [1, 1024])
        bq2rowf = sb("bq2rowf", [1, 1024])
        zt = sb("zt", [128, 2, 2, M])
        wc_s = sb("wc_s", [128, 2, M])
        lnw = sb("lnw", [128, 16])
        xmax = sb("xmax", [128, 1])
        sc1 = sb("sc1", [1, 4])
        wint = sb("wint", [128, 32768], BF16)

        AR = 17408
        arena = sb("arena", [128, AR])

        def carve(off, n):
            assert off + n <= AR, (off, n)
            return arena[:, off:off + n], off + n

        o = 0
        a_x, o = carve(o, 4096)
        a_xn, o = carve(o, 1024)
        a_xnt, o = carve(o, 4096)
        a_q, o = carve(o, 1024)
        a_qg, o = carve(o, 1024)
        a_k, o = carve(o, 1024)
        a_kt, o = carve(o, 1024)
        a_v, o = carve(o, 1024)
        a_a, o = carve(o, 2048)
        a_r, o = carve(o, 1024)
        assert o <= AR

        x_t = a_x.rearrange("p (t c) -> p t c", t=4)
        xnt = a_xnt.rearrange("p (dc c) -> p dc c", dc=8)
        qt = a_q.rearrange("p (mc c) -> p mc c", mc=2)
        qgt = a_qg.rearrange("p (mc c) -> p mc c", mc=2)
        kt_ = a_k.rearrange("p (mc c) -> p mc c", mc=2)
        ktok = a_kt.rearrange("p (t c) -> p t c", t=4)
        v_s = a_v.rearrange("p (t c) -> p t c", t=4)
        a_sb = a_a.rearrange("p (k c) -> p k c", k=4)
        rt = a_r.rearrange("p (ic c) -> p ic c", ic=2)

        x1b = a_xnt.rearrange("p (t c) -> p t c", t=4)
        rpb = a_k.rearrange("p (ic c) -> p ic c", ic=2)
        gall = a_x.rearrange("p (j c) -> p j c", j=8)
        gacc = a_xn

        p3_tq = arena[:, 4096 + 1024 + 4096:4096 + 1024 + 4096 + 1024]
        p3_xint = a_k.bitcast(BF16)
        p3_xintT = a_xnt.bitcast(BF16)[:, 0:4096].rearrange(
            "p (dc c) -> p dc c", dc=8)
        p4_w2q = a_xnt.bitcast(BF16)[:, 0:4096]
        p3_i1 = a_a.rearrange("p (g c) -> p g c", g=4)
        p3_wbuf = arena[:, 13312:13312 + 4096]

        p4_hint = arena[:, 0:2048].bitcast(BF16).rearrange(
            "p (r c) -> p r c", r=8)
        p4_i1 = arena[:, 2048:2048 + 2048].rearrange(
            "p (r c) -> p r c", r=4)
        p4_t0 = arena[:, 4096:4096 + 1024].rearrange(
            "p (r c) -> p r c", r=2)
        p4_t1 = arena[:, 5120:5120 + 512]
        p4_x1 = arena[:, 5632:5632 + 4096].rearrange(
            "p (t c) -> p t c", t=4)
        p4_out = arena[:, 9728:9728 + 4096].rearrange(
            "p (t c) -> p t c", t=4)
        p4_u = arena[:, 13824:13824 + 512]
        p4_sb2 = arena[:, 14336:14336 + 1024]
        p4_bq2 = arena[:, 15360:15360 + 1024]

        w1v = wint.rearrange("p (dc f) -> p dc f", dc=8)
        w2v = wint.rearrange("p (fc d) -> p fc d", fc=32)

        ps = [stack.enter_context(nc.psum_tensor(f"ps{i}", [128, 512], F32))
              for i in range(8)]
        ps7b = ps[7].ap().bitcast(BF16)

        sL = _Sem(stack.enter_context(nc.semaphore("sL")))
        sS = _Sem(stack.enter_context(nc.semaphore("sS")))
        sG = _Sem(stack.enter_context(nc.semaphore("sG")))
        sT = _Sem(stack.enter_context(nc.semaphore("sT")))
        sV = _Sem(stack.enter_context(nc.semaphore("sV")))
        sC = _Sem(stack.enter_context(nc.semaphore("sC")))
        sCC = _Sem(stack.enter_context(nc.semaphore("sCC")))

        plan = {k: [] for k in ENGINES}
        lastw = {k: {} for k in ENGINES}
        milestones = {}

        def milestone(name):
            milestones[name] = {k: len(v) for k, v in plan.items()}

        def wait(eng, sem, val=None):
            v = sem.n if val is None else val
            if v <= 0:
                return
            if lastw[eng].get(id(sem), -1) >= v:
                return
            lastw[eng][id(sem)] = v
            plan[eng].append(lambda e, s=sem.h, vv=v: e.wait_ge(s, vv))

        def op(eng, fn, inc=None, k=1, serialize=False):
            if inc is None:
                plan[eng].append(fn)
            else:
                inc.n += k
                plan[eng].append(
                    lambda e, f=fn, s=inc.h, kk=k: f(e).then_inc(s, kk))
                if serialize:
                    v = inc.n
                    plan[eng].append(
                        lambda e, s=inc.h, vv=v: e.wait_ge(s, vv))
                    lastw[eng][id(inc)] = max(lastw[eng].get(id(inc), -1), v)

        for dst, src in [(wq_s[:, :, :], wqt.rearrange("(dc p) m -> p dc m", p=128)),
                         (wk_s[:, :, :], wkt.rearrange("(dc p) m -> p dc m", p=128)),
                         (wv_s[:, :, :], wvt.rearrange("(dc p) m -> p dc m", p=128)),
                         (wo_s[:, :, :], wot.rearrange("(mc p) h -> p mc h", p=128)),
                         (m0_s[:, :, :], m0t.rearrange("(jc p) i -> p jc i", p=128)),
                         (mask[:, :], maskc),
                         (identf[:, :], identf_in),
                         (identb[:, :], identb_in),
                         (dqr[:, :], dqrow_in),
                         (dkr[:, :], dkrow_in),
                         (coefr[:, :], coef_in),
                         (sc1[:, 0:1], gate_in),
                         (b1_s[:, :], b1_in),
                         (b2_s[:, :], b2_in)]:
            op("sync", lambda e, d=dst, s=src: e.dma_start(out=d, in_=s),
               inc=sL, k=16)
        consts_loaded = sL.n

        op("vector", lambda e: e.memset(ones1[:, :], 1.0), inc=sV)
        op("vector", lambda e: e.memset(zt[:, 0, :, :], 0.0), inc=sV)
        op("vector", lambda e: e.memset(mf[:, :], -3.0e38), inc=sV)
        op("vector", lambda e: e.memset(xmax[:, :], -3.0e38), inc=sV)
        op("vector", lambda e: e.memset(srow[:, :], 0.0), inc=sV)
        op("vector", lambda e: e.memset(lnw[:, :], 0.0), inc=sV)
        wait("vector", sL, consts_loaded)
        op("vector", lambda e: e.memset(sc1[0:1, 2:4], 0.0), inc=sV)
        wait("scalar", sL, consts_loaded)
        op("scalar", lambda e: e.activation(sc1[0:1, 1:2], sc1[0:1, 0:1],
                                            ACT.Sigmoid), inc=sC)
        wait("tensor", sL, consts_loaded)
        wait("tensor", sV, sV.n)
        wait("tensor", sC, sC.n)
        op("tensor", lambda e: e.matmul(ps[0][:, :], ones1[:, :], dqr[:, :],
                                        start=True, stop=True))
        op("tensor", lambda e: e.matmul(ps[1][:, :], ones1[:, :], dkr[:, :],
                                        start=True, stop=True))
        op("tensor", lambda e: e.matmul(ps[2][:, 0:16], ones1[:, :], coefr[:, :],
                                        start=True, stop=True))
        op("tensor", lambda e: e.matmul(ps[2][:, 16:17], ones1[:, :],
                                        sc1[0:1, 1:2], start=True, stop=True),
           inc=sT)
        wait("vector", sT, sT.n)
        op("vector", lambda e: e.tensor_copy(dq_b[:, :], ps[0][:, :]), inc=sV)
        op("vector", lambda e: e.tensor_copy(dk_b[:, :], ps[1][:, :]), inc=sV)
        op("vector", lambda e: e.tensor_copy(coefb[:, :], ps[2][:, 0:16]), inc=sV)
        op("vector", lambda e: e.tensor_copy(scm[:, 0:1], ps[2][:, 16:17]),
           inc=sV)
        const_done_v = sV.n
        const_done_t = sT.n

        milestone("consts")
        ln_evac_v = [0] * 5
        z_update_v = const_done_v
        last_tr = 0
        qg_spill_g = 0
        rp_spill_g = 0
        xld = [0] * 64
        xn_done_g = [0] * 64
        rs_done_g = [0] * 64
        xld_prev = 0

        def emit_x_tile_load(g):
            nonlocal_ = None
            if g >= 4:
                wait("sync", sC, xn_done_g[g - 4])
                wait("sync", sV, rs_done_g[g - 4])
            if g >= 1:
                wait("sync", sL, xld[g - 1])
            op("sync", lambda e, g_=g: e.dma_start(
                out=x_t[:, g_ % 4, :],
                in_=xr[g_ * 128:(g_ + 1) * 128, :]), inc=sL, k=16)
            xld[g] = sL.n

        for g in range(4):
            emit_x_tile_load(g)

        for s in range(NSUB):
            par = s % 2
            dsub = float(np.float32(DECAY ** (C * s)))

            for tt in range(4):
                wait("scalar", sL, xld[s * 4 + tt])
                wait("scalar", sT, last_tr)
                op("scalar", lambda e, tt_=tt: e.activation(
                    a_xn[:, :], x_t[:, tt_, :], ACT.Square,
                    accum_out=lnw[:, 4 + tt_:5 + tt_]), inc=sC)
                sq_done = sC.n
                wait("vector", sL, xld[s * 4 + tt])
                op("vector", lambda e, tt_=tt: e.reduce_sum(
                    lnw[:, tt_:tt_ + 1], x_t[:, tt_, :], axis=AX), inc=sV,
                    serialize=True)
                rs_done_g[s * 4 + tt] = sV.n
                wait("vector", sC, sq_done)
                op("vector", lambda e, tt_=tt: e.tensor_scalar(
                    out=lnw[:, 8:9], in0=lnw[:, tt_:tt_ + 1], scalar1=RD,
                    scalar2=None, op0=OP.mult), inc=sV, serialize=True)
                op("vector", lambda e: e.tensor_tensor(
                    out=lnw[:, 9:10], in0=lnw[:, 8:9], in1=lnw[:, 8:9],
                    op=OP.mult), inc=sV, serialize=True)
                op("vector", lambda e, tt_=tt: e.scalar_tensor_tensor(
                    out=lnw[:, 10:11], in0=lnw[:, 4 + tt_:5 + tt_], scalar=RD,
                    in1=lnw[:, 9:10], op0=OP.mult, op1=OP.subtract), inc=sV,
                    serialize=True)
                op("vector", lambda e: e.tensor_scalar(
                    out=lnw[:, 11:12], in0=lnw[:, 10:11], scalar1=EPS_LN,
                    scalar2=None, op0=OP.add), inc=sV, serialize=True)
                op("vector", lambda e: e.reciprocal(
                    lnw[:, 12:13], lnw[:, 11:12]), inc=sV, serialize=True)
                rec_done = sV.n
                wait("scalar", sV, rec_done)
                op("scalar", lambda e: e.activation(
                    lnw[:, 13:14], lnw[:, 12:13], ACT.Sqrt), inc=sC)
                rstd_done = sC.n
                wait("vector", sC, rstd_done)
                op("vector", lambda e: e.scalar_tensor_tensor(
                    out=lnw[:, 14:15], in0=lnw[:, 8:9], scalar=-1.0,
                    in1=lnw[:, 13:14], op0=OP.mult, op1=OP.mult), inc=sV,
                    serialize=True)
                nmr_done = sV.n
                wait("scalar", sV, nmr_done)
                op("scalar", lambda e, tt_=tt: e.activation(
                    a_xn[:, :], x_t[:, tt_, :], ACT.Identity,
                    bias=lnw[:, 14:15], scale=lnw[:, 13:14]), inc=sC)
                xn_done = sC.n
                xn_done_g[s * 4 + tt] = xn_done
                if debug_dump and s == 0:
                    wait("sync", sV, nmr_done)
                    op("sync", lambda e, tt_=tt: e.dma_start(
                        out=dbg_ln[:, tt_ * 16:(tt_ + 1) * 16],
                        in_=lnw[:, 0:16]), inc=sS, k=16)
                    if tt == 0:
                        wait("sync", sC, xn_done)
                        op("sync", lambda e: e.dma_start(
                            out=dbg_xn[:, :], in_=a_xn[:, :]), inc=sS, k=16)
                        op("sync", lambda e: e.dma_start(
                            out=dbg_x0[:, :], in_=x_t[:, 0, :]), inc=sS, k=16)
                        wait("scalar", sS, sS.n)
                        wait("vector", sS, sS.n)
                wait("tensor", sC, xn_done)
                wait("tensor", sV, ln_evac_v[-1] if s + tt else const_done_v)
                for dc in range(8):
                    bank, sl_ = (0, dc) if dc < 4 else (1, dc - 4)
                    last = dc == 7
                    op("tensor", lambda e, b=bank, q=sl_, dc_=dc: e.matmul(
                        ps[b][:, q * 128:(q + 1) * 128],
                        a_xn[:, dc_ * 128:(dc_ + 1) * 128], identf[:, :],
                        is_transpose=True), inc=sT if last else None)
                tr_done = sT.n
                last_tr = tr_done
                wait("vector", sT, tr_done)
                op("vector", lambda e, tt_=tt: e.tensor_copy(
                    xnt[:, 0:4, tt_ * 128:(tt_ + 1) * 128],
                    ps[0][:, :].rearrange("p (a c) -> p a c", a=4)), inc=sV)
                op("vector", lambda e, tt_=tt: e.tensor_copy(
                    xnt[:, 4:8, tt_ * 128:(tt_ + 1) * 128],
                    ps[1][:, :].rearrange("p (a c) -> p a c", a=4)), inc=sV)
                ln_evac_v.append(sV.n)

            xnt_done = sV.n
            for tt2 in range(4):
                g2 = (s + 1) * 4 + tt2
                if g2 < 64:
                    emit_x_tile_load(g2)

            wait("tensor", sV, xnt_done)
            for (wsrc, bank0) in ((wq_s, 2), (wk_s, 4)):
                for mc in range(2):
                    for dc in range(8):
                        op("tensor", lambda e, w=wsrc, b=bank0 + mc, mc_=mc,
                           dc_=dc: e.matmul(
                               ps[b][:, :],
                               w[:, dc_, mc_ * 128:(mc_ + 1) * 128],
                               xnt[:, dc_, :],
                               start=(dc_ == 0), stop=(dc_ == 7)),
                           inc=sT if dc == 7 else None)
            qk_done = sT.n
            wait("vector", sT, qk_done)
            wait("vector", sG, qg_spill_g)
            for mc in range(2):
                op("vector", lambda e, mc_=mc: e.tensor_tensor(
                    out=qt[:, mc_, :], in0=ps[2 + mc_][:, :], in1=dq_b[:, :],
                    op=OP.mult), inc=sV)
                op("vector", lambda e, mc_=mc, d_=dsub: e.tensor_scalar(
                    out=qgt[:, mc_, :], in0=qt[:, mc_, :], scalar1=d_,
                    scalar2=None, op0=OP.mult), inc=sV)
                op("vector", lambda e, mc_=mc: e.tensor_tensor(
                    out=kt_[:, mc_, :], in0=ps[4 + mc_][:, :], in1=dk_b[:, :],
                    op=OP.mult), inc=sV)
            qk_evac = sV.n
            wait("gpsimd", sV, qk_evac)
            op("gpsimd", lambda e, s_=s: e.dma_start(
                out=qg_d[s_, :, :], in_=a_qg), inc=sG, k=16)
            qg_spill_g = sG.n

            wait("tensor", sV, qk_evac)
            for tt in range(4):
                bank = 2 + tt // 2
                cofs = (tt % 2) * 256
                for dc in range(8):
                    op("tensor", lambda e, b=bank, co=cofs, tt_=tt, dc_=dc:
                       e.matmul(ps[b][:, co:co + 256],
                                xnt[:, dc_, tt_ * 128:(tt_ + 1) * 128],
                                wv_s[:, dc_, :],
                                start=(dc_ == 0), stop=(dc_ == 7)),
                       inc=sT if dc == 7 else None)
            v_done = sT.n
            wait("vector", sT, v_done)
            op("vector", lambda e: e.tensor_copy(
                v_s[:, 0:2, :], ps[2][:, :].rearrange("p (a c) -> p a c", a=2)),
                inc=sV)
            op("vector", lambda e: e.tensor_copy(
                v_s[:, 2:4, :], ps[3][:, :].rearrange("p (a c) -> p a c", a=2)),
                inc=sV)
            v_evac = sV.n

            wait("tensor", sV, v_evac)
            for tt in range(4):
                for mc in range(2):
                    idx = tt * 2 + mc
                    bank, sl_ = (0, idx) if idx < 4 else (1, idx - 4)
                    op("tensor", lambda e, b=bank, q=sl_, tt_=tt, mc_=mc:
                       e.matmul(ps[b][:, q * 128:(q + 1) * 128],
                                kt_[:, mc_, tt_ * 128:(tt_ + 1) * 128],
                                identf[:, :], is_transpose=True),
                       inc=sT if idx == 7 else None)
            ktok_tr = sT.n
            wait("vector", sT, ktok_tr)
            op("vector", lambda e: e.tensor_copy(
                ktok[:, 0:2, :], ps[0][:, :].rearrange("p (a c) -> p a c", a=2)),
                inc=sV)
            op("vector", lambda e: e.tensor_copy(
                ktok[:, 2:4, :], ps[1][:, :].rearrange("p (a c) -> p a c", a=2)),
                inc=sV)
            ktok_evac = sV.n

            a_banks = {0: 4, 1: 5, 2: 2, 3: 3}
            wait("tensor", sV, ktok_evac)
            for kt2 in range(4):
                b = a_banks[kt2]
                for mc in range(2):
                    op("tensor", lambda e, b_=b, kt3=kt2, mc_=mc: e.matmul(
                        ps[b_][:, kt3 * 128:512],
                        kt_[:, mc_, kt3 * 128:(kt3 + 1) * 128],
                        qt[:, mc_, kt3 * 128:512],
                        start=(mc_ == 0), stop=(mc_ == 1)),
                       inc=sT if (kt2 == 3 and mc == 1) else None)
            at_done = sT.n
            wait("vector", sT, at_done)
            for kt2 in range(4):
                b = a_banks[kt2]
                op("vector", lambda e, b_=b, kt3=kt2: e.tensor_tensor(
                    out=a_sb[:, kt3, kt3 * 128:(kt3 + 1) * 128],
                    in0=ps[b_][:, kt3 * 128:(kt3 + 1) * 128],
                    in1=mask[:, :], op=OP.mult), inc=sV)
                if kt2 < 3:
                    op("vector", lambda e, b_=b, kt3=kt2: e.tensor_copy(
                        a_sb[:, kt3, (kt3 + 1) * 128:512],
                        ps[b_][:, (kt3 + 1) * 128:512]), inc=sV)
            a_evac = sV.n

            wait("tensor", sV, a_evac)
            wait("tensor", sV, z_update_v)
            for ic in range(2):
                b = 6 + ic
                for jc in range(2):
                    op("tensor", lambda e, b_=b, ic_=ic, jc_=jc, p_=par:
                       e.matmul(ps[b_][:, :],
                                zt[:, p_, jc_, ic_ * 128:(ic_ + 1) * 128],
                                qt[:, jc_, :],
                                start=(jc_ == 0), stop=False))
                for kt2 in range(4):
                    op("tensor", lambda e, b_=b, ic_=ic, kt3=kt2: e.matmul(
                        ps[b_][:, kt3 * 128:512],
                        v_s[:, kt3, ic_ * 128:(ic_ + 1) * 128],
                        a_sb[:, kt3, kt3 * 128:512],
                        start=False, stop=(kt3 == 3)),
                       inc=sT if (ic == 1 and kt2 == 3) else None)
            r_done = sT.n
            wait("vector", sT, r_done)
            wait("vector", sG, rp_spill_g)
            for ic in range(2):
                op("vector", lambda e, ic_=ic: e.tensor_copy(
                    rt[:, ic_, :], ps[6 + ic_][:, :]), inc=sV)
            r_evac = sV.n
            wait("gpsimd", sV, r_evac)
            op("gpsimd", lambda e, s_=s: e.dma_start(
                out=rp_d[s_, :, :], in_=a_r), inc=sG, k=16)
            rp_spill_g = sG.n

            wait("tensor", sV, a_evac)
            for jc in range(2):
                for ut in range(4):
                    op("tensor", lambda e, jc_=jc, ut_=ut: e.matmul(
                        ps[4][:, jc_ * 256:(jc_ + 1) * 256],
                        ktok[:, ut_, jc_ * 128:(jc_ + 1) * 128],
                        v_s[:, ut_, :],
                        start=(ut_ == 0), stop=(ut_ == 3)),
                       inc=sT if (jc == 1 and ut == 3) else None)
            p_done = sT.n
            wait("vector", sT, p_done)
            op("vector", lambda e, p_=par: e.tensor_tensor(
                out=zt[:, 1 - p_, :, :].rearrange("p a c -> p (a c)"),
                in0=zt[:, p_, :, :].rearrange("p a c -> p (a c)"),
                in1=ps[4][:, :], op=OP.add), inc=sV)
            op("vector", lambda e, p_=par: e.tensor_scalar(
                out=zt[:, 1 - p_, :, :].rearrange("p a c -> p (a c)"),
                in0=zt[:, 1 - p_, :, :].rearrange("p a c -> p (a c)"),
                scalar1=D64, scalar2=None, op0=OP.mult), inc=sV)
            z_update_v = sV.n

        milestone("p1a")
        g_parity = NSUB % 2

        wait("gpsimd", sV, z_update_v)
        op("gpsimd", lambda e: e.dma_start(
            out=g_in.ap(), in_=zt[:, g_parity, :, :].rearrange(
                "p a c -> p (a c)")), inc=sG, k=16)
        wait("gpsimd", sG, sG.n)
        op("gpsimd", lambda e: e.collective_compute(
            "AllGather", OP.bypass, replica_groups=[list(range(NCORES))],
            ins=[g_in.ap().opt()], outs=[g_out.ap().opt()]), inc=sCC)
        wait("gpsimd", sCC, sCC.n)
        gall2 = a_xnt.rearrange("p (j c) -> p j c", j=8)
        op("gpsimd", lambda e: e.dma_start(
            out=gall2[:, :, :],
            in_=g_out.ap().rearrange("(j p) c -> p j c", p=128)), inc=sG, k=16)
        g_loaded = sG.n
        wait("vector", sG, g_loaded)
        op("vector", lambda e: e.tensor_scalar(
            out=gacc[:, 0:512], in0=gall2[:, 0, :], scalar1=coefb[:, 0:1],
            scalar2=None, op0=OP.mult), inc=sV)
        for j in range(1, NCORES):
            op("vector", lambda e, j_=j: e.scalar_tensor_tensor(
                out=gacc[:, 0:512], in0=gall2[:, j_, :],
                scalar=coefb[:, j_:j_ + 1], in1=gacc[:, 0:512],
                op0=OP.mult, op1=OP.add), inc=sV)
        op("vector", lambda e: e.scalar_tensor_tensor(
            out=wc_s[:, :, :].rearrange("p a c -> p (a c)"),
            in0=m0_s[:, :, :].rearrange("p a c -> p (a c)"),
            scalar=coefb[:, 8:9], in1=gacc[:, 0:512],
            op0=OP.mult, op1=OP.add), inc=sV)
        wc_done = sV.n

        milestone("cc1")
        qg_bufs = [qgt, a_q.rearrange("p (mc c) -> p mc c", mc=2)]
        rp_bufs = [rpb, a_kt.rearrange("p (mc c) -> p mc c", mc=2)]
        qg_raw = [a_qg, a_q]
        rp_raw = [a_k, a_kt]
        qg_ld_s = [0] * NSUB
        x_ld_s = [0] * NSUB
        rcorr_done_s = [0] * NSUB
        rf_done_s = [0] * NSUB
        x1stt_done_s = [0] * NSUB
        x1store_s = [0] * NSUB
        spills_done = max(qg_spill_g, rp_spill_g)

        def emit_p1b_loads(s):
            pb = s % 2
            if s == 0:
                wait("sync", sG, spills_done)
                wait("sync", sT, sT.n)
                wait("sync", sV, sV.n)
            else:
                wait("sync", sL, sL.n)
                if s >= 2:
                    wait("sync", sT, rcorr_done_s[s - 2])
                    wait("sync", sV, rf_done_s[s - 2])
            op("sync", lambda e, s_=s, p_=pb: e.dma_start(
                out=qg_raw[p_], in_=qg_d[s_, :, :]), inc=sL, k=16)
            op("sync", lambda e, s_=s, p_=pb: e.dma_start(
                out=rp_raw[p_], in_=rp_d[s_, :, :]), inc=sL, k=16)
            qg_ld_s[s] = sL.n

        def emit_p1b_xload(s):
            wait("sync", sL, sL.n)
            if s == 0:
                wait("sync", sC, xn_done_g[63])
                wait("sync", sV, rs_done_g[63])
            else:
                wait("sync", sV, x1stt_done_s[s - 1])
            op("sync", lambda e, s_=s: e.dma_start(
                out=x_t[:, :, :],
                in_=xr[s_ * SUB:(s_ + 1) * SUB, :].rearrange(
                    "(t p) c -> p t c", p=128)), inc=sL, k=16)
            x_ld_s[s] = sL.n

        emit_p1b_loads(0)
        emit_p1b_xload(0)
        for s in range(NSUB):
            pb = s % 2
            qgv = qg_bufs[pb]
            rpv = rp_bufs[pb]

            wait("tensor", sL, qg_ld_s[s])
            wait("tensor", sV, wc_done)
            if s >= 1:
                wait("tensor", sV, rf_done_s[s - 1])
            for ic in range(2):
                for jc in range(2):
                    op("tensor", lambda e, ic_=ic, jc_=jc, q_=qgv: e.matmul(
                        ps[6 + ic_][:, :],
                        wc_s[:, jc_, ic_ * 128:(ic_ + 1) * 128],
                        q_[:, jc_, :],
                        start=(jc_ == 0), stop=(jc_ == 1)),
                       inc=sT if (ic == 1 and jc == 1) else None)
            rc_done = sT.n
            rcorr_done_s[s] = rc_done
            wait("vector", sT, rc_done)
            for ic in range(2):
                op("vector", lambda e, ic_=ic, r_=rpv: e.tensor_tensor(
                    out=rt[:, ic_, :], in0=ps[6 + ic_][:, :],
                    in1=r_[:, ic_, :], op=OP.add), inc=sV)
            rf_done = sV.n
            rf_done_s[s] = rf_done

            if s + 1 < NSUB:
                emit_p1b_loads(s + 1)

            wait("tensor", sV, rf_done)
            x1_done_hist = []
            first_stt = True
            for tt in range(4):
                b0 = 2 + (tt % 2)
                b1 = 4 + (tt % 2)
                if tt >= 2:
                    wait("tensor", sV, x1_done_hist[tt - 2])
                for half, bb in ((0, b0), (1, b1)):
                    for ic in range(2):
                        op("tensor", lambda e, b_=bb, ic_=ic, tt_=tt, h_=half:
                           e.matmul(ps[b_][:, :],
                                    rt[:, ic_, tt_ * 128:(tt_ + 1) * 128],
                                    wo_s[:, ic_, h_ * 512:(h_ + 1) * 512],
                                    start=(ic_ == 0), stop=(ic_ == 1)),
                           inc=sT if ic == 1 else None)
                wo_done = sT.n
                wait("vector", sT, wo_done)
                wait("vector", sL, x_ld_s[s])
                if first_stt and s >= 1:
                    wait("vector", sG, x1store_s[s - 1])
                    first_stt = False
                for half, bb in ((0, b0), (1, b1)):
                    op("vector", lambda e, b_=bb, tt_=tt, h_=half:
                       e.scalar_tensor_tensor(
                           out=x1b[:, tt_, h_ * 512:(h_ + 1) * 512],
                           in0=ps[b_][:, :], scalar=scm[:, 0:1],
                           in1=x_t[:, tt_, h_ * 512:(h_ + 1) * 512],
                           op0=OP.mult, op1=OP.add), inc=sV)
                x1_done = sV.n
                x1_done_hist.append(x1_done)

                wait("scalar", sV, x1_done)
                op("scalar", lambda e, tt_=tt: e.activation(
                    a_xn[:, :], x1b[:, tt_, :], ACT.Square,
                    accum_out=lnw[:, 4:5]), inc=sC)
                sq2 = sC.n
                op("vector", lambda e, tt_=tt: e.reduce_sum(
                    lnw[:, 0:1], x1b[:, tt_, :], axis=AX), inc=sV,
                    serialize=True)
                wait("vector", sC, sq2)
                op("vector", lambda e: e.tensor_scalar(
                    out=lnw[:, 8:9], in0=lnw[:, 0:1], scalar1=RD,
                    scalar2=None, op0=OP.mult), inc=sV, serialize=True)
                op("vector", lambda e: e.tensor_tensor(
                    out=lnw[:, 9:10], in0=lnw[:, 8:9], in1=lnw[:, 8:9],
                    op=OP.mult), inc=sV, serialize=True)
                op("vector", lambda e: e.scalar_tensor_tensor(
                    out=lnw[:, 10:11], in0=lnw[:, 4:5], scalar=RD,
                    in1=lnw[:, 9:10], op0=OP.mult, op1=OP.subtract), inc=sV,
                    serialize=True)
                op("vector", lambda e: e.tensor_scalar(
                    out=lnw[:, 11:12], in0=lnw[:, 10:11], scalar1=EPS_LN,
                    scalar2=None, op0=OP.add), inc=sV, serialize=True)
                op("vector", lambda e: e.reciprocal(
                    lnw[:, 12:13], lnw[:, 11:12]), inc=sV, serialize=True)
                rec2 = sV.n
                wait("scalar", sV, rec2)
                tile_idx = s * 4 + tt
                op("scalar", lambda e, ti=tile_idx: e.activation(
                    stats[:, ti, 0:1], lnw[:, 12:13], ACT.Sqrt), inc=sC)
                rstd2 = sC.n
                wait("vector", sC, rstd2)
                op("vector", lambda e, ti=tile_idx: e.scalar_tensor_tensor(
                    out=stats[:, ti, 1:2], in0=lnw[:, 8:9], scalar=-1.0,
                    in1=stats[:, ti, 0:1], op0=OP.mult, op1=OP.mult), inc=sV,
                    serialize=True)
                op("vector", lambda e, tt_=tt: e.reduce_max(
                    lnw[:, 14:15], x1b[:, tt_, :], axis=AX), inc=sV,
                    serialize=True)
                op("vector", lambda e, ti=tile_idx: e.tensor_scalar(
                    out=lnw[:, 15:16], in0=lnw[:, 14:15],
                    scalar1=lnw[:, 8:9], scalar2=stats[:, ti, 0:1],
                    op0=OP.subtract, op1=OP.mult), inc=sV, serialize=True)
                op("vector", lambda e: e.tensor_tensor(
                    out=xmax[:, :], in0=xmax[:, :], in1=lnw[:, 15:16],
                    op=OP.max), inc=sV, serialize=True)
            slab_v = sV.n
            x1stt_done_s[s] = x1_done_hist[-1]
            wait("gpsimd", sV, slab_v)
            op("gpsimd", lambda e, s_=s: e.dma_start(
                out=x1_d[s_ * 4:(s_ + 1) * 4, :, :].rearrange(
                    "t p c -> p t c"), in_=x1b[:, :, :]), inc=sG, k=16)
            x1store_s[s] = sG.n
            if s + 1 < NSUB:
                emit_p1b_xload(s + 1)

        milestone("p1b")
        wait("tensor", sV, sV.n)
        op("tensor", lambda e: e.matmul(
            ps[0][0:1, 0:128], xmax[:, :], identf[:, :], is_transpose=True),
            inc=sT)
        wait("vector", sT, sT.n)
        op("vector", lambda e: e.reduce_max(
            sc1[0:1, 2:3], ps[0][0:1, 0:128], axis=AX), inc=sV,
            serialize=True)
        wait("gpsimd", sV, sV.n)
        op("gpsimd", lambda e: e.dma_start(out=m2_i.ap(), in_=sc1[0:1, 2:3]),
           inc=sG, k=16)
        wait("gpsimd", sG, sG.n)
        op("gpsimd", lambda e: e.collective_compute(
            "AllReduce", OP.max, replica_groups=[list(range(NCORES))],
            ins=[m2_i.ap().opt()], outs=[m2_o.ap().opt()]), inc=sCC)
        wait("gpsimd", sCC, sCC.n)
        op("gpsimd", lambda e: e.dma_start(out=sc1[0:1, 2:3], in_=m2_o.ap()),
           inc=sG, k=16)
        wait("vector", sG, sG.n)
        op("vector", lambda e: e.tensor_scalar(
            out=srow[0:1, 2:3], in0=sc1[0:1, 2:3], scalar1=EPS_Q,
            scalar2=R255, op0=OP.max, op1=OP.mult), inc=sV, serialize=True)
        op("vector", lambda e: e.reciprocal(srow[0:1, 1:2], srow[0:1, 2:3]),
           inc=sV, serialize=True)
        s1_ready = sV.n
        wait("tensor", sV, s1_ready)
        op("tensor", lambda e: e.matmul(
            ps[0][:, 0:16], ones1[:, :], srow[0:1, 0:16], start=True,
            stop=True), inc=sT)
        wait("vector", sT, sT.n)
        op("vector", lambda e: e.tensor_copy(scm[:, 1:3], ps[0][:, 1:3]),
           inc=sV, serialize=True)
        s1b_done = sV.n

        milestone("cc2")
        for ft in range(32):
            wait("sync", sV, sV.n)
            wait("sync", sT, sT.n)
            op("sync", lambda e, ft_=ft: e.dma_start(
                out=p3_wbuf[:, 0:1024], in_=w1_in[ft_ * 128:(ft_ + 1) * 128, :]),
                inc=sL, k=16)
            wl = sL.n
            wait("vector", sL, wl)
            op("vector", lambda e, ft_=ft: e.tensor_reduce(
                lnw[:, 0:1], p3_wbuf[:, 0:1024], AX, OP.max,
                apply_absolute_value=True), inc=sV, serialize=True)
            op("vector", lambda e, ft_=ft: e.tensor_scalar(
                out=sw1[:, ft_:ft_ + 1], in0=lnw[:, 0:1], scalar1=EPS_Q,
                scalar2=R127, op0=OP.max, op1=OP.mult), inc=sV, serialize=True)
            op("vector", lambda e, ft_=ft: e.reciprocal(
                lnw[:, 1:2], sw1[:, ft_:ft_ + 1]), inc=sV, serialize=True)
            op("vector", lambda e: e.tensor_scalar(
                out=p3_tq[:, :], in0=p3_wbuf[:, 0:1024], scalar1=lnw[:, 1:2],
                scalar2=MAGIC, op0=OP.mult, op1=OP.add), inc=sV)
            op("vector", lambda e: e.tensor_scalar(
                out=p3_xint[:, 0:1024], in0=p3_tq[:, :], scalar1=MAGIC,
                scalar2=None, op0=OP.subtract), inc=sV)
            wq_done = sV.n
            wait("tensor", sV, wq_done)
            for dc in range(8):
                op("tensor", lambda e, dc_=dc: e.matmul(
                    ps7b[:, dc_ * 128:(dc_ + 1) * 128],
                    p3_xint[:, dc_ * 128:(dc_ + 1) * 128], identb[:, :],
                    is_transpose=True), inc=sT if dc == 7 else None)
            wtr = sT.n
            wait("vector", sT, wtr)
            op("vector", lambda e, ft_=ft: e.tensor_copy(
                w1v[:, :, ft_ * 128:(ft_ + 1) * 128],
                ps7b[:, :].rearrange("p (a c) -> p a c", a=8)), inc=sV)
        w1_ready_v = sV.n
        wait("vector", sV, max(w1_ready_v, s1b_done))
        op("vector", lambda e: e.tensor_scalar(
            out=s1sw[:, :], in0=sw1[:, :], scalar1=scm[:, 2:3], scalar2=None,
            op0=OP.mult), inc=sV, serialize=True)
        op("vector", lambda e: e.reciprocal(ccol[:, :], s1sw[:, :]), inc=sV,
           serialize=True)
        op("vector", lambda e: e.tensor_tensor(
            out=bcol[:, :], in0=b1_s[:, :], in1=ccol[:, :], op=OP.mult),
            inc=sV, serialize=True)
        op("vector", lambda e: e.tensor_scalar(
            out=bcol[:, :], in0=bcol[:, :], scalar1=MAGIC, scalar2=None,
            op0=OP.add), inc=sV, serialize=True)
        op("vector", lambda e: e.tensor_scalar(
            out=b1i[:, :], in0=bcol[:, :], scalar1=MAGIC, scalar2=None,
            op0=OP.subtract), inc=sV, serialize=True)
        b1i_done = sV.n

        milestone("p3w")
        i1_evac_c = [0] * 5
        wait("sync", sS, sS.n)
        wait("sync", sG, sG.n)
        p3_tq_done = 0
        x1l_s = [0] * NSUB
        xn2_tile = [[0] * 4 for _ in range(NSUB)]

        def p3_xload(s):
            if s == 0:
                wait("gpsimd", sV, b1i_done)
            for tt in range(4):
                wait("gpsimd", sG, sG.n)
                if s > 0:
                    wait("gpsimd", sC, xn2_tile[s - 1][tt])
                op("gpsimd", lambda e, s_=s, tt_=tt: e.dma_start(
                    out=x_t[:, tt_, :], in_=x1_d[s_ * 4 + tt_, :, :]),
                    inc=sG, k=16)
            x1l_s[s] = sG.n

        p3_xload(0)
        for sl in range(NSUB):
            for tt in range(4):
                ti = sl * 4 + tt
                wait("scalar", sG, x1l_s[sl])
                wait("scalar", sV, max(b1i_done, p3_tq_done))
                op("scalar", lambda e, tt_=tt, ti_=ti: e.activation(
                    a_xn[:, :], x_t[:, tt_, :], ACT.Identity,
                    bias=stats[:, ti_, 1:2], scale=stats[:, ti_, 0:1]), inc=sC)
                xn2d = sC.n
                xn2_tile[sl][tt] = xn2d
                wait("vector", sC, xn2d)
                op("vector", lambda e: e.tensor_scalar(
                    out=p3_tq[:, :], in0=a_xn[:, :], scalar1=scm[:, 1:2],
                    scalar2=MAGIC, op0=OP.mult, op1=OP.add), inc=sV)
                p3_tq_done = sV.n
                op("vector", lambda e: e.tensor_scalar(
                    out=p3_xint[:, 0:1024], in0=p3_tq[:, :], scalar1=MAGIC,
                    scalar2=MAGIC, op0=OP.max, op1=OP.subtract), inc=sV)
                xq_done = sV.n
                wait("tensor", sV, xq_done)
                for dc in range(8):
                    op("tensor", lambda e, dc_=dc: e.matmul(
                        ps7b[:, dc_ * 128:(dc_ + 1) * 128],
                        p3_xint[:, dc_ * 128:(dc_ + 1) * 128], identb[:, :],
                        is_transpose=True), inc=sT if dc == 7 else None)
                xtr = sT.n
                wait("vector", sT, xtr)
                op("vector", lambda e, tt_=tt: e.tensor_copy(
                    p3_xintT[:, :, tt_ * 128:(tt_ + 1) * 128],
                    ps7b[:, :].rearrange("p (a c) -> p a c", a=8)), inc=sV)
            xT_done = sV.n
            if sl + 1 < NSUB:
                p3_xload(sl + 1)

            wait("tensor", sV, xT_done)
            for fc in range(32):
                b = fc % 4
                if fc >= 4:
                    wait("tensor", sC, i1_evac_c[-4])
                for dc in range(8):
                    op("tensor", lambda e, b_=b, fc_=fc, dc_=dc: e.matmul(
                        ps[b_][:, :],
                        w1v[:, dc_, fc_ * 128:(fc_ + 1) * 128],
                        p3_xintT[:, dc_, :],
                        start=(dc_ == 0), stop=(dc_ == 7)),
                       inc=sT if dc == 7 else None)
                mm_done = sT.n
                wait("scalar", sT, mm_done)
                op("scalar", lambda e, b_=b, fc_=fc: e.copy(
                    p3_i1[:, fc_ % 4, :], ps[b_][:, :]), inc=sC)
                i1_evac_c.append(sC.n)
                wait("vector", sC, sC.n)
                op("vector", lambda e, fc_=fc: e.reduce_max(
                    lnw[:, 3:4], p3_i1[:, fc_ % 4, :], axis=AX), inc=sV,
                    serialize=True)
                op("vector", lambda e, fc_=fc: e.tensor_tensor(
                    out=mf[:, fc_:fc_ + 1], in0=mf[:, fc_:fc_ + 1],
                    in1=lnw[:, 3:4], op=OP.max), inc=sV, serialize=True)
                if fc % 4 == 3:
                    wait("sync", sC, sC.n)
                    op("sync", lambda e, s_=sl, f0=fc - 3: e.dma_start(
                        out=i1_d[s_, f0:f0 + 4, :, :].rearrange(
                            "f p c -> p f c"), in_=p3_i1[:, :, :]), inc=sS,
                        k=16)
                    st_i1 = sS.n
                    wait("scalar", sS, st_i1)
                    wait("scalar", sV, sV.n)

        milestone("p3")
        p3_t_done = sT.n
        p3_c_end = sC.n
        p3_v_end = sV.n
        w2buf = [p3_wbuf, arena[:, 0:4096]]
        w2_ld = [0] * 8
        w2_q_done = [0] * 8

        def p4w_load(ft):
            if ft == 0:
                wait("sync", sV, p3_v_end)
                wait("sync", sT, p3_t_done)
            else:
                wait("sync", sL, w2_ld[ft - 1])
            if ft == 1:
                wait("sync", sC, p3_c_end)
            if ft >= 2:
                wait("sync", sV, w2_q_done[ft - 2])
            op("sync", lambda e, ft_=ft, w_=w2buf[ft % 2]: e.dma_start(
                out=w_, in_=w2_in[ft_ * 128:(ft_ + 1) * 128, :]), inc=sL, k=16)
            w2_ld[ft] = sL.n

        p4w_load(0)
        p4w_load(1)
        wait("vector", sT, p3_t_done)
        for ft in range(8):
            wb = w2buf[ft % 2]
            wait("vector", sL, w2_ld[ft])
            op("vector", lambda e, w_=wb: e.tensor_reduce(
                lnw[:, 0:1], w_, AX, OP.max,
                apply_absolute_value=True), inc=sV, serialize=True)
            op("vector", lambda e, ft_=ft: e.tensor_scalar(
                out=sw2[:, ft_:ft_ + 1], in0=lnw[:, 0:1], scalar1=EPS_Q,
                scalar2=R127, op0=OP.max, op1=OP.mult), inc=sV, serialize=True)
            op("vector", lambda e, ft_=ft: e.reciprocal(
                lnw[:, 1:2], sw2[:, ft_:ft_ + 1]), inc=sV, serialize=True)
            for hh in range(4):
                op("vector", lambda e, hh_=hh, w_=wb: e.tensor_scalar(
                    out=p3_tq[:, :], in0=w_[:, hh_ * 1024:(hh_ + 1) * 1024],
                    scalar1=lnw[:, 1:2], scalar2=MAGIC, op0=OP.mult,
                    op1=OP.add), inc=sV)
                op("vector", lambda e, hh_=hh: e.tensor_scalar(
                    out=p4_w2q[:, hh_ * 1024:(hh_ + 1) * 1024], in0=p3_tq[:, :],
                    scalar1=MAGIC, scalar2=None, op0=OP.subtract), inc=sV)
            wq2 = sV.n
            w2_q_done[ft] = wq2
            if ft + 2 < 8:
                p4w_load(ft + 2)
            wait("tensor", sV, wq2)
            wait("tensor", sT, p3_t_done)
            for g in range(4):
                wait("tensor", sV, sV.n)
                for q in range(8):
                    fc = g * 8 + q
                    op("tensor", lambda e, q_=q, fc_=fc: e.matmul(
                        ps7b[:, q_ * 128:(q_ + 1) * 128],
                        p4_w2q[:, fc_ * 128:(fc_ + 1) * 128], identb[:, :],
                        is_transpose=True), inc=sT if q == 7 else None)
                wt2 = sT.n
                wait("vector", sT, wt2)
                op("vector", lambda e, g_=g, ft_=ft: e.tensor_copy(
                    w2v[:, g_ * 8:(g_ + 1) * 8, ft_ * 128:(ft_ + 1) * 128],
                    ps7b[:, :].rearrange("p (a c) -> p a c", a=8)), inc=sV)
        w2_ready_v = sV.n

        wait("vector", sV, w2_ready_v)
        op("vector", lambda e: e.tensor_tensor(
            out=ccol[:, :], in0=mf[:, :], in1=b1i[:, :], op=OP.add), inc=sV,
            serialize=True)
        op("vector", lambda e: e.tensor_tensor(
            out=ccol[:, :], in0=ccol[:, :], in1=s1sw[:, :], op=OP.mult),
            inc=sV, serialize=True)
        op("vector", lambda e: e.reduce_max(lnw[:, 5:6], ccol[:, :], axis=AX),
           inc=sV, serialize=True)
        hm_col = sV.n
        wait("tensor", sV, hm_col)
        op("tensor", lambda e: e.matmul(
            ps[0][0:1, 0:128], lnw[:, 5:6], identf[:, :], is_transpose=True),
            inc=sT)
        wait("vector", sT, sT.n)
        op("vector", lambda e: e.reduce_max(
            sc1[0:1, 3:4], ps[0][0:1, 0:128], axis=AX), inc=sV,
            serialize=True)
        op("vector", lambda e: e.tensor_scalar(
            out=sc1[0:1, 3:4], in0=sc1[0:1, 3:4], scalar1=0.0, scalar2=None,
            op0=OP.max), inc=sV, serialize=True)
        wait("gpsimd", sV, sV.n)
        op("gpsimd", lambda e: e.dma_start(out=m3_i.ap(), in_=sc1[0:1, 3:4]),
           inc=sG, k=16)
        wait("gpsimd", sG, sG.n)
        op("gpsimd", lambda e: e.collective_compute(
            "AllReduce", OP.max, replica_groups=[list(range(NCORES))],
            ins=[m3_i.ap().opt()], outs=[m3_o.ap().opt()]), inc=sCC)
        wait("gpsimd", sCC, sCC.n)
        op("gpsimd", lambda e: e.dma_start(out=sc1[0:1, 3:4], in_=m3_o.ap()),
           inc=sG, k=16)
        wait("vector", sG, sG.n)
        op("vector", lambda e: e.tensor_scalar(
            out=srow[0:1, 4:5], in0=sc1[0:1, 3:4], scalar1=EPS_Q,
            scalar2=R255, op0=OP.max, op1=OP.mult), inc=sV, serialize=True)
        op("vector", lambda e: e.reciprocal(srow[0:1, 3:4], srow[0:1, 4:5]),
           inc=sV, serialize=True)
        s2_ready = sV.n
        wait("tensor", sV, s2_ready)
        op("tensor", lambda e: e.matmul(
            ps[0][:, 0:16], ones1[:, :], srow[0:1, 0:16], start=True,
            stop=True), inc=sT)
        wait("vector", sT, sT.n)
        op("vector", lambda e: e.tensor_copy(scm[:, 3:5], ps[0][:, 3:5]),
           inc=sV, serialize=True)
        op("vector", lambda e: e.tensor_scalar(
            out=ccol[:, :], in0=s1sw[:, :], scalar1=scm[:, 3:4], scalar2=None,
            op0=OP.mult), inc=sV, serialize=True)
        op("vector", lambda e: e.tensor_tensor(
            out=bcol[:, :], in0=b1i[:, :], in1=ccol[:, :], op=OP.mult),
            inc=sV, serialize=True)
        op("vector", lambda e: e.tensor_scalar(
            out=s2sw[:, :], in0=sw2[:, :], scalar1=scm[:, 4:5], scalar2=None,
            op0=OP.mult), inc=sV, serialize=True)
        op("vector", lambda e: e.reciprocal(b2i[:, :], s2sw[:, :]), inc=sV,
           serialize=True)
        op("vector", lambda e: e.tensor_tensor(
            out=b2i[:, :], in0=b2_s[:, :], in1=b2i[:, :], op=OP.mult), inc=sV,
            serialize=True)
        op("vector", lambda e: e.tensor_scalar(
            out=b2i[:, :], in0=b2i[:, :], scalar1=MAGIC, scalar2=None,
            op0=OP.add), inc=sV, serialize=True)
        op("vector", lambda e: e.tensor_scalar(
            out=b2i[:, :], in0=b2i[:, :], scalar1=MAGIC, scalar2=None,
            op0=OP.subtract), inc=sV, serialize=True)
        op("vector", lambda e: e.tensor_tensor(
            out=bq2v[:, :], in0=b2i[:, :], in1=s2sw[:, :], op=OP.mult),
            inc=sV, serialize=True)
        rows_ready = sV.n
        wait("tensor", sV, rows_ready)
        op("tensor", lambda e: e.matmul(
            ps[0][0:8, 0:128], s2sw[:, :], identf[:, :], is_transpose=True))
        op("tensor", lambda e: e.matmul(
            ps[1][0:8, 0:128], bq2v[:, :], identf[:, :], is_transpose=True),
            inc=sT)
        wait("vector", sT, sT.n)
        op("vector", lambda e: e.tensor_copy(s2row[:, :], ps[0][0:8, 0:128]),
           inc=sV)
        op("vector", lambda e: e.tensor_copy(bq2row[:, :], ps[1][0:8, 0:128]),
           inc=sV)
        wait("gpsimd", sV, sV.n)
        op("gpsimd", lambda e: e.dma_start(
            out=rowb_d.ap()[0], in_=s2row[:, :]), inc=sG, k=16)
        op("gpsimd", lambda e: e.dma_start(
            out=rowb_d.ap()[1], in_=bq2row[:, :]), inc=sG, k=16)
        wait("gpsimd", sG, sG.n)
        op("gpsimd", lambda e: e.dma_start(
            out=s2rowf[0:1, :],
            in_=rowb_d.ap()[0].rearrange("j c -> (j c)").unsqueeze(0)),
            inc=sG, k=16)
        op("gpsimd", lambda e: e.dma_start(
            out=bq2rowf[0:1, :],
            in_=rowb_d.ap()[1].rearrange("j c -> (j c)").unsqueeze(0)),
            inc=sG, k=16)
        wait("tensor", sG, sG.n)
        for i_, (bk, rowf) in enumerate(
                ((2, s2rowf), (3, s2rowf), (4, bq2rowf), (5, bq2rowf))):
            co = (i_ % 2) * 512
            op("tensor", lambda e, b_=bk, co_=co, r_=rowf: e.matmul(
                ps[b_][:, :], ones1[:, :], r_[0:1, co_:co_ + 512],
                start=True, stop=True), inc=sT if i_ == 3 else None)
        wait("vector", sT, sT.n)
        op("vector", lambda e: e.tensor_copy(p4_sb2[:, 0:512], ps[2][:, :]))
        op("vector", lambda e: e.tensor_copy(p4_sb2[:, 512:1024], ps[3][:, :]))
        op("vector", lambda e: e.tensor_copy(p4_bq2[:, 0:512], ps[4][:, :]))
        op("vector", lambda e: e.tensor_copy(p4_bq2[:, 512:1024], ps[5][:, :]),
           inc=sV)
        p4_consts = sV.n

        milestone("p4w")
        wait("sync", sS, sS.n)
        if debug_dump:
            wait("sync", sV, sV.n)
            op("sync", lambda e: e.dma_start(out=dbg_w2[:, :, :], in_=w2v[:, :, :]), inc=sS, k=16)
            op("sync", lambda e: e.dma_start(out=dbg_sb[:, 0:1024], in_=p4_sb2[:, :]), inc=sS, k=16)
            op("sync", lambda e: e.dma_start(out=dbg_sb[:, 1024:2048], in_=p4_bq2[:, :]), inc=sS, k=16)
            op("sync", lambda e: e.dma_start(out=dbg_cb[:, 0:32], in_=ccol[:, :]), inc=sS, k=16)
            op("sync", lambda e: e.dma_start(out=dbg_cb[:, 32:64], in_=bcol[:, :]), inc=sS, k=16)
            op("sync", lambda e: e.dma_start(out=dbg_cb[:, 64:96], in_=s1sw[:, :]), inc=sS, k=16)
            op("sync", lambda e: e.dma_start(out=dbg_cb[:, 96:128], in_=b1i[:, :]), inc=sS, k=16)
            op("sync", lambda e: e.dma_start(out=dbg_cb[:, 128:133], in_=scm[:, 0:5]), inc=sS, k=16)
        NF = 16 * 32
        i1_ld = [0] * NF
        t0_inc = [0] * NF
        t1_inc = [0] * NF
        hint_inc = [0] * NF
        fc_mm = [0] * NF
        ep_mult_v = [0] * 16
        ep_add_v = [0] * 16
        out_st = [0] * 16
        x1_ld4 = [0] * 16

        wait("sync", sT, sT.n)
        wait("sync", sC, sC.n)
        wait("sync", sV, sV.n)

        ep_mult_bank = [[0] * 8 for _ in range(16)]

        def emit_epilogue_mults(sl):
            if sl > 0:
                wait("vector", sG, out_st[sl - 1])
            wait("vector", sT, fc_mm[sl * 32 + 31])
            for tt in range(4):
                for hh in range(2):
                    b = tt * 2 + hh
                    op("vector", lambda e, b_=b, hh_=hh, tt_=tt:
                       e.tensor_tensor(
                           out=p4_out[:, tt_, hh_ * 512:(hh_ + 1) * 512],
                           in0=ps[b_][:, :],
                           in1=p4_sb2[:, hh_ * 512:(hh_ + 1) * 512],
                           op=OP.mult), inc=sV)
                    ep_mult_bank[sl][b] = sV.n
            ep_mult_v[sl] = sV.n

        def emit_epilogue_adds(sl):
            wait("vector", sG, x1_ld4[sl])
            for tt in range(4):
                op("vector", lambda e, tt_=tt: e.tensor_tensor(
                    out=p4_x1[:, tt_, :], in0=p4_x1[:, tt_, :],
                    in1=p4_bq2[:, :], op=OP.add), inc=sV)
            for tt in range(4):
                op("vector", lambda e, tt_=tt: e.tensor_tensor(
                    out=p4_out[:, tt_, :], in0=p4_out[:, tt_, :],
                    in1=p4_x1[:, tt_, :], op=OP.add), inc=sV)
            ep_add_v[sl] = sV.n
            wait("gpsimd", sV, ep_add_v[sl])
            op("gpsimd", lambda e, s_=sl: e.dma_start(
                out=out_d[s_ * 512:(s_ + 1) * 512, :].rearrange(
                    "(t p) c -> p t c", p=128), in_=p4_out[:, :, :]), inc=sG,
                k=16)
            out_st[sl] = sG.n

        for F in range(NF):
            sl, fc = divmod(F, 32)
            if F >= 4:
                wait("sync", sC, t0_inc[F - 4])

            if F >= 1:
                wait("sync", sL, i1_ld[F - 1])
            op("sync", lambda e, s_=sl, f_=fc, r=F % 4: e.dma_start(
                out=p4_i1[:, r, :], in_=i1_d[s_, f_, :, :]), inc=sL, k=16)
            i1_ld[F] = sL.n
            if fc == 9:
                if sl > 0:
                    wait("gpsimd", sV, ep_add_v[sl - 1])
                op("gpsimd", lambda e, s_=sl: e.dma_start(
                    out=p4_x1[:, :, :],
                    in_=x1_d[s_ * 4:(s_ + 1) * 4, :, :].rearrange(
                        "t p c -> p t c")), inc=sG, k=16)
                x1_ld4[sl] = sG.n

            wait("scalar", sL, i1_ld[F])
            wait("scalar", sV, p4_consts)
            if F >= 2:
                wait("scalar", sV, t1_inc[F - 2])
            op("scalar", lambda e, f_=fc, r=F % 4, r0=F % 2: e.activation(
                p4_t0[:, r0, :], p4_i1[:, r, :], ACT.Identity,
                bias=bcol[:, f_:f_ + 1], scale=ccol[:, f_:f_ + 1]), inc=sC)
            t0_inc[F] = sC.n

            wait("vector", sC, t0_inc[F])
            op("vector", lambda e, r0=F % 2: e.tensor_scalar(
                out=p4_t1[:, :], in0=p4_t0[:, r0, :], scalar1=MAGIC,
                scalar2=None, op0=OP.add), inc=sV)
            t1_inc[F] = sV.n
            if F >= 8:
                wait("vector", sT, fc_mm[F - 8])
            op("vector", lambda e, r=F % 8: e.tensor_scalar(
                out=p4_hint[:, r, :], in0=p4_t1[:, :], scalar1=MAGIC,
                scalar2=MAGIC, op0=OP.max, op1=OP.subtract), inc=sV)
            hint_inc[F] = sV.n
            if debug_dump:
                wait("sync", sV, hint_inc[F])
                op("sync", lambda e, F_=F, r=F % 8: e.dma_start(
                    out=dbg_ha[F_], in_=p4_hint[:, r, :]), inc=sS, k=16)
            if fc == 2 and sl > 0:
                emit_epilogue_mults(sl - 1)
            if fc == 8 and sl > 0:
                emit_epilogue_adds(sl - 1)

            def emit_tensor_fc(Ft, slack_v):
                slt, fct = divmod(Ft, 32)
                wait("tensor", sV, slack_v)
                for tt in range(4):
                    for hh in range(2):
                        b = tt * 2 + hh
                        if fct == 0 and slt > 0:
                            wait("tensor", sV, ep_mult_bank[slt - 1][b])
                        op("tensor", lambda e, b_=b, tt_=tt, hh_=hh,
                           f_=fct, r=Ft % 8: e.matmul(
                               ps[b_][:, :],
                               p4_hint[:, r, tt_ * 128:(tt_ + 1) * 128],
                               w2v[:, f_, hh_ * 512:(hh_ + 1) * 512],
                               start=(f_ == 0), stop=(f_ == 31)),
                           inc=sT if (tt == 3 and hh == 1) else None)
                fc_mm[Ft] = sT.n

            if F >= 2:
                emit_tensor_fc(F - 2, hint_inc[F])
        op("scalar", lambda e: e.memzero(lnw[:, 1:2]), inc=sC)
        op("vector", lambda e: e.memzero(p4_t1[:, 0:1]), inc=sV)
        op("vector", lambda e: e.memzero(p4_t1[:, 1:2]), inc=sV)
        emit_tensor_fc(NF - 2, sV.n)
        emit_tensor_fc(NF - 1, sV.n)
        emit_epilogue_mults(15)
        emit_epilogue_adds(15)

        milestone("full")
        wait("sync", sS, sS.n)
        wait("gpsimd", sG, sG.n)
        if stop_stage is not None:
            cut = milestones[stop_stage]
            for k in ENGINES:
                del plan[k][cut[k]:]

        with nc.Block() as block:
            @block.sync
            def _(eng):
                for f in plan["sync"]:
                    f(eng)

            @block.tensor
            def _(eng):
                for f in plan["tensor"]:
                    f(eng)

            @block.vector
            def _(eng):
                for f in plan["vector"]:
                    f(eng)

            @block.scalar
            def _(eng):
                for f in plan["scalar"]:
                    f(eng)

            @block.gpsimd
            def _(eng):
                for f in plan["gpsimd"]:
                    f(eng)

    return nc


def _host_inputs(L0_attn_x, L0_attn_Wq, L0_attn_Wk, L0_attn_Wv, L0_attn_Wo,
                 L0_attn_gate, L0_attn_M0, L0_ffn_w1, L0_ffn_b1, L0_ffn_w2,
                 L0_ffn_b2):
    f32 = np.float32
    x = np.asarray(L0_attn_x, dtype=f32)
    wq = np.asarray(L0_attn_Wq, dtype=f32)
    wk = np.asarray(L0_attn_Wk, dtype=f32)
    wv = np.asarray(L0_attn_Wv, dtype=f32)
    wo = np.asarray(L0_attn_Wo, dtype=f32)
    gate = np.asarray(L0_attn_gate, dtype=f32).reshape(1, 1)
    m0 = np.asarray(L0_attn_M0, dtype=f32)
    w1 = np.ascontiguousarray(np.asarray(L0_ffn_w1, dtype=f32))
    b1 = np.asarray(L0_ffn_b1, dtype=f32)
    w2 = np.ascontiguousarray(np.asarray(L0_ffn_w2, dtype=f32))
    b2 = np.asarray(L0_ffn_b2, dtype=f32)

    wqt = np.ascontiguousarray(wq.T)
    wkt = np.ascontiguousarray(wk.T)
    wvt = np.ascontiguousarray(wv.T)
    wot = np.ascontiguousarray(wo.T)
    m0t = np.ascontiguousarray(m0.T)
    b1r = np.ascontiguousarray(b1.reshape(FF // 128, 128).T)
    b2r = np.ascontiguousarray(b2.reshape(D // 128, 128).T)

    maskv = np.zeros((128, 128), dtype=f32)
    for kr in range(128):
        for qr in range(128):
            if kr // 8 < qr // 8:
                maskv[kr, qr] = 1.0
    identf = np.eye(128, dtype=f32)
    identb = np.eye(128, dtype=f32).astype(ml_dtypes.bfloat16)
    taus = np.arange(SUB) // B
    dqrow = (DECAY ** taus.astype(np.float64)).astype(f32).reshape(1, SUB)
    dkrow = ((DECAY ** (-1.0 - taus.astype(np.float64))) / B).astype(
        f32).reshape(1, SUB)

    in_maps = []
    for c in range(NCORES):
        xs = np.ascontiguousarray(
            x[:, c * L:(c + 1) * L, :].transpose(1, 0, 2).reshape(RPC, D))
        coef = np.zeros((1, 16), dtype=f32)
        for j in range(c):
            coef[0, j] = f32(DECAY ** (L * (c - 1 - j)))
        coef[0, 8] = f32(DECAY ** (L * c))
        in_maps.append(dict(
            xr=xs, wqt=wqt, wkt=wkt, wvt=wvt, wot=wot, m0t=m0t,
            gate_in=gate, w1_in=w1, b1_in=b1r, w2_in=w2, b2_in=b2r,
            maskc=maskv, identf_in=identf, identb_in=identb,
            dqrow_in=dqrow, dkrow_in=dkrow, coef_in=coef))
    return in_maps


def kernel(**inputs):
    in_maps = _host_inputs(**inputs)
    nc = _build_nc()
    res = run_bass_kernel_spmd(nc, in_maps, core_ids=list(range(NCORES)))
    outs = []
    for c in range(NCORES):
        oc = np.asarray(res.results[c]["out"])
        outs.append(oc.reshape(L, B, D).transpose(1, 0, 2))
    return np.ascontiguousarray(np.concatenate(outs, axis=1))


# version 18
# speedup vs baseline: 1.0686x; 1.0443x over previous
import numpy as np
import ml_dtypes
from contextlib import ExitStack

import concourse.bass as bass
import concourse.mybir as mybir
from concourse.bass_utils import run_bass_kernel_spmd

F32 = mybir.dt.float32
BF16 = mybir.dt.bfloat16
OP = mybir.AluOpType
ACT = mybir.ActivationFunctionType
AX = mybir.AxisListType.X

MAGIC = 12582912.0
DECAY = 0.99
NCORES = 8
L = 1024
B = 8
RPC = L * B
D = 1024
M = 256
FF = 4096
C = 64
SUB = C * B
NSUB = L // C
EPS_LN = 1e-5
EPS_Q = 1e-8
R255 = float(np.float32(1.0 / 255.0))
R127 = float(np.float32(1.0 / 127.0))
RD = float(np.float32(1.0 / D))
D64 = float(np.float32(DECAY ** 64))

ENGINES = ("sync", "tensor", "vector", "scalar", "gpsimd")


class _Sem:
    __slots__ = ("h", "n")

    def __init__(self, h):
        self.h = h
        self.n = 0


def _build_nc(debug_dump=False, stop_stage=None):
    nc = bass.Bass("TRN2", target_bir_lowering=False, detect_race_conditions=False)
    _dk = "ExternalOutput" if debug_dump else "Internal"

    xr = nc.dram_tensor("xr", [RPC, D], F32, kind="ExternalInput").ap()
    wqt = nc.dram_tensor("wqt", [D, M], F32, kind="ExternalInput").ap()
    wkt = nc.dram_tensor("wkt", [D, M], F32, kind="ExternalInput").ap()
    wvt = nc.dram_tensor("wvt", [D, M], F32, kind="ExternalInput").ap()
    wot = nc.dram_tensor("wot", [M, D], F32, kind="ExternalInput").ap()
    m0t = nc.dram_tensor("m0t", [M, M], F32, kind="ExternalInput").ap()
    gate_in = nc.dram_tensor("gate_in", [1, 1], F32, kind="ExternalInput").ap()
    w1_in = nc.dram_tensor("w1_in", [FF, D], F32, kind="ExternalInput").ap()
    b1_in = nc.dram_tensor("b1_in", [128, FF // 128], F32, kind="ExternalInput").ap()
    w2_in = nc.dram_tensor("w2_in", [D, FF], F32, kind="ExternalInput").ap()
    b2_in = nc.dram_tensor("b2_in", [128, D // 128], F32, kind="ExternalInput").ap()
    maskc = nc.dram_tensor("maskc", [128, 128], F32, kind="ExternalInput").ap()
    identf_in = nc.dram_tensor("identf_in", [128, 128], F32, kind="ExternalInput").ap()
    identb_in = nc.dram_tensor("identb_in", [128, 128], BF16, kind="ExternalInput").ap()
    dqrow_in = nc.dram_tensor("dqrow_in", [1, SUB], F32, kind="ExternalInput").ap()
    dkrow_in = nc.dram_tensor("dkrow_in", [1, SUB], F32, kind="ExternalInput").ap()
    coef_in = nc.dram_tensor("coef_in", [1, 16], F32, kind="ExternalInput").ap()

    out_d = nc.dram_tensor("out", [RPC, D], F32, kind="ExternalOutput").ap()

    qg_d = nc.dram_tensor("qg_d", [NSUB, 128, 2 * SUB], F32, kind=_dk).ap()
    rp_d = nc.dram_tensor("rp_d", [NSUB, 128, 2 * SUB], F32, kind=_dk).ap()
    x1_d = nc.dram_tensor("x1_d", [64, 128, D], F32, kind=_dk).ap()
    i1_d = nc.dram_tensor("i1_d", [16, 32, 128, 512], F32, kind=_dk).ap()
    g_in = nc.dram_tensor("g_in", [128, 512], F32, kind="Internal")
    g_out = nc.dram_tensor("g_out", [NCORES * 128, 512], F32, kind="Internal")
    m2_i = nc.dram_tensor("m2_i", [1, 1], F32, kind="Internal")
    m2_o = nc.dram_tensor("m2_o", [1, 1], F32, kind="Internal")
    m3_i = nc.dram_tensor("m3_i", [1, 1], F32, kind="Internal")
    m3_o = nc.dram_tensor("m3_o", [1, 1], F32, kind="Internal")
    rowb_d = nc.dram_tensor("rowb_d", [2, 8, 128], F32, kind="Internal")
    if debug_dump:
        dbg_hint = nc.dram_tensor("dbg_hint", [128, 32, 256], BF16,
                                  kind="ExternalOutput").ap()
        dbg_cb = nc.dram_tensor("dbg_cb", [128, 144], F32,
                                kind="ExternalOutput").ap()
        dbg_w2 = nc.dram_tensor("dbg_w2", [128, 32, 1024], BF16,
                                kind="ExternalOutput").ap()
        dbg_sb = nc.dram_tensor("dbg_sb", [128, 2048], F32,
                                kind="ExternalOutput").ap()
        dbg_ln = nc.dram_tensor("dbg_ln", [128, 64], F32,
                                kind="ExternalOutput").ap()
        dbg_ha = nc.dram_tensor("dbg_ha", [512, 128, 512], BF16,
                                kind="ExternalOutput").ap()
        dbg_xn = nc.dram_tensor("dbg_xn", [128, 1024], F32,
                                kind="ExternalOutput").ap()
        dbg_x0 = nc.dram_tensor("dbg_x0", [128, 1024], F32,
                                kind="ExternalOutput").ap()

    stack = ExitStack()
    with stack:
        sb = lambda name, shape, dt=F32: stack.enter_context(
            nc.sbuf_tensor(name, shape, dt))

        ones1 = sb("ones1", [1, 128])
        identf = sb("identf", [128, 128])
        identb = sb("identb", [128, 128], BF16)
        mask = sb("mask", [128, 128])
        wq_s = sb("wq_s", [128, 8, M])
        wk_s = sb("wk_s", [128, 8, M])
        wv_s = sb("wv_s", [128, 8, M])
        wo_s = sb("wo_s", [128, 2, D])
        m0_s = sb("m0_s", [128, 2, M])
        dqr = sb("dqr", [1, SUB])
        dkr = sb("dkr", [1, SUB])
        dq_b = sb("dq_b", [128, SUB])
        dk_b = sb("dk_b", [128, SUB])
        coefr = sb("coefr", [1, 16])
        coefb = sb("coefb", [128, 16])
        scm = sb("scm", [128, 16])
        srow = sb("srow", [1, 16])
        stats = sb("stats", [128, 64, 2])
        mf = sb("mf", [128, 32])
        sw1 = sb("sw1", [128, 32])
        s1sw = sb("s1sw", [128, 32])
        b1i = sb("b1i", [128, 32])
        ccol = sb("ccol", [128, 32])
        bcol = sb("bcol", [128, 32])
        b1_s = sb("b1_s", [128, 32])
        sw2 = sb("sw2", [128, 8])
        b2i = sb("b2i", [128, 8])
        b2_s = sb("b2_s", [128, 8])
        s2sw = sb("s2sw", [128, 8])
        bq2v = sb("bq2v", [128, 8])
        s2row = sb("s2row", [8, 128])
        bq2row = sb("bq2row", [8, 128])
        s2rowf = sb("s2rowf", [1, 1024])
        bq2rowf = sb("bq2rowf", [1, 1024])
        zt = sb("zt", [128, 2, 2, M])
        wc_s = sb("wc_s", [128, 2, M])
        lnw = sb("lnw", [128, 16])
        xmax = sb("xmax", [128, 1])
        sc1 = sb("sc1", [1, 4])
        wint = sb("wint", [128, 32768], BF16)

        AR = 17408
        arena = sb("arena", [128, AR])

        def carve(off, n):
            assert off + n <= AR, (off, n)
            return arena[:, off:off + n], off + n

        o = 0
        a_x, o = carve(o, 4096)
        a_xn, o = carve(o, 1024)
        a_xnt, o = carve(o, 4096)
        a_q, o = carve(o, 1024)
        a_qg, o = carve(o, 1024)
        a_k, o = carve(o, 1024)
        a_kt, o = carve(o, 1024)
        a_v, o = carve(o, 1024)
        a_a, o = carve(o, 2048)
        a_r, o = carve(o, 1024)
        assert o <= AR

        x_t = a_x.rearrange("p (t c) -> p t c", t=4)
        xnt = a_xnt.rearrange("p (dc c) -> p dc c", dc=8)
        qt = a_q.rearrange("p (mc c) -> p mc c", mc=2)
        qgt = a_qg.rearrange("p (mc c) -> p mc c", mc=2)
        kt_ = a_k.rearrange("p (mc c) -> p mc c", mc=2)
        ktok = a_kt.rearrange("p (t c) -> p t c", t=4)
        v_s = a_v.rearrange("p (t c) -> p t c", t=4)
        a_sb = a_a.rearrange("p (k c) -> p k c", k=4)
        rt = a_r.rearrange("p (ic c) -> p ic c", ic=2)

        x1b = a_xnt.rearrange("p (t c) -> p t c", t=4)
        rpb = a_k.rearrange("p (ic c) -> p ic c", ic=2)
        gall = a_x.rearrange("p (j c) -> p j c", j=8)
        gacc = a_xn

        p3_tq = arena[:, 4096 + 1024 + 4096:4096 + 1024 + 4096 + 1024]
        p3_xint = a_k.bitcast(BF16)
        p3_xintT = a_xnt.bitcast(BF16)[:, 0:4096].rearrange(
            "p (dc c) -> p dc c", dc=8)
        p3_xintTv = [p3_xintT,
                     a_xnt.bitcast(BF16)[:, 4096:8192].rearrange(
                         "p (dc c) -> p dc c", dc=8)]
        p4_w2q = a_xnt.bitcast(BF16)[:, 0:4096]
        p3_i1 = a_a.rearrange("p (g c) -> p g c", g=4)
        p3_wbuf = arena[:, 13312:13312 + 4096]

        p4_hint = arena[:, 0:2048].bitcast(BF16).rearrange(
            "p (r c) -> p r c", r=8)
        p4_i1 = arena[:, 2048:2048 + 2048].rearrange(
            "p (r c) -> p r c", r=4)
        p4_t0 = arena[:, 4096:4096 + 1024].rearrange(
            "p (r c) -> p r c", r=2)
        p4_t1 = arena[:, 5120:5120 + 512]
        p4_x1 = arena[:, 5632:5632 + 4096].rearrange(
            "p (t c) -> p t c", t=4)
        p4_out = arena[:, 9728:9728 + 4096].rearrange(
            "p (t c) -> p t c", t=4)
        p4_u = arena[:, 13824:13824 + 512]
        p4_sb2 = arena[:, 14336:14336 + 1024]
        p4_bq2 = arena[:, 15360:15360 + 1024]

        w1v = wint.rearrange("p (dc f) -> p dc f", dc=8)
        w2v = wint.rearrange("p (fc d) -> p fc d", fc=32)

        ps = [stack.enter_context(nc.psum_tensor(f"ps{i}", [128, 512], F32))
              for i in range(8)]
        ps7b = ps[7].ap().bitcast(BF16)

        sL = _Sem(stack.enter_context(nc.semaphore("sL")))
        sS = _Sem(stack.enter_context(nc.semaphore("sS")))
        sG = _Sem(stack.enter_context(nc.semaphore("sG")))
        sT = _Sem(stack.enter_context(nc.semaphore("sT")))
        sV = _Sem(stack.enter_context(nc.semaphore("sV")))
        sC = _Sem(stack.enter_context(nc.semaphore("sC")))
        sCC = _Sem(stack.enter_context(nc.semaphore("sCC")))

        plan = {k: [] for k in ENGINES}
        lastw = {k: {} for k in ENGINES}
        milestones = {}

        def milestone(name):
            milestones[name] = {k: len(v) for k, v in plan.items()}

        def wait(eng, sem, val=None):
            v = sem.n if val is None else val
            if v <= 0:
                return
            if lastw[eng].get(id(sem), -1) >= v:
                return
            lastw[eng][id(sem)] = v
            plan[eng].append(lambda e, s=sem.h, vv=v: e.wait_ge(s, vv))

        def op(eng, fn, inc=None, k=1, serialize=False):
            if inc is None:
                plan[eng].append(fn)
            else:
                inc.n += k
                plan[eng].append(
                    lambda e, f=fn, s=inc.h, kk=k: f(e).then_inc(s, kk))
                if serialize:
                    v = inc.n
                    plan[eng].append(
                        lambda e, s=inc.h, vv=v: e.wait_ge(s, vv))
                    lastw[eng][id(inc)] = max(lastw[eng].get(id(inc), -1), v)

        for dst, src in [(wq_s[:, :, :], wqt.rearrange("(dc p) m -> p dc m", p=128)),
                         (wk_s[:, :, :], wkt.rearrange("(dc p) m -> p dc m", p=128)),
                         (wv_s[:, :, :], wvt.rearrange("(dc p) m -> p dc m", p=128)),
                         (wo_s[:, :, :], wot.rearrange("(mc p) h -> p mc h", p=128)),
                         (m0_s[:, :, :], m0t.rearrange("(jc p) i -> p jc i", p=128)),
                         (mask[:, :], maskc),
                         (identf[:, :], identf_in),
                         (identb[:, :], identb_in),
                         (dqr[:, :], dqrow_in),
                         (dkr[:, :], dkrow_in),
                         (coefr[:, :], coef_in),
                         (sc1[:, 0:1], gate_in),
                         (b1_s[:, :], b1_in),
                         (b2_s[:, :], b2_in)]:
            op("sync", lambda e, d=dst, s=src: e.dma_start(out=d, in_=s),
               inc=sL, k=16)
        consts_loaded = sL.n

        op("vector", lambda e: e.memset(ones1[:, :], 1.0), inc=sV)
        op("vector", lambda e: e.memset(zt[:, 0, :, :], 0.0), inc=sV)
        op("vector", lambda e: e.memset(mf[:, :], -3.0e38), inc=sV)
        op("vector", lambda e: e.memset(xmax[:, :], -3.0e38), inc=sV)
        op("vector", lambda e: e.memset(srow[:, :], 0.0), inc=sV)
        op("vector", lambda e: e.memset(lnw[:, :], 0.0), inc=sV)
        wait("vector", sL, consts_loaded)
        op("vector", lambda e: e.memset(sc1[0:1, 2:4], 0.0), inc=sV)
        wait("scalar", sL, consts_loaded)
        op("scalar", lambda e: e.activation(sc1[0:1, 1:2], sc1[0:1, 0:1],
                                            ACT.Sigmoid), inc=sC)
        wait("tensor", sL, consts_loaded)
        wait("tensor", sV, sV.n)
        wait("tensor", sC, sC.n)
        op("tensor", lambda e: e.matmul(ps[0][:, :], ones1[:, :], dqr[:, :],
                                        start=True, stop=True))
        op("tensor", lambda e: e.matmul(ps[1][:, :], ones1[:, :], dkr[:, :],
                                        start=True, stop=True))
        op("tensor", lambda e: e.matmul(ps[2][:, 0:16], ones1[:, :], coefr[:, :],
                                        start=True, stop=True))
        op("tensor", lambda e: e.matmul(ps[2][:, 16:17], ones1[:, :],
                                        sc1[0:1, 1:2], start=True, stop=True),
           inc=sT)
        wait("vector", sT, sT.n)
        op("vector", lambda e: e.tensor_copy(dq_b[:, :], ps[0][:, :]), inc=sV)
        op("vector", lambda e: e.tensor_copy(dk_b[:, :], ps[1][:, :]), inc=sV)
        op("vector", lambda e: e.tensor_copy(coefb[:, :], ps[2][:, 0:16]), inc=sV)
        op("vector", lambda e: e.tensor_copy(scm[:, 0:1], ps[2][:, 16:17]),
           inc=sV)
        const_done_v = sV.n
        const_done_t = sT.n

        milestone("consts")
        ln_evac_v = [0] * 5
        z_update_v = const_done_v
        last_tr = 0
        qg_spill_g = 0
        rp_spill_g = 0
        xld = [0] * 64
        xn_done_g = [0] * 64
        rs_done_g = [0] * 64
        xld_prev = 0

        def emit_x_tile_load(g):
            nonlocal_ = None
            if g >= 4:
                wait("sync", sC, xn_done_g[g - 4])
                wait("sync", sV, rs_done_g[g - 4])
            if g >= 1:
                wait("sync", sL, xld[g - 1])
            op("sync", lambda e, g_=g: e.dma_start(
                out=x_t[:, g_ % 4, :],
                in_=xr[g_ * 128:(g_ + 1) * 128, :]), inc=sL, k=16)
            xld[g] = sL.n

        for g in range(4):
            emit_x_tile_load(g)

        for s in range(NSUB):
            par = s % 2
            dsub = float(np.float32(DECAY ** (C * s)))

            for tt in range(4):
                wait("scalar", sL, xld[s * 4 + tt])
                wait("scalar", sT, last_tr)
                op("scalar", lambda e, tt_=tt: e.activation(
                    a_xn[:, :], x_t[:, tt_, :], ACT.Square,
                    accum_out=lnw[:, 4 + tt_:5 + tt_]), inc=sC)
                sq_done = sC.n
                wait("vector", sL, xld[s * 4 + tt])
                op("vector", lambda e, tt_=tt: e.reduce_sum(
                    lnw[:, tt_:tt_ + 1], x_t[:, tt_, :], axis=AX), inc=sV,
                    serialize=True)
                rs_done_g[s * 4 + tt] = sV.n
                wait("vector", sC, sq_done)
                op("vector", lambda e, tt_=tt: e.tensor_scalar(
                    out=lnw[:, 8:9], in0=lnw[:, tt_:tt_ + 1], scalar1=RD,
                    scalar2=None, op0=OP.mult), inc=sV, serialize=True)
                op("vector", lambda e: e.tensor_tensor(
                    out=lnw[:, 9:10], in0=lnw[:, 8:9], in1=lnw[:, 8:9],
                    op=OP.mult), inc=sV, serialize=True)
                op("vector", lambda e, tt_=tt: e.scalar_tensor_tensor(
                    out=lnw[:, 10:11], in0=lnw[:, 4 + tt_:5 + tt_], scalar=RD,
                    in1=lnw[:, 9:10], op0=OP.mult, op1=OP.subtract), inc=sV,
                    serialize=True)
                op("vector", lambda e: e.tensor_scalar(
                    out=lnw[:, 11:12], in0=lnw[:, 10:11], scalar1=EPS_LN,
                    scalar2=None, op0=OP.add), inc=sV, serialize=True)
                op("vector", lambda e: e.reciprocal(
                    lnw[:, 12:13], lnw[:, 11:12]), inc=sV, serialize=True)
                rec_done = sV.n
                wait("scalar", sV, rec_done)
                op("scalar", lambda e: e.activation(
                    lnw[:, 13:14], lnw[:, 12:13], ACT.Sqrt), inc=sC)
                rstd_done = sC.n
                wait("vector", sC, rstd_done)
                op("vector", lambda e: e.scalar_tensor_tensor(
                    out=lnw[:, 14:15], in0=lnw[:, 8:9], scalar=-1.0,
                    in1=lnw[:, 13:14], op0=OP.mult, op1=OP.mult), inc=sV,
                    serialize=True)
                nmr_done = sV.n
                wait("scalar", sV, nmr_done)
                op("scalar", lambda e, tt_=tt: e.activation(
                    a_xn[:, :], x_t[:, tt_, :], ACT.Identity,
                    bias=lnw[:, 14:15], scale=lnw[:, 13:14]), inc=sC)
                xn_done = sC.n
                xn_done_g[s * 4 + tt] = xn_done
                if debug_dump and s == 0:
                    wait("sync", sV, nmr_done)
                    op("sync", lambda e, tt_=tt: e.dma_start(
                        out=dbg_ln[:, tt_ * 16:(tt_ + 1) * 16],
                        in_=lnw[:, 0:16]), inc=sS, k=16)
                    if tt == 0:
                        wait("sync", sC, xn_done)
                        op("sync", lambda e: e.dma_start(
                            out=dbg_xn[:, :], in_=a_xn[:, :]), inc=sS, k=16)
                        op("sync", lambda e: e.dma_start(
                            out=dbg_x0[:, :], in_=x_t[:, 0, :]), inc=sS, k=16)
                        wait("scalar", sS, sS.n)
                        wait("vector", sS, sS.n)
                wait("tensor", sC, xn_done)
                wait("tensor", sV, ln_evac_v[-1] if s + tt else const_done_v)
                for dc in range(8):
                    bank, sl_ = (0, dc) if dc < 4 else (1, dc - 4)
                    last = dc == 7
                    op("tensor", lambda e, b=bank, q=sl_, dc_=dc: e.matmul(
                        ps[b][:, q * 128:(q + 1) * 128],
                        a_xn[:, dc_ * 128:(dc_ + 1) * 128], identf[:, :],
                        is_transpose=True), inc=sT if last else None)
                tr_done = sT.n
                last_tr = tr_done
                wait("vector", sT, tr_done)
                op("vector", lambda e, tt_=tt: e.tensor_copy(
                    xnt[:, 0:4, tt_ * 128:(tt_ + 1) * 128],
                    ps[0][:, :].rearrange("p (a c) -> p a c", a=4)), inc=sV)
                op("vector", lambda e, tt_=tt: e.tensor_copy(
                    xnt[:, 4:8, tt_ * 128:(tt_ + 1) * 128],
                    ps[1][:, :].rearrange("p (a c) -> p a c", a=4)), inc=sV)
                ln_evac_v.append(sV.n)

            xnt_done = sV.n
            for tt2 in range(4):
                g2 = (s + 1) * 4 + tt2
                if g2 < 64:
                    emit_x_tile_load(g2)

            wait("tensor", sV, xnt_done)
            for (wsrc, bank0) in ((wq_s, 2), (wk_s, 4)):
                for mc in range(2):
                    for dc in range(8):
                        op("tensor", lambda e, w=wsrc, b=bank0 + mc, mc_=mc,
                           dc_=dc: e.matmul(
                               ps[b][:, :],
                               w[:, dc_, mc_ * 128:(mc_ + 1) * 128],
                               xnt[:, dc_, :],
                               start=(dc_ == 0), stop=(dc_ == 7)),
                           inc=sT if dc == 7 else None)
            qk_done = sT.n
            wait("vector", sT, qk_done)
            wait("vector", sG, qg_spill_g)
            for mc in range(2):
                op("vector", lambda e, mc_=mc: e.tensor_tensor(
                    out=qt[:, mc_, :], in0=ps[2 + mc_][:, :], in1=dq_b[:, :],
                    op=OP.mult), inc=sV)
                op("vector", lambda e, mc_=mc, d_=dsub: e.tensor_scalar(
                    out=qgt[:, mc_, :], in0=qt[:, mc_, :], scalar1=d_,
                    scalar2=None, op0=OP.mult), inc=sV)
                op("vector", lambda e, mc_=mc: e.tensor_tensor(
                    out=kt_[:, mc_, :], in0=ps[4 + mc_][:, :], in1=dk_b[:, :],
                    op=OP.mult), inc=sV)
            qk_evac = sV.n
            wait("gpsimd", sV, qk_evac)
            op("gpsimd", lambda e, s_=s: e.dma_start(
                out=qg_d[s_, :, :], in_=a_qg), inc=sG, k=16)
            qg_spill_g = sG.n

            wait("tensor", sV, qk_evac)
            for tt in range(4):
                bank = 2 + tt // 2
                cofs = (tt % 2) * 256
                for dc in range(8):
                    op("tensor", lambda e, b=bank, co=cofs, tt_=tt, dc_=dc:
                       e.matmul(ps[b][:, co:co + 256],
                                xnt[:, dc_, tt_ * 128:(tt_ + 1) * 128],
                                wv_s[:, dc_, :],
                                start=(dc_ == 0), stop=(dc_ == 7)),
                       inc=sT if dc == 7 else None)
            v_done = sT.n
            wait("vector", sT, v_done)
            op("vector", lambda e: e.tensor_copy(
                v_s[:, 0:2, :], ps[2][:, :].rearrange("p (a c) -> p a c", a=2)),
                inc=sV)
            op("vector", lambda e: e.tensor_copy(
                v_s[:, 2:4, :], ps[3][:, :].rearrange("p (a c) -> p a c", a=2)),
                inc=sV)
            v_evac = sV.n

            wait("tensor", sV, v_evac)
            for tt in range(4):
                for mc in range(2):
                    idx = tt * 2 + mc
                    bank, sl_ = (0, idx) if idx < 4 else (1, idx - 4)
                    op("tensor", lambda e, b=bank, q=sl_, tt_=tt, mc_=mc:
                       e.matmul(ps[b][:, q * 128:(q + 1) * 128],
                                kt_[:, mc_, tt_ * 128:(tt_ + 1) * 128],
                                identf[:, :], is_transpose=True),
                       inc=sT if idx == 7 else None)
            ktok_tr = sT.n
            wait("vector", sT, ktok_tr)
            op("vector", lambda e: e.tensor_copy(
                ktok[:, 0:2, :], ps[0][:, :].rearrange("p (a c) -> p a c", a=2)),
                inc=sV)
            op("vector", lambda e: e.tensor_copy(
                ktok[:, 2:4, :], ps[1][:, :].rearrange("p (a c) -> p a c", a=2)),
                inc=sV)
            ktok_evac = sV.n

            a_banks = {0: 4, 1: 5, 2: 2, 3: 3}
            wait("tensor", sV, ktok_evac)
            for kt2 in range(4):
                b = a_banks[kt2]
                for mc in range(2):
                    op("tensor", lambda e, b_=b, kt3=kt2, mc_=mc: e.matmul(
                        ps[b_][:, kt3 * 128:512],
                        kt_[:, mc_, kt3 * 128:(kt3 + 1) * 128],
                        qt[:, mc_, kt3 * 128:512],
                        start=(mc_ == 0), stop=(mc_ == 1)),
                       inc=sT if (kt2 == 3 and mc == 1) else None)
            at_done = sT.n
            wait("vector", sT, at_done)
            for kt2 in range(4):
                b = a_banks[kt2]
                op("vector", lambda e, b_=b, kt3=kt2: e.tensor_tensor(
                    out=a_sb[:, kt3, kt3 * 128:(kt3 + 1) * 128],
                    in0=ps[b_][:, kt3 * 128:(kt3 + 1) * 128],
                    in1=mask[:, :], op=OP.mult), inc=sV)
                if kt2 < 3:
                    op("vector", lambda e, b_=b, kt3=kt2: e.tensor_copy(
                        a_sb[:, kt3, (kt3 + 1) * 128:512],
                        ps[b_][:, (kt3 + 1) * 128:512]), inc=sV)
            a_evac = sV.n

            wait("tensor", sV, a_evac)
            wait("tensor", sV, z_update_v)
            for ic in range(2):
                b = 6 + ic
                for jc in range(2):
                    op("tensor", lambda e, b_=b, ic_=ic, jc_=jc, p_=par:
                       e.matmul(ps[b_][:, :],
                                zt[:, p_, jc_, ic_ * 128:(ic_ + 1) * 128],
                                qt[:, jc_, :],
                                start=(jc_ == 0), stop=False))
                for kt2 in range(4):
                    op("tensor", lambda e, b_=b, ic_=ic, kt3=kt2: e.matmul(
                        ps[b_][:, kt3 * 128:512],
                        v_s[:, kt3, ic_ * 128:(ic_ + 1) * 128],
                        a_sb[:, kt3, kt3 * 128:512],
                        start=False, stop=(kt3 == 3)),
                       inc=sT if (ic == 1 and kt2 == 3) else None)
            r_done = sT.n
            wait("vector", sT, r_done)
            wait("vector", sG, rp_spill_g)
            for ic in range(2):
                op("vector", lambda e, ic_=ic: e.tensor_copy(
                    rt[:, ic_, :], ps[6 + ic_][:, :]), inc=sV)
            r_evac = sV.n
            wait("gpsimd", sV, r_evac)
            op("gpsimd", lambda e, s_=s: e.dma_start(
                out=rp_d[s_, :, :], in_=a_r), inc=sG, k=16)
            rp_spill_g = sG.n

            wait("tensor", sV, a_evac)
            for jc in range(2):
                for ut in range(4):
                    op("tensor", lambda e, jc_=jc, ut_=ut: e.matmul(
                        ps[4][:, jc_ * 256:(jc_ + 1) * 256],
                        ktok[:, ut_, jc_ * 128:(jc_ + 1) * 128],
                        v_s[:, ut_, :],
                        start=(ut_ == 0), stop=(ut_ == 3)),
                       inc=sT if (jc == 1 and ut == 3) else None)
            p_done = sT.n
            wait("vector", sT, p_done)
            op("vector", lambda e, p_=par: e.tensor_tensor(
                out=zt[:, 1 - p_, :, :].rearrange("p a c -> p (a c)"),
                in0=zt[:, p_, :, :].rearrange("p a c -> p (a c)"),
                in1=ps[4][:, :], op=OP.add), inc=sV)
            op("vector", lambda e, p_=par: e.tensor_scalar(
                out=zt[:, 1 - p_, :, :].rearrange("p a c -> p (a c)"),
                in0=zt[:, 1 - p_, :, :].rearrange("p a c -> p (a c)"),
                scalar1=D64, scalar2=None, op0=OP.mult), inc=sV)
            z_update_v = sV.n

        milestone("p1a")
        g_parity = NSUB % 2

        wait("gpsimd", sV, z_update_v)
        op("gpsimd", lambda e: e.dma_start(
            out=g_in.ap(), in_=zt[:, g_parity, :, :].rearrange(
                "p a c -> p (a c)")), inc=sG, k=16)
        wait("gpsimd", sG, sG.n)
        op("gpsimd", lambda e: e.collective_compute(
            "AllGather", OP.bypass, replica_groups=[list(range(NCORES))],
            ins=[g_in.ap().opt()], outs=[g_out.ap().opt()]), inc=sCC)
        wait("gpsimd", sCC, sCC.n)
        gall2 = a_xnt.rearrange("p (j c) -> p j c", j=8)
        op("gpsimd", lambda e: e.dma_start(
            out=gall2[:, :, :],
            in_=g_out.ap().rearrange("(j p) c -> p j c", p=128)), inc=sG, k=16)
        g_loaded = sG.n
        wait("vector", sG, g_loaded)
        op("vector", lambda e: e.tensor_scalar(
            out=gacc[:, 0:512], in0=gall2[:, 0, :], scalar1=coefb[:, 0:1],
            scalar2=None, op0=OP.mult), inc=sV)
        for j in range(1, NCORES):
            op("vector", lambda e, j_=j: e.scalar_tensor_tensor(
                out=gacc[:, 0:512], in0=gall2[:, j_, :],
                scalar=coefb[:, j_:j_ + 1], in1=gacc[:, 0:512],
                op0=OP.mult, op1=OP.add), inc=sV)
        op("vector", lambda e: e.scalar_tensor_tensor(
            out=wc_s[:, :, :].rearrange("p a c -> p (a c)"),
            in0=m0_s[:, :, :].rearrange("p a c -> p (a c)"),
            scalar=coefb[:, 8:9], in1=gacc[:, 0:512],
            op0=OP.mult, op1=OP.add), inc=sV)
        wc_done = sV.n

        milestone("cc1")
        qg_bufs = [qgt, a_q.rearrange("p (mc c) -> p mc c", mc=2)]
        rp_bufs = [rpb, a_kt.rearrange("p (mc c) -> p mc c", mc=2)]
        qg_raw = [a_qg, a_q]
        rp_raw = [a_k, a_kt]
        qg_ld_s = [0] * NSUB
        x_ld_s = [0] * NSUB
        rcorr_done_s = [0] * NSUB
        rf_done_s = [0] * NSUB
        x1stt_done_s = [0] * NSUB
        x1store_s = [0] * NSUB
        spills_done = max(qg_spill_g, rp_spill_g)

        def emit_p1b_loads(s):
            pb = s % 2
            if s == 0:
                wait("sync", sG, spills_done)
                wait("sync", sT, sT.n)
                wait("sync", sV, sV.n)
            else:
                wait("sync", sL, sL.n)
                if s >= 2:
                    wait("sync", sT, rcorr_done_s[s - 2])
                    wait("sync", sV, rf_done_s[s - 2])
            op("sync", lambda e, s_=s, p_=pb: e.dma_start(
                out=qg_raw[p_], in_=qg_d[s_, :, :]), inc=sL, k=16)
            op("sync", lambda e, s_=s, p_=pb: e.dma_start(
                out=rp_raw[p_], in_=rp_d[s_, :, :]), inc=sL, k=16)
            qg_ld_s[s] = sL.n

        def emit_p1b_xload(s):
            wait("sync", sL, sL.n)
            if s == 0:
                wait("sync", sC, xn_done_g[63])
                wait("sync", sV, rs_done_g[63])
            else:
                wait("sync", sV, x1stt_done_s[s - 1])
            op("sync", lambda e, s_=s: e.dma_start(
                out=x_t[:, :, :],
                in_=xr[s_ * SUB:(s_ + 1) * SUB, :].rearrange(
                    "(t p) c -> p t c", p=128)), inc=sL, k=16)
            x_ld_s[s] = sL.n

        emit_p1b_loads(0)
        emit_p1b_xload(0)
        for s in range(NSUB):
            pb = s % 2
            qgv = qg_bufs[pb]
            rpv = rp_bufs[pb]

            wait("tensor", sL, qg_ld_s[s])
            wait("tensor", sV, wc_done)
            if s >= 1:
                wait("tensor", sV, rf_done_s[s - 1])
            for ic in range(2):
                for jc in range(2):
                    op("tensor", lambda e, ic_=ic, jc_=jc, q_=qgv: e.matmul(
                        ps[6 + ic_][:, :],
                        wc_s[:, jc_, ic_ * 128:(ic_ + 1) * 128],
                        q_[:, jc_, :],
                        start=(jc_ == 0), stop=(jc_ == 1)),
                       inc=sT if (ic == 1 and jc == 1) else None)
            rc_done = sT.n
            rcorr_done_s[s] = rc_done
            wait("vector", sT, rc_done)
            for ic in range(2):
                op("vector", lambda e, ic_=ic, r_=rpv: e.tensor_tensor(
                    out=rt[:, ic_, :], in0=ps[6 + ic_][:, :],
                    in1=r_[:, ic_, :], op=OP.add), inc=sV)
            rf_done = sV.n
            rf_done_s[s] = rf_done

            if s + 1 < NSUB:
                emit_p1b_loads(s + 1)

            wait("tensor", sV, rf_done)
            x1_done_hist = []
            first_stt = True
            for tt in range(4):
                b0 = 2 + (tt % 2)
                b1 = 4 + (tt % 2)
                if tt >= 2:
                    wait("tensor", sV, x1_done_hist[tt - 2])
                for half, bb in ((0, b0), (1, b1)):
                    for ic in range(2):
                        op("tensor", lambda e, b_=bb, ic_=ic, tt_=tt, h_=half:
                           e.matmul(ps[b_][:, :],
                                    rt[:, ic_, tt_ * 128:(tt_ + 1) * 128],
                                    wo_s[:, ic_, h_ * 512:(h_ + 1) * 512],
                                    start=(ic_ == 0), stop=(ic_ == 1)),
                           inc=sT if ic == 1 else None)
                wo_done = sT.n
                wait("vector", sT, wo_done)
                wait("vector", sL, x_ld_s[s])
                if first_stt and s >= 1:
                    wait("vector", sG, x1store_s[s - 1])
                    first_stt = False
                for half, bb in ((0, b0), (1, b1)):
                    op("vector", lambda e, b_=bb, tt_=tt, h_=half:
                       e.scalar_tensor_tensor(
                           out=x1b[:, tt_, h_ * 512:(h_ + 1) * 512],
                           in0=ps[b_][:, :], scalar=scm[:, 0:1],
                           in1=x_t[:, tt_, h_ * 512:(h_ + 1) * 512],
                           op0=OP.mult, op1=OP.add), inc=sV)
                x1_done = sV.n
                x1_done_hist.append(x1_done)

                wait("scalar", sV, x1_done)
                op("scalar", lambda e, tt_=tt: e.activation(
                    a_xn[:, :], x1b[:, tt_, :], ACT.Square,
                    accum_out=lnw[:, 4:5]), inc=sC)
                sq2 = sC.n
                op("vector", lambda e, tt_=tt: e.reduce_sum(
                    lnw[:, 0:1], x1b[:, tt_, :], axis=AX), inc=sV,
                    serialize=True)
                wait("vector", sC, sq2)
                op("vector", lambda e: e.tensor_scalar(
                    out=lnw[:, 8:9], in0=lnw[:, 0:1], scalar1=RD,
                    scalar2=None, op0=OP.mult), inc=sV, serialize=True)
                op("vector", lambda e: e.tensor_tensor(
                    out=lnw[:, 9:10], in0=lnw[:, 8:9], in1=lnw[:, 8:9],
                    op=OP.mult), inc=sV, serialize=True)
                op("vector", lambda e: e.scalar_tensor_tensor(
                    out=lnw[:, 10:11], in0=lnw[:, 4:5], scalar=RD,
                    in1=lnw[:, 9:10], op0=OP.mult, op1=OP.subtract), inc=sV,
                    serialize=True)
                op("vector", lambda e: e.tensor_scalar(
                    out=lnw[:, 11:12], in0=lnw[:, 10:11], scalar1=EPS_LN,
                    scalar2=None, op0=OP.add), inc=sV, serialize=True)
                op("vector", lambda e: e.reciprocal(
                    lnw[:, 12:13], lnw[:, 11:12]), inc=sV, serialize=True)
                rec2 = sV.n
                wait("scalar", sV, rec2)
                tile_idx = s * 4 + tt
                op("scalar", lambda e, ti=tile_idx: e.activation(
                    stats[:, ti, 0:1], lnw[:, 12:13], ACT.Sqrt), inc=sC)
                rstd2 = sC.n
                wait("vector", sC, rstd2)
                op("vector", lambda e, ti=tile_idx: e.scalar_tensor_tensor(
                    out=stats[:, ti, 1:2], in0=lnw[:, 8:9], scalar=-1.0,
                    in1=stats[:, ti, 0:1], op0=OP.mult, op1=OP.mult), inc=sV,
                    serialize=True)
                op("vector", lambda e, tt_=tt: e.reduce_max(
                    lnw[:, 14:15], x1b[:, tt_, :], axis=AX), inc=sV,
                    serialize=True)
                op("vector", lambda e, ti=tile_idx: e.tensor_scalar(
                    out=lnw[:, 15:16], in0=lnw[:, 14:15],
                    scalar1=lnw[:, 8:9], scalar2=stats[:, ti, 0:1],
                    op0=OP.subtract, op1=OP.mult), inc=sV, serialize=True)
                op("vector", lambda e: e.tensor_tensor(
                    out=xmax[:, :], in0=xmax[:, :], in1=lnw[:, 15:16],
                    op=OP.max), inc=sV, serialize=True)
            slab_v = sV.n
            x1stt_done_s[s] = x1_done_hist[-1]
            wait("gpsimd", sV, slab_v)
            op("gpsimd", lambda e, s_=s: e.dma_start(
                out=x1_d[s_ * 4:(s_ + 1) * 4, :, :].rearrange(
                    "t p c -> p t c"), in_=x1b[:, :, :]), inc=sG, k=16)
            x1store_s[s] = sG.n
            if s + 1 < NSUB:
                emit_p1b_xload(s + 1)

        milestone("p1b")
        wait("tensor", sV, sV.n)
        op("tensor", lambda e: e.matmul(
            ps[0][0:1, 0:128], xmax[:, :], identf[:, :], is_transpose=True),
            inc=sT)
        wait("vector", sT, sT.n)
        op("vector", lambda e: e.reduce_max(
            sc1[0:1, 2:3], ps[0][0:1, 0:128], axis=AX), inc=sV,
            serialize=True)
        wait("gpsimd", sV, sV.n)
        op("gpsimd", lambda e: e.dma_start(out=m2_i.ap(), in_=sc1[0:1, 2:3]),
           inc=sG, k=16)
        wait("gpsimd", sG, sG.n)
        op("gpsimd", lambda e: e.collective_compute(
            "AllReduce", OP.max, replica_groups=[list(range(NCORES))],
            ins=[m2_i.ap().opt()], outs=[m2_o.ap().opt()]), inc=sCC)
        wait("gpsimd", sCC, sCC.n)
        op("gpsimd", lambda e: e.dma_start(out=sc1[0:1, 2:3], in_=m2_o.ap()),
           inc=sG, k=16)
        wait("vector", sG, sG.n)
        op("vector", lambda e: e.tensor_scalar(
            out=srow[0:1, 2:3], in0=sc1[0:1, 2:3], scalar1=EPS_Q,
            scalar2=R255, op0=OP.max, op1=OP.mult), inc=sV, serialize=True)
        op("vector", lambda e: e.reciprocal(srow[0:1, 1:2], srow[0:1, 2:3]),
           inc=sV, serialize=True)
        s1_ready = sV.n
        wait("tensor", sV, s1_ready)
        op("tensor", lambda e: e.matmul(
            ps[0][:, 0:16], ones1[:, :], srow[0:1, 0:16], start=True,
            stop=True), inc=sT)
        wait("vector", sT, sT.n)
        op("vector", lambda e: e.tensor_copy(scm[:, 1:3], ps[0][:, 1:3]),
           inc=sV, serialize=True)
        s1b_done = sV.n

        milestone("cc2")
        for ft in range(32):
            wait("sync", sV, sV.n)
            wait("sync", sT, sT.n)
            op("sync", lambda e, ft_=ft: e.dma_start(
                out=p3_wbuf[:, 0:1024], in_=w1_in[ft_ * 128:(ft_ + 1) * 128, :]),
                inc=sL, k=16)
            wl = sL.n
            wait("vector", sL, wl)
            op("vector", lambda e, ft_=ft: e.tensor_reduce(
                lnw[:, 0:1], p3_wbuf[:, 0:1024], AX, OP.max,
                apply_absolute_value=True), inc=sV, serialize=True)
            op("vector", lambda e, ft_=ft: e.tensor_scalar(
                out=sw1[:, ft_:ft_ + 1], in0=lnw[:, 0:1], scalar1=EPS_Q,
                scalar2=R127, op0=OP.max, op1=OP.mult), inc=sV, serialize=True)
            op("vector", lambda e, ft_=ft: e.reciprocal(
                lnw[:, 1:2], sw1[:, ft_:ft_ + 1]), inc=sV, serialize=True)
            op("vector", lambda e: e.tensor_scalar(
                out=p3_tq[:, :], in0=p3_wbuf[:, 0:1024], scalar1=lnw[:, 1:2],
                scalar2=MAGIC, op0=OP.mult, op1=OP.add), inc=sV)
            op("vector", lambda e: e.tensor_scalar(
                out=p3_xint[:, 0:1024], in0=p3_tq[:, :], scalar1=MAGIC,
                scalar2=None, op0=OP.subtract), inc=sV)
            wq_done = sV.n
            wait("tensor", sV, wq_done)
            for dc in range(8):
                op("tensor", lambda e, dc_=dc: e.matmul(
                    ps7b[:, dc_ * 128:(dc_ + 1) * 128],
                    p3_xint[:, dc_ * 128:(dc_ + 1) * 128], identb[:, :],
                    is_transpose=True), inc=sT if dc == 7 else None)
            wtr = sT.n
            wait("vector", sT, wtr)
            op("vector", lambda e, ft_=ft: e.tensor_copy(
                w1v[:, :, ft_ * 128:(ft_ + 1) * 128],
                ps7b[:, :].rearrange("p (a c) -> p a c", a=8)), inc=sV)
        w1_ready_v = sV.n
        wait("vector", sV, max(w1_ready_v, s1b_done))
        op("vector", lambda e: e.tensor_scalar(
            out=s1sw[:, :], in0=sw1[:, :], scalar1=scm[:, 2:3], scalar2=None,
            op0=OP.mult), inc=sV, serialize=True)
        op("vector", lambda e: e.reciprocal(ccol[:, :], s1sw[:, :]), inc=sV,
           serialize=True)
        op("vector", lambda e: e.tensor_tensor(
            out=bcol[:, :], in0=b1_s[:, :], in1=ccol[:, :], op=OP.mult),
            inc=sV, serialize=True)
        op("vector", lambda e: e.tensor_scalar(
            out=bcol[:, :], in0=bcol[:, :], scalar1=MAGIC, scalar2=None,
            op0=OP.add), inc=sV, serialize=True)
        op("vector", lambda e: e.tensor_scalar(
            out=b1i[:, :], in0=bcol[:, :], scalar1=MAGIC, scalar2=None,
            op0=OP.subtract), inc=sV, serialize=True)
        b1i_done = sV.n

        milestone("p3w")
        i1_evac_c = [0] * 5
        wait("sync", sS, sS.n)
        wait("sync", sG, sG.n)
        p3_tq_done = [0]
        x1l_s = [0] * NSUB
        xn2_tile = [[0] * 4 for _ in range(NSUB)]

        def p3_xload(s):
            if s == 0:
                wait("gpsimd", sV, b1i_done)
            for tt in range(4):
                wait("gpsimd", sG, sG.n)
                if s > 0:
                    wait("gpsimd", sC, xn2_tile[s - 1][tt])
                op("gpsimd", lambda e, s_=s, tt_=tt: e.dma_start(
                    out=x_t[:, tt_, :], in_=x1_d[s_ * 4 + tt_, :, :]),
                    inc=sG, k=16)
            x1l_s[s] = sG.n

        xT_done_s = [0] * NSUB
        xq_tile = [0] * 4

        def p3_tileA(sl, tt):
            ti = sl * 4 + tt
            wait("scalar", sG, x1l_s[sl])
            wait("scalar", sV, max(b1i_done, p3_tq_done[0]))
            op("scalar", lambda e, tt_=tt, ti_=ti: e.activation(
                a_xn[:, :], x_t[:, tt_, :], ACT.Identity,
                bias=stats[:, ti_, 1:2], scale=stats[:, ti_, 0:1]), inc=sC)
            xn2d = sC.n
            xn2_tile[sl][tt] = xn2d
            wait("vector", sC, xn2d)
            op("vector", lambda e: e.tensor_scalar(
                out=p3_tq[:, :], in0=a_xn[:, :], scalar1=scm[:, 1:2],
                scalar2=MAGIC, op0=OP.mult, op1=OP.add), inc=sV)
            p3_tq_done[0] = sV.n
            op("vector", lambda e, tt_=tt: e.tensor_scalar(
                out=p3_xint[:, (tt_ % 2) * 1024:(tt_ % 2) * 1024 + 1024],
                in0=p3_tq[:, :], scalar1=MAGIC,
                scalar2=MAGIC, op0=OP.max, op1=OP.subtract), inc=sV)
            xq_tile[tt] = sV.n

        def p3_tileB(sl, tt):
            xT = p3_xintTv[sl % 2]
            wait("tensor", sV, xq_tile[tt])
            for dc in range(8):
                op("tensor", lambda e, dc_=dc, tt_=tt: e.matmul(
                    ps7b[:, dc_ * 128:(dc_ + 1) * 128],
                    p3_xint[:, (tt_ % 2) * 1024 + dc_ * 128:
                            (tt_ % 2) * 1024 + (dc_ + 1) * 128], identb[:, :],
                    is_transpose=True), inc=sT if dc == 7 else None)
            xtr = sT.n
            wait("vector", sT, xtr)
            op("vector", lambda e, tt_=tt, xT_=xT: e.tensor_copy(
                xT_[:, :, tt_ * 128:(tt_ + 1) * 128],
                ps7b[:, :].rearrange("p (a c) -> p a c", a=8)), inc=sV)
            if tt == 3:
                xT_done_s[sl] = sV.n

        p3_xload(0)
        for tt in range(4):
            p3_tileA(0, tt)
            p3_tileB(0, tt)
        p3_xload(1)
        for sl in range(NSUB):
            xTm = p3_xintTv[sl % 2]
            wait("tensor", sV, xT_done_s[sl])
            for fc in range(32):
                b = fc % 4
                wait("tensor", sC, i1_evac_c[-4])
                if sl + 1 < NSUB:
                    if fc in (8, 12, 16, 20):
                        p3_tileB(sl + 1, (fc - 8) // 4)
                    if fc in (4, 8, 12, 16):
                        p3_tileA(sl + 1, (fc - 4) // 4)
                for dc in range(8):
                    op("tensor", lambda e, b_=b, fc_=fc, dc_=dc, xT_=xTm:
                       e.matmul(
                        ps[b_][:, :],
                        w1v[:, dc_, fc_ * 128:(fc_ + 1) * 128],
                        xT_[:, dc_, :],
                        start=(dc_ == 0), stop=(dc_ == 7)),
                       inc=sT if dc == 7 else None)
                mm_done = sT.n
                wait("scalar", sT, mm_done)
                op("scalar", lambda e, b_=b, fc_=fc: e.copy(
                    p3_i1[:, fc_ % 4, :], ps[b_][:, :]), inc=sC)
                i1_evac_c.append(sC.n)
                wait("vector", sC, sC.n)
                op("vector", lambda e, fc_=fc: e.reduce_max(
                    lnw[:, 3:4], p3_i1[:, fc_ % 4, :], axis=AX), inc=sV,
                    serialize=True)
                op("vector", lambda e, fc_=fc: e.tensor_tensor(
                    out=mf[:, fc_:fc_ + 1], in0=mf[:, fc_:fc_ + 1],
                    in1=lnw[:, 3:4], op=OP.max), inc=sV, serialize=True)
                if fc % 4 == 3:
                    wait("sync", sC, sC.n)
                    op("sync", lambda e, s_=sl, f0=fc - 3: e.dma_start(
                        out=i1_d[s_, f0:f0 + 4, :, :].rearrange(
                            "f p c -> p f c"), in_=p3_i1[:, :, :]), inc=sS,
                        k=16)
                    st_i1 = sS.n
                    wait("scalar", sS, st_i1)
                    wait("scalar", sV, sV.n)
            if sl + 2 < NSUB:
                p3_xload(sl + 2)

        milestone("p3")
        p3_t_done = sT.n
        p3_c_end = sC.n
        p3_v_end = sV.n
        w2buf = [p3_wbuf, arena[:, 0:4096]]
        w2_ld = [0] * 8
        w2_q_done = [0] * 8

        def p4w_load(ft):
            if ft == 0:
                wait("sync", sV, p3_v_end)
                wait("sync", sT, p3_t_done)
            else:
                wait("sync", sL, w2_ld[ft - 1])
            if ft == 1:
                wait("sync", sC, p3_c_end)
            if ft >= 2:
                wait("sync", sV, w2_q_done[ft - 2])
            op("sync", lambda e, ft_=ft, w_=w2buf[ft % 2]: e.dma_start(
                out=w_, in_=w2_in[ft_ * 128:(ft_ + 1) * 128, :]), inc=sL, k=16)
            w2_ld[ft] = sL.n

        p4w_load(0)
        p4w_load(1)
        wait("vector", sT, p3_t_done)
        for ft in range(8):
            wb = w2buf[ft % 2]
            wait("vector", sL, w2_ld[ft])
            op("vector", lambda e, w_=wb: e.tensor_reduce(
                lnw[:, 0:1], w_, AX, OP.max,
                apply_absolute_value=True), inc=sV, serialize=True)
            op("vector", lambda e, ft_=ft: e.tensor_scalar(
                out=sw2[:, ft_:ft_ + 1], in0=lnw[:, 0:1], scalar1=EPS_Q,
                scalar2=R127, op0=OP.max, op1=OP.mult), inc=sV, serialize=True)
            op("vector", lambda e, ft_=ft: e.reciprocal(
                lnw[:, 1:2], sw2[:, ft_:ft_ + 1]), inc=sV, serialize=True)
            for hh in range(4):
                op("vector", lambda e, hh_=hh, w_=wb: e.tensor_scalar(
                    out=p3_tq[:, :], in0=w_[:, hh_ * 1024:(hh_ + 1) * 1024],
                    scalar1=lnw[:, 1:2], scalar2=MAGIC, op0=OP.mult,
                    op1=OP.add), inc=sV)
                op("vector", lambda e, hh_=hh: e.tensor_scalar(
                    out=p4_w2q[:, hh_ * 1024:(hh_ + 1) * 1024], in0=p3_tq[:, :],
                    scalar1=MAGIC, scalar2=None, op0=OP.subtract), inc=sV)
            wq2 = sV.n
            w2_q_done[ft] = wq2
            if ft + 2 < 8:
                p4w_load(ft + 2)
            wait("tensor", sV, wq2)
            wait("tensor", sT, p3_t_done)
            for g in range(4):
                wait("tensor", sV, sV.n)
                for q in range(8):
                    fc = g * 8 + q
                    op("tensor", lambda e, q_=q, fc_=fc: e.matmul(
                        ps7b[:, q_ * 128:(q_ + 1) * 128],
                        p4_w2q[:, fc_ * 128:(fc_ + 1) * 128], identb[:, :],
                        is_transpose=True), inc=sT if q == 7 else None)
                wt2 = sT.n
                wait("vector", sT, wt2)
                op("vector", lambda e, g_=g, ft_=ft: e.tensor_copy(
                    w2v[:, g_ * 8:(g_ + 1) * 8, ft_ * 128:(ft_ + 1) * 128],
                    ps7b[:, :].rearrange("p (a c) -> p a c", a=8)), inc=sV)
        w2_ready_v = sV.n

        wait("vector", sV, w2_ready_v)
        op("vector", lambda e: e.tensor_tensor(
            out=ccol[:, :], in0=mf[:, :], in1=b1i[:, :], op=OP.add), inc=sV,
            serialize=True)
        op("vector", lambda e: e.tensor_tensor(
            out=ccol[:, :], in0=ccol[:, :], in1=s1sw[:, :], op=OP.mult),
            inc=sV, serialize=True)
        op("vector", lambda e: e.reduce_max(lnw[:, 5:6], ccol[:, :], axis=AX),
           inc=sV, serialize=True)
        hm_col = sV.n
        wait("tensor", sV, hm_col)
        op("tensor", lambda e: e.matmul(
            ps[0][0:1, 0:128], lnw[:, 5:6], identf[:, :], is_transpose=True),
            inc=sT)
        wait("vector", sT, sT.n)
        op("vector", lambda e: e.reduce_max(
            sc1[0:1, 3:4], ps[0][0:1, 0:128], axis=AX), inc=sV,
            serialize=True)
        op("vector", lambda e: e.tensor_scalar(
            out=sc1[0:1, 3:4], in0=sc1[0:1, 3:4], scalar1=0.0, scalar2=None,
            op0=OP.max), inc=sV, serialize=True)
        wait("gpsimd", sV, sV.n)
        op("gpsimd", lambda e: e.dma_start(out=m3_i.ap(), in_=sc1[0:1, 3:4]),
           inc=sG, k=16)
        wait("gpsimd", sG, sG.n)
        op("gpsimd", lambda e: e.collective_compute(
            "AllReduce", OP.max, replica_groups=[list(range(NCORES))],
            ins=[m3_i.ap().opt()], outs=[m3_o.ap().opt()]), inc=sCC)
        wait("gpsimd", sCC, sCC.n)
        op("gpsimd", lambda e: e.dma_start(out=sc1[0:1, 3:4], in_=m3_o.ap()),
           inc=sG, k=16)
        wait("vector", sG, sG.n)
        op("vector", lambda e: e.tensor_scalar(
            out=srow[0:1, 4:5], in0=sc1[0:1, 3:4], scalar1=EPS_Q,
            scalar2=R255, op0=OP.max, op1=OP.mult), inc=sV, serialize=True)
        op("vector", lambda e: e.reciprocal(srow[0:1, 3:4], srow[0:1, 4:5]),
           inc=sV, serialize=True)
        s2_ready = sV.n
        wait("tensor", sV, s2_ready)
        op("tensor", lambda e: e.matmul(
            ps[0][:, 0:16], ones1[:, :], srow[0:1, 0:16], start=True,
            stop=True), inc=sT)
        wait("vector", sT, sT.n)
        op("vector", lambda e: e.tensor_copy(scm[:, 3:5], ps[0][:, 3:5]),
           inc=sV, serialize=True)
        op("vector", lambda e: e.tensor_scalar(
            out=ccol[:, :], in0=s1sw[:, :], scalar1=scm[:, 3:4], scalar2=None,
            op0=OP.mult), inc=sV, serialize=True)
        op("vector", lambda e: e.tensor_tensor(
            out=bcol[:, :], in0=b1i[:, :], in1=ccol[:, :], op=OP.mult),
            inc=sV, serialize=True)
        op("vector", lambda e: e.tensor_scalar(
            out=s2sw[:, :], in0=sw2[:, :], scalar1=scm[:, 4:5], scalar2=None,
            op0=OP.mult), inc=sV, serialize=True)
        op("vector", lambda e: e.reciprocal(b2i[:, :], s2sw[:, :]), inc=sV,
           serialize=True)
        op("vector", lambda e: e.tensor_tensor(
            out=b2i[:, :], in0=b2_s[:, :], in1=b2i[:, :], op=OP.mult), inc=sV,
            serialize=True)
        op("vector", lambda e: e.tensor_scalar(
            out=b2i[:, :], in0=b2i[:, :], scalar1=MAGIC, scalar2=None,
            op0=OP.add), inc=sV, serialize=True)
        op("vector", lambda e: e.tensor_scalar(
            out=b2i[:, :], in0=b2i[:, :], scalar1=MAGIC, scalar2=None,
            op0=OP.subtract), inc=sV, serialize=True)
        op("vector", lambda e: e.tensor_tensor(
            out=bq2v[:, :], in0=b2i[:, :], in1=s2sw[:, :], op=OP.mult),
            inc=sV, serialize=True)
        rows_ready = sV.n
        wait("tensor", sV, rows_ready)
        op("tensor", lambda e: e.matmul(
            ps[0][0:8, 0:128], s2sw[:, :], identf[:, :], is_transpose=True))
        op("tensor", lambda e: e.matmul(
            ps[1][0:8, 0:128], bq2v[:, :], identf[:, :], is_transpose=True),
            inc=sT)
        wait("vector", sT, sT.n)
        op("vector", lambda e: e.tensor_copy(s2row[:, :], ps[0][0:8, 0:128]),
           inc=sV)
        op("vector", lambda e: e.tensor_copy(bq2row[:, :], ps[1][0:8, 0:128]),
           inc=sV)
        wait("gpsimd", sV, sV.n)
        op("gpsimd", lambda e: e.dma_start(
            out=rowb_d.ap()[0], in_=s2row[:, :]), inc=sG, k=16)
        op("gpsimd", lambda e: e.dma_start(
            out=rowb_d.ap()[1], in_=bq2row[:, :]), inc=sG, k=16)
        wait("gpsimd", sG, sG.n)
        op("gpsimd", lambda e: e.dma_start(
            out=s2rowf[0:1, :],
            in_=rowb_d.ap()[0].rearrange("j c -> (j c)").unsqueeze(0)),
            inc=sG, k=16)
        op("gpsimd", lambda e: e.dma_start(
            out=bq2rowf[0:1, :],
            in_=rowb_d.ap()[1].rearrange("j c -> (j c)").unsqueeze(0)),
            inc=sG, k=16)
        wait("tensor", sG, sG.n)
        for i_, (bk, rowf) in enumerate(
                ((2, s2rowf), (3, s2rowf), (4, bq2rowf), (5, bq2rowf))):
            co = (i_ % 2) * 512
            op("tensor", lambda e, b_=bk, co_=co, r_=rowf: e.matmul(
                ps[b_][:, :], ones1[:, :], r_[0:1, co_:co_ + 512],
                start=True, stop=True), inc=sT if i_ == 3 else None)
        wait("vector", sT, sT.n)
        op("vector", lambda e: e.tensor_copy(p4_sb2[:, 0:512], ps[2][:, :]))
        op("vector", lambda e: e.tensor_copy(p4_sb2[:, 512:1024], ps[3][:, :]))
        op("vector", lambda e: e.tensor_copy(p4_bq2[:, 0:512], ps[4][:, :]))
        op("vector", lambda e: e.tensor_copy(p4_bq2[:, 512:1024], ps[5][:, :]),
           inc=sV)
        p4_consts = sV.n

        milestone("p4w")
        wait("sync", sS, sS.n)
        if debug_dump:
            wait("sync", sV, sV.n)
            op("sync", lambda e: e.dma_start(out=dbg_w2[:, :, :], in_=w2v[:, :, :]), inc=sS, k=16)
            op("sync", lambda e: e.dma_start(out=dbg_sb[:, 0:1024], in_=p4_sb2[:, :]), inc=sS, k=16)
            op("sync", lambda e: e.dma_start(out=dbg_sb[:, 1024:2048], in_=p4_bq2[:, :]), inc=sS, k=16)
            op("sync", lambda e: e.dma_start(out=dbg_cb[:, 0:32], in_=ccol[:, :]), inc=sS, k=16)
            op("sync", lambda e: e.dma_start(out=dbg_cb[:, 32:64], in_=bcol[:, :]), inc=sS, k=16)
            op("sync", lambda e: e.dma_start(out=dbg_cb[:, 64:96], in_=s1sw[:, :]), inc=sS, k=16)
            op("sync", lambda e: e.dma_start(out=dbg_cb[:, 96:128], in_=b1i[:, :]), inc=sS, k=16)
            op("sync", lambda e: e.dma_start(out=dbg_cb[:, 128:133], in_=scm[:, 0:5]), inc=sS, k=16)
        NF = 16 * 32
        i1_ld = [0] * NF
        t0_inc = [0] * NF
        t1_inc = [0] * NF
        hint_inc = [0] * NF
        fc_mm = [0] * NF
        ep_mult_v = [0] * 16
        ep_add_v = [0] * 16
        out_st = [0] * 16
        x1_ld4 = [0] * 16

        wait("sync", sT, sT.n)
        wait("sync", sC, sC.n)
        wait("sync", sV, sV.n)

        ep_mult_bank = [[0] * 8 for _ in range(16)]

        def emit_epilogue_mults(sl):
            if sl > 0:
                wait("vector", sG, out_st[sl - 1])
            wait("vector", sT, fc_mm[sl * 32 + 31])
            for tt in range(4):
                for hh in range(2):
                    b = tt * 2 + hh
                    op("vector", lambda e, b_=b, hh_=hh, tt_=tt:
                       e.tensor_tensor(
                           out=p4_out[:, tt_, hh_ * 512:(hh_ + 1) * 512],
                           in0=ps[b_][:, :],
                           in1=p4_sb2[:, hh_ * 512:(hh_ + 1) * 512],
                           op=OP.mult), inc=sV)
                    ep_mult_bank[sl][b] = sV.n
            ep_mult_v[sl] = sV.n

        def emit_epilogue_adds(sl):
            wait("vector", sG, x1_ld4[sl])
            for tt in range(4):
                op("vector", lambda e, tt_=tt: e.tensor_tensor(
                    out=p4_x1[:, tt_, :], in0=p4_x1[:, tt_, :],
                    in1=p4_bq2[:, :], op=OP.add), inc=sV)
            for tt in range(4):
                op("vector", lambda e, tt_=tt: e.tensor_tensor(
                    out=p4_out[:, tt_, :], in0=p4_out[:, tt_, :],
                    in1=p4_x1[:, tt_, :], op=OP.add), inc=sV)
            ep_add_v[sl] = sV.n
            wait("gpsimd", sV, ep_add_v[sl])
            op("gpsimd", lambda e, s_=sl: e.dma_start(
                out=out_d[s_ * 512:(s_ + 1) * 512, :].rearrange(
                    "(t p) c -> p t c", p=128), in_=p4_out[:, :, :]), inc=sG,
                k=16)
            out_st[sl] = sG.n

        for F in range(NF):
            sl, fc = divmod(F, 32)
            if F >= 4:
                wait("sync", sC, t0_inc[F - 4])

            if F >= 1:
                wait("sync", sL, i1_ld[F - 1])
            op("sync", lambda e, s_=sl, f_=fc, r=F % 4: e.dma_start(
                out=p4_i1[:, r, :], in_=i1_d[s_, f_, :, :]), inc=sL, k=16)
            i1_ld[F] = sL.n
            if fc == 9:
                if sl > 0:
                    wait("gpsimd", sV, ep_add_v[sl - 1])
                op("gpsimd", lambda e, s_=sl: e.dma_start(
                    out=p4_x1[:, :, :],
                    in_=x1_d[s_ * 4:(s_ + 1) * 4, :, :].rearrange(
                        "t p c -> p t c")), inc=sG, k=16)
                x1_ld4[sl] = sG.n

            wait("scalar", sL, i1_ld[F])
            wait("scalar", sV, p4_consts)
            if F >= 2:
                wait("scalar", sV, t1_inc[F - 2])
            op("scalar", lambda e, f_=fc, r=F % 4, r0=F % 2: e.activation(
                p4_t0[:, r0, :], p4_i1[:, r, :], ACT.Identity,
                bias=bcol[:, f_:f_ + 1], scale=ccol[:, f_:f_ + 1]), inc=sC)
            t0_inc[F] = sC.n

            wait("vector", sC, t0_inc[F])
            op("vector", lambda e, r0=F % 2: e.tensor_scalar(
                out=p4_t1[:, :], in0=p4_t0[:, r0, :], scalar1=MAGIC,
                scalar2=None, op0=OP.add), inc=sV)
            t1_inc[F] = sV.n
            if F >= 8:
                wait("vector", sT, fc_mm[F - 8])
            op("vector", lambda e, r=F % 8: e.tensor_scalar(
                out=p4_hint[:, r, :], in0=p4_t1[:, :], scalar1=MAGIC,
                scalar2=MAGIC, op0=OP.max, op1=OP.subtract), inc=sV)
            hint_inc[F] = sV.n
            if debug_dump:
                wait("sync", sV, hint_inc[F])
                op("sync", lambda e, F_=F, r=F % 8: e.dma_start(
                    out=dbg_ha[F_], in_=p4_hint[:, r, :]), inc=sS, k=16)
            if fc == 2 and sl > 0:
                emit_epilogue_mults(sl - 1)
            if fc == 8 and sl > 0:
                emit_epilogue_adds(sl - 1)

            def emit_tensor_fc(Ft, slack_v):
                slt, fct = divmod(Ft, 32)
                wait("tensor", sV, slack_v)
                for tt in range(4):
                    for hh in range(2):
                        b = tt * 2 + hh
                        if fct == 0 and slt > 0:
                            wait("tensor", sV, ep_mult_bank[slt - 1][b])
                        op("tensor", lambda e, b_=b, tt_=tt, hh_=hh,
                           f_=fct, r=Ft % 8: e.matmul(
                               ps[b_][:, :],
                               p4_hint[:, r, tt_ * 128:(tt_ + 1) * 128],
                               w2v[:, f_, hh_ * 512:(hh_ + 1) * 512],
                               start=(f_ == 0), stop=(f_ == 31)),
                           inc=sT if (tt == 3 and hh == 1) else None)
                fc_mm[Ft] = sT.n

            if F >= 2:
                emit_tensor_fc(F - 2, hint_inc[F])
        op("scalar", lambda e: e.memzero(lnw[:, 1:2]), inc=sC)
        op("vector", lambda e: e.memzero(p4_t1[:, 0:1]), inc=sV)
        op("vector", lambda e: e.memzero(p4_t1[:, 1:2]), inc=sV)
        emit_tensor_fc(NF - 2, sV.n)
        emit_tensor_fc(NF - 1, sV.n)
        emit_epilogue_mults(15)
        emit_epilogue_adds(15)

        milestone("full")
        wait("sync", sS, sS.n)
        wait("gpsimd", sG, sG.n)
        if stop_stage is not None:
            cut = milestones[stop_stage]
            for k in ENGINES:
                del plan[k][cut[k]:]

        with nc.Block() as block:
            @block.sync
            def _(eng):
                for f in plan["sync"]:
                    f(eng)

            @block.tensor
            def _(eng):
                for f in plan["tensor"]:
                    f(eng)

            @block.vector
            def _(eng):
                for f in plan["vector"]:
                    f(eng)

            @block.scalar
            def _(eng):
                for f in plan["scalar"]:
                    f(eng)

            @block.gpsimd
            def _(eng):
                for f in plan["gpsimd"]:
                    f(eng)

    return nc


def _host_inputs(L0_attn_x, L0_attn_Wq, L0_attn_Wk, L0_attn_Wv, L0_attn_Wo,
                 L0_attn_gate, L0_attn_M0, L0_ffn_w1, L0_ffn_b1, L0_ffn_w2,
                 L0_ffn_b2):
    f32 = np.float32
    x = np.asarray(L0_attn_x, dtype=f32)
    wq = np.asarray(L0_attn_Wq, dtype=f32)
    wk = np.asarray(L0_attn_Wk, dtype=f32)
    wv = np.asarray(L0_attn_Wv, dtype=f32)
    wo = np.asarray(L0_attn_Wo, dtype=f32)
    gate = np.asarray(L0_attn_gate, dtype=f32).reshape(1, 1)
    m0 = np.asarray(L0_attn_M0, dtype=f32)
    w1 = np.ascontiguousarray(np.asarray(L0_ffn_w1, dtype=f32))
    b1 = np.asarray(L0_ffn_b1, dtype=f32)
    w2 = np.ascontiguousarray(np.asarray(L0_ffn_w2, dtype=f32))
    b2 = np.asarray(L0_ffn_b2, dtype=f32)

    wqt = np.ascontiguousarray(wq.T)
    wkt = np.ascontiguousarray(wk.T)
    wvt = np.ascontiguousarray(wv.T)
    wot = np.ascontiguousarray(wo.T)
    m0t = np.ascontiguousarray(m0.T)
    b1r = np.ascontiguousarray(b1.reshape(FF // 128, 128).T)
    b2r = np.ascontiguousarray(b2.reshape(D // 128, 128).T)

    maskv = np.zeros((128, 128), dtype=f32)
    for kr in range(128):
        for qr in range(128):
            if kr // 8 < qr // 8:
                maskv[kr, qr] = 1.0
    identf = np.eye(128, dtype=f32)
    identb = np.eye(128, dtype=f32).astype(ml_dtypes.bfloat16)
    taus = np.arange(SUB) // B
    dqrow = (DECAY ** taus.astype(np.float64)).astype(f32).reshape(1, SUB)
    dkrow = ((DECAY ** (-1.0 - taus.astype(np.float64))) / B).astype(
        f32).reshape(1, SUB)

    in_maps = []
    for c in range(NCORES):
        xs = np.ascontiguousarray(
            x[:, c * L:(c + 1) * L, :].transpose(1, 0, 2).reshape(RPC, D))
        coef = np.zeros((1, 16), dtype=f32)
        for j in range(c):
            coef[0, j] = f32(DECAY ** (L * (c - 1 - j)))
        coef[0, 8] = f32(DECAY ** (L * c))
        in_maps.append(dict(
            xr=xs, wqt=wqt, wkt=wkt, wvt=wvt, wot=wot, m0t=m0t,
            gate_in=gate, w1_in=w1, b1_in=b1r, w2_in=w2, b2_in=b2r,
            maskc=maskv, identf_in=identf, identb_in=identb,
            dqrow_in=dqrow, dkrow_in=dkrow, coef_in=coef))
    return in_maps


def kernel(**inputs):
    in_maps = _host_inputs(**inputs)
    nc = _build_nc()
    res = run_bass_kernel_spmd(nc, in_maps, core_ids=list(range(NCORES)))
    outs = []
    for c in range(NCORES):
        oc = np.asarray(res.results[c]["out"])
        outs.append(oc.reshape(L, B, D).transpose(1, 0, 2))
    return np.ascontiguousarray(np.concatenate(outs, axis=1))
